# version 15
# baseline (speedup 1.0000x reference)
import numpy as np
import concourse.bass as bass
import concourse.mybir as mybir
from contextlib import ExitStack

F32 = mybir.dt.float32
BF16 = mybir.dt.bfloat16
AF = mybir.ActivationFunctionType
ALU = mybir.AluOpType
AX = mybir.AxisListType.X

LN_EPS = 1e-5


class Sem:
    def __init__(self, es, nc, name):
        self.h = es.enter_context(nc.semaphore(name))
        self.v = 0

    def inc(self, n=1):
        self.v += n
        return self.h

    def dinc(self):
        self.v += 16
        return self.h

    @property
    def cur(self):
        return (self.h, self.v)


def ln_emit(blk, sv, z, y_out, scr, st, n_free, prod, key, extra_wait=None):
    def v1(v):
        for w in (extra_wait or []):
            v.wait_ge(*w)
        v.reduce_sum(st[:, 0:1], z[:], axis=AX).then_inc(sv.inc(), 1)
        v.wait_ge(sv.h, sv.v)
        v.tensor_scalar(st[:, 1:2], st[:, 0:1], 1.0 / n_free, None, ALU.mult).then_inc(sv.inc(), 1)
        v.wait_ge(sv.h, sv.v)
        v.tensor_scalar(z[:], z[:], st[:, 1:2], None, ALU.subtract).then_inc(sv.inc(), 1)
    blk.vector(v1)

    def s1(s):
        s.wait_ge(sv.h, sv.v)
        s.activation(scr[:], z[:], AF.Square, accum_out=st[:, 2:3]).then_inc(sv.inc(), 1)
    blk.scalar(s1)

    def v2(v):
        v.wait_ge(sv.h, sv.v)
        v.tensor_scalar(st[:, 3:4], st[:, 2:3], 1.0 / n_free, LN_EPS, ALU.mult, ALU.add).then_inc(sv.inc(), 1)
    blk.vector(v2)

    def s2(s):
        s.wait_ge(sv.h, sv.v)
        s.activation(st[:, 0:1], st[:, 3:4], AF.Ln).then_inc(sv.inc(), 1)
        s.wait_ge(sv.h, sv.v)
        s.activation(st[:, 1:2], st[:, 0:1], AF.Exp, scale=-0.5).then_inc(sv.inc(), 1)
    blk.scalar(s2)

    def v3(v):
        v.wait_ge(sv.h, sv.v)
        v.tensor_scalar(y_out[:], z[:], st[:, 1:2], None, ALU.mult).then_inc(sv.inc(), 1)
        prod[key] = (sv.h, sv.v)
    blk.vector(v3)


def build_l1ffn(nc, es, blk, prod, B, T=2048, pref="f_", x_ap=None, y_ap=None, first=True, ses=None):
    ses = ses or es
    """L1 ffn + residual + postLN: y = LN(x + ffn(x)). x_ap/y_ap: DRAM [B,T,1024] fp32."""
    C = 1024
    NT = T // 128
    NCH = C // 128
    wrt_d = nc.dram_tensor(pref + "WrT", [C, C], F32, kind="ExternalInput").ap()
    wkt_d = nc.dram_tensor(pref + "WkT", [C, C], F32, kind="ExternalInput").ap()
    wvt_d = nc.dram_tensor(pref + "WvT", [C, C], F32, kind="ExternalInput").ap()
    mur_d = nc.dram_tensor(pref + "mur", [128, NCH], F32, kind="ExternalInput").ap()
    muk_d = nc.dram_tensor(pref + "muk", [128, NCH], F32, kind="ExternalInput").ap()
    id_d = nc.dram_tensor(pref + "ident", [128, 128], BF16, kind="ExternalInput").ap()

    E = es.enter_context
    xt = E(nc.sbuf_tensor(pref + "xt", [128, NT * C], F32))
    wr = E(nc.sbuf_tensor(pref + "wr", [128, NCH * C], BF16))
    wk = E(nc.sbuf_tensor(pref + "wk", [128, NCH * C], BF16))
    wv = E(nc.sbuf_tensor(pref + "wv", [128, NCH * C], BF16))
    mur = E(nc.sbuf_tensor(pref + "mur_s", [128, NCH], F32))
    muk = E(nc.sbuf_tensor(pref + "muk_s", [128, NCH], F32))
    murn = E(nc.sbuf_tensor(pref + "murn", [128, NCH], F32))
    mukn = E(nc.sbuf_tensor(pref + "mukn", [128, NCH], F32))
    idb = E(nc.sbuf_tensor(pref + "idb", [128, 128], BF16))
    xbf = E(nc.sbuf_tensor(pref + "xbf", [128, 2 * C], BF16))
    NE = NCH * 129
    xfm = E(nc.sbuf_tensor(pref + "xfm", [128, 2 * NE], BF16))
    xr = E(nc.sbuf_tensor(pref + "xr", [128, 2 * C], BF16))
    xk = E(nc.sbuf_tensor(pref + "xk", [128, 2 * C], BF16))
    rt = E(nc.sbuf_tensor(pref + "rt", [128, 2 * C], F32))
    dif = E(nc.sbuf_tensor(pref + "dif", [128, 2 * C], BF16))
    kab = E(nc.sbuf_tensor(pref + "kab", [128, 2 * C], BF16))
    kafm = E(nc.sbuf_tensor(pref + "kafm", [128, 2 * C], BF16))
    zt = E(nc.sbuf_tensor(pref + "zt", [128, 2 * C], F32))
    scr = E(nc.sbuf_tensor(pref + "scr", [128, 2 * C], F32))
    st = E(nc.sbuf_tensor(pref + "st", [128, 16], F32))
    yst = E(nc.sbuf_tensor(pref + "yst", [128, 4 * C], F32))
    pxT = E(nc.psum_tensor(pref + "pxT", [128, 1024], BF16))
    pr = E(nc.psum_tensor(pref + "pr", [128, 1024], F32))
    pk = E(nc.psum_tensor(pref + "pk", [128, 1024], F32))
    pkT = E(nc.psum_tensor(pref + "pkT", [128, 1024], BF16))
    pv = E(nc.psum_tensor(pref + "pv", [128, 1024], F32))

    sW = Sem(ses, nc, pref + "sW")
    sC = Sem(ses, nc, pref + "sC")
    sX = Sem(ses, nc, pref + "sX")
    sT = Sem(ses, nc, pref + "sT")
    sV = Sem(ses, nc, pref + "sV")
    sS = Sem(ses, nc, pref + "sS")
    sAO0 = Sem(ses, nc, pref + "sAO0")
    sAO1 = Sem(ses, nc, pref + "sAO1")
    sAOp = [sAO0, sAO1]
    sAOr = Sem(ses, nc, pref + "sAOr")
    sO = Sem(ses, nc, pref + "sO")
    sLN = Sem(ses, nc, pref + "sLN")

    def g0(g):
        g.dma_start(wr[:].rearrange("p (ch d) -> p ch d", d=C), wrt_d.rearrange("(ch p) d -> p ch d", p=128)).then_inc(sW.dinc(), 16)
        g.dma_start(wk[:].rearrange("p (ch d) -> p ch d", d=C), wkt_d.rearrange("(ch p) d -> p ch d", p=128)).then_inc(sW.dinc(), 16)
        g.dma_start(wv[:].rearrange("p (ch d) -> p ch d", d=C), wvt_d.rearrange("(ch p) d -> p ch d", p=128)).then_inc(sW.dinc(), 16)
    blk.gpsimd(g0)

    def s0(sync):
        sync.dma_start(mur[:], mur_d).then_inc(sC.dinc(), 16)
        sync.dma_start(muk[:], muk_d).then_inc(sC.dinc(), 16)
        sync.dma_start(idb[:], id_d).then_inc(sC.dinc(), 16)
    blk.sync(s0)
    consts_in = sC.v

    def v0(v):
        v.wait_ge(sC.h, consts_in)
        v.tensor_scalar(murn[:], mur[:], -1.0, 1.0, ALU.mult, ALU.add)
        v.tensor_scalar(mukn[:], muk[:], -1.0, 1.0, ALU.mult, ALU.add).then_inc(sV.inc(), 1)
    blk.vector(v0)
    prod["mu_ready"] = (sV.h, sV.v)

    for b in range(B):
        def sxb(sync, b=b):
            if b > 0:
                sync.wait_ge(*prod[("ffn_zdone", b - 1, NT - 1)])
            sync.dma_start(xt[:].rearrange("p (n c) -> p n c", c=C), x_ap[b].rearrange("(n p) c -> p n c", p=128)).then_inc(sX.dinc(), 16)
        blk.sync(sxb)
        prod[("ffn_x", b)] = (sX.h, sX.v)

        for i in range(NT):
            k = b * NT + i
            pp = k % 2
            xtile = xt[:, i * C:(i + 1) * C]

            def vcast(v, b=b, i=i, pp=pp, xtile=xtile):
                v.wait_ge(*prod[("ffn_x", b)])
                if b * NT + i >= 2:
                    kk = b * NT + i - 2
                    v.wait_ge(*prod[("ffn_xT", kk // NT, kk % NT)])
                v.tensor_copy(xbf[:, pp * C:(pp + 1) * C], xtile).then_inc(sV.inc(), 1)
            blk.vector(vcast)
            prod[("ffn_cast", b, i)] = (sV.h, sV.v)

            def t1(t, b=b, i=i, pp=pp):
                t.wait_ge(sC.h, 48)
                t.wait_ge(*prod[("ffn_cast", b, i)])
                if b * NT + i >= 1:
                    kk = b * NT + i - 1
                    t.wait_ge(*prod[("ffn_evx", kk // NT, kk % NT)])
                ins = None
                for ch in range(NCH):
                    ins = t.transpose(pxT[:, ch * 128:(ch + 1) * 128],
                                      xbf[:, pp * C + ch * 128: pp * C + (ch + 1) * 128], idb[:])
                ins.then_inc(sT.inc(), 1)
            blk.tensor(t1)
            prod[("ffn_xT", b, i)] = (sT.h, sT.v)

            def vevac(v, b=b, i=i, pp=pp, k=k):
                v.wait_ge(*prod[("ffn_xT", b, i)])
                if k >= 2:
                    kk = k - 2
                    v.wait_ge(*prod[("ffn_mix", kk // NT, kk % NT)])
                if i == 0:
                    v.memset(xfm[:, pp * NE:(pp + 1) * NE].rearrange("p (ch e) -> p ch e", e=129)[:, :, 0:1], 0.0).then_inc(sV.inc(), 1)
                else:
                    v.tensor_copy(
                        xfm[:, pp * NE:(pp + 1) * NE].rearrange("p (ch e) -> p ch e", e=129)[:, :, 0:1],
                        xfm[:, (1 - pp) * NE:(2 - pp) * NE].rearrange("p (ch e) -> p ch e", e=129)[:, :, 128:129]).then_inc(sV.inc(), 1)
                v.wait_ge(sV.h, sV.v)
                v.tensor_copy(
                    xfm[:, pp * NE:(pp + 1) * NE].rearrange("p (ch e) -> p ch e", e=129)[:, :, 1:129],
                    pxT[:].rearrange("p (ch e) -> p ch e", e=128)).then_inc(sV.inc(), 1)
            blk.vector(vevac)
            prod[("ffn_evx", b, i)] = (sV.h, sV.v)

            def vmix(v, b=b, i=i, pp=pp, k=k):
                v.wait_ge(*prod[("ffn_evx", b, i)])
                v.wait_ge(*prod["mu_ready"])
                if k >= 2:
                    kk = k - 2
                    v.wait_ge(*prod[("ffn_kmm", kk // NT, kk % NT)])
                ext = xfm[:, pp * NE:(pp + 1) * NE].rearrange("p (ch e) -> p ch e", e=129)
                cur = ext[:, :, 1:129]
                prv = ext[:, :, 0:128]
                dif3 = dif[:, pp * C:(pp + 1) * C].rearrange("p (ch e) -> p ch e", e=128)
                v.tensor_tensor(dif3, cur, prv, ALU.subtract).then_inc(sV.inc(), 1)
                v.wait_ge(sV.h, sV.v)
                ins = None
                for ch in range(NCH):
                    v.scalar_tensor_tensor(xr[:, pp * C + ch * 128:pp * C + (ch + 1) * 128],
                                           dif3[:, ch], mur[:, ch:ch + 1],
                                           prv[:, ch], ALU.mult, ALU.add)
                    ins = v.scalar_tensor_tensor(xk[:, pp * C + ch * 128:pp * C + (ch + 1) * 128],
                                           dif3[:, ch], muk[:, ch:ch + 1],
                                           prv[:, ch], ALU.mult, ALU.add)
                ins.then_inc(sV.inc(), 1)
            blk.vector(vmix)
            prod[("ffn_mix", b, i)] = (sV.h, sV.v)

            def t2(t, b=b, i=i, pp=pp, k=k):
                t.wait_ge(sW.h, 48)
                t.wait_ge(*prod[("ffn_mix", b, i)])
                if k >= 1:
                    kk = k - 1
                    t.wait_ge(*prod[("ffn_sig", kk // NT, kk % NT)])
                    t.wait_ge(*prod[("ffn_relu", kk // NT, kk % NT)])
                ins = None
                for ch in range(NCH):
                    lhs = xr[:, pp * C + ch * 128: pp * C + (ch + 1) * 128]
                    t.matmul(pr[:, 0:512], lhs, wr[:, ch * C: ch * C + 512], start=(ch == 0), stop=(ch == NCH - 1))
                    ins = t.matmul(pr[:, 512:1024], lhs, wr[:, ch * C + 512:(ch + 1) * C], start=(ch == 0), stop=(ch == NCH - 1))
                ins.then_inc(sT.inc(), 1)
                for ch in range(NCH):
                    lhs = xk[:, pp * C + ch * 128: pp * C + (ch + 1) * 128]
                    t.matmul(pk[:, 0:512], lhs, wk[:, ch * C: ch * C + 512], start=(ch == 0), stop=(ch == NCH - 1))
                    ins = t.matmul(pk[:, 512:1024], lhs, wk[:, ch * C + 512:(ch + 1) * C], start=(ch == 0), stop=(ch == NCH - 1))
                ins.then_inc(sT.inc(), 1)
            blk.tensor(t2)
            prod[("ffn_rmm", b, i)] = (sT.h, sT.v - 1)
            prod[("ffn_kmm", b, i)] = (sT.h, sT.v)

            def ssig(s, b=b, i=i, pp=pp):
                s.wait_ge(*prod[("ffn_rmm", b, i)])
                s.activation(rt[:, pp * C:(pp + 1) * C], pr[:], AF.Exp, scale=-1.0).then_inc(sS.inc(), 1)
            blk.scalar(ssig)
            prod[("ffn_sige", b, i)] = (sS.h, sS.v)

            def vsig(v, b=b, i=i, pp=pp):
                v.wait_ge(*prod[("ffn_sige", b, i)])
                v.tensor_scalar(rt[:, pp * C:(pp + 1) * C], rt[:, pp * C:(pp + 1) * C], 1.0, None, ALU.add).then_inc(sV.inc(), 1)
                v.wait_ge(sV.h, sV.v)
                v.reciprocal(rt[:, pp * C:(pp + 1) * C], rt[:, pp * C:(pp + 1) * C]).then_inc(sV.inc(), 1)
            blk.vector(vsig)
            prod[("ffn_sig", b, i)] = (sV.h, sV.v)

            def vrelu(v, b=b, i=i, pp=pp, k=k):
                v.wait_ge(*prod[("ffn_kmm", b, i)])
                if k >= 1:
                    kk = k - 1
                    v.wait_ge(*prod[("ffn_kaT", kk // NT, kk % NT)])
                v.tensor_scalar(kab[:, pp * C:(pp + 1) * C], pk[:], 0.0, None, ALU.max).then_inc(sV.inc(), 1)
                v.wait_ge(sV.h, sV.v)
                v.tensor_tensor(kab[:, pp * C:(pp + 1) * C], kab[:, pp * C:(pp + 1) * C],
                                kab[:, pp * C:(pp + 1) * C], ALU.mult).then_inc(sV.inc(), 1)
            blk.vector(vrelu)
            prod[("ffn_relu", b, i)] = (sV.h, sV.v)

            def t3(t, b=b, i=i, pp=pp, k=k):
                t.wait_ge(*prod[("ffn_relu", b, i)])
                if k >= 1:
                    kk = k - 1
                    t.wait_ge(*prod[("ffn_evk", kk // NT, kk % NT)])
                ins = None
                for ch in range(NCH):
                    ins = t.transpose(pkT[:, ch * 128:(ch + 1) * 128],
                                      kab[:, pp * C + ch * 128: pp * C + (ch + 1) * 128], idb[:])
                ins.then_inc(sT.inc(), 1)
            blk.tensor(t3)
            prod[("ffn_kaT", b, i)] = (sT.h, sT.v)

            def vevk(v, b=b, i=i, pp=pp, k=k):
                v.wait_ge(*prod[("ffn_kaT", b, i)])
                if k >= 2:
                    kk = k - 2
                    v.wait_ge(*prod[("ffn_vmm", kk // NT, kk % NT)])
                v.tensor_copy(kafm[:, pp * C:(pp + 1) * C], pkT[:]).then_inc(sV.inc(), 1)
            blk.vector(vevk)
            prod[("ffn_evk", b, i)] = (sV.h, sV.v)

            def t4(t, b=b, i=i, pp=pp, k=k):
                t.wait_ge(*prod[("ffn_evk", b, i)])
                if k >= 1:
                    kk = k - 1
                    t.wait_ge(*prod[("ffn_rv", kk // NT, kk % NT)])
                ins = None
                for ch in range(NCH):
                    lhs = kafm[:, pp * C + ch * 128: pp * C + (ch + 1) * 128]
                    t.matmul(pv[:, 0:512], lhs, wv[:, ch * C: ch * C + 512], start=(ch == 0), stop=(ch == NCH - 1))
                    ins = t.matmul(pv[:, 512:1024], lhs, wv[:, ch * C + 512:(ch + 1) * C], start=(ch == 0), stop=(ch == NCH - 1))
                ins.then_inc(sT.inc(), 1)
            blk.tensor(t4)
            prod[("ffn_vmm", b, i)] = (sT.h, sT.v)

            def vrv(v, b=b, i=i, pp=pp, xtile=xtile, k=k):
                v.wait_ge(*prod[("ffn_vmm", b, i)])
                v.wait_ge(*prod[("ffn_sig", b, i)])
                if k >= 2:
                    kk = k - 2
                    v.wait_ge(*prod[("ffn_lnout", kk // NT, kk % NT)])
                v.tensor_tensor(zt[:, pp * C:(pp + 1) * C], rt[:, pp * C:(pp + 1) * C], pv[:], ALU.mult).then_inc(sV.inc(), 1)
                v.wait_ge(sV.h, sV.v)
                v.tensor_tensor(zt[:, pp * C:(pp + 1) * C], zt[:, pp * C:(pp + 1) * C], xtile, ALU.add).then_inc(sV.inc(), 1)
            blk.vector(vrv)
            prod[("ffn_rv", b, i)] = (sV.h, sV.v)
            prod[("ffn_zdone", b, i)] = (sV.h, sV.v)

            slot = i % 4
            gkey = ("ffn_ydma", b, i // 4 - 1) if i // 4 >= 1 else (("ffn_ydma", b - 1, (NT - 1) // 4) if b > 0 else None)
            ews = [prod[("ffn_rv", b, i)]]
            if gkey is not None and gkey in prod:
                ews.append(prod[gkey])
            ln_emit(blk, sLN, zt[:, pp * C:(pp + 1) * C], yst[:, slot * C:(slot + 1) * C],
                    scr[:, pp * C:(pp + 1) * C], st[:, 8:12], C, prod, ("ffn_lnout", b, i),
                    extra_wait=ews)

            if slot == 3 or i == NT - 1:
                g0i = i - slot
                def sydma(sync, b=b, i=i, g0i=g0i, slot=slot):
                    for jj in range(g0i, i + 1):
                        sync.wait_ge(*prod[("ffn_lnout", b, jj)])
                    sync.dma_start(
                        y_ap[b].rearrange("(n p) c -> p n c", p=128)[:, g0i:i + 1],
                        yst[:, 0:(slot + 1) * C].rearrange("p (n c) -> p n c", c=C)).then_inc(sO.dinc(), 16)
                blk.sync(sydma)
                prod[("ffn_ydma", b, i // 4)] = (sO.h, sO.v)

    def sfin(sync):
        sync.wait_ge(sO.h, sO.v)
    blk.sync(sfin)
    return [v for v in locals().values() if isinstance(v, Sem)]


def l1ffn_host_inputs(W_r, W_k, W_v, mu_r, mu_k):
    C = W_r.shape[0]
    NCH = C // 128
    import ml_dtypes
    return {
        "f_WrT": np.ascontiguousarray(W_r.T),
        "f_WkT": np.ascontiguousarray(W_k.T),
        "f_WvT": np.ascontiguousarray(W_v.T),
        "f_mur": np.ascontiguousarray(mu_r.reshape(NCH, 128).T),
        "f_muk": np.ascontiguousarray(mu_k.reshape(NCH, 128).T),
        "f_ident": np.eye(128, dtype=ml_dtypes.bfloat16),
    }


def l0attn_host_inputs(Wqkv, bqkv, Wo, bo, pref="a_"):
    import ml_dtypes
    C = 1024
    FG = 16
    WqkvT = Wqkv.T
    cols_qk = []
    cols_v = []
    for fg in range(FG):
        cols_qk.extend(range(fg * 192, fg * 192 + 128))
        cols_v.extend(range(fg * 192 + 128, fg * 192 + 192))
    Wcat = np.ascontiguousarray(np.concatenate([WqkvT[:, cols_qk], WqkvT[:, cols_v]], axis=1))
    bqk = np.ascontiguousarray(np.stack([bqkv[cols_qk[fg * 128:(fg + 1) * 128]] for fg in range(FG)], axis=1))
    bv = bqkv[np.array(cols_v)].reshape(1, C).astype(np.float32)
    return {
        pref + "Wcat": Wcat.astype(np.float32),
        pref + "bqk": bqk.astype(np.float32),
        pref + "bv": bv,
        pref + "WoT": np.ascontiguousarray(Wo.T),
        pref + "bo": bo.reshape(1, C).astype(np.float32),
        pref + "ident": np.eye(128, dtype=ml_dtypes.bfloat16),
        pref + "ones": np.ones((1, 128), dtype=ml_dtypes.bfloat16),
    }


def build_l0attn(nc, es, blk, prod, B, pref="a_", x_ap=None, y_ap=None, ses=None):
    ses = ses or es
    C = 1024
    NH = 16
    T = NH * 128
    FG = 16
    NCH = 8
    NP = NH // 2
    NPAIR = B * NP
    w_d = nc.dram_tensor(pref + "Wcat", [C, 3 * C], F32, kind="ExternalInput").ap()
    bqk_d = nc.dram_tensor(pref + "bqk", [128, FG], F32, kind="ExternalInput").ap()
    bv_d = nc.dram_tensor(pref + "bv", [1, C], F32, kind="ExternalInput").ap()
    wo_d = nc.dram_tensor(pref + "WoT", [C, C], F32, kind="ExternalInput").ap()
    bo_d = nc.dram_tensor(pref + "bo", [1, C], F32, kind="ExternalInput").ap()
    id_d = nc.dram_tensor(pref + "ident", [128, 128], BF16, kind="ExternalInput").ap()
    on_d = nc.dram_tensor(pref + "ones", [1, 128], BF16, kind="ExternalInput").ap()

    E = es.enter_context
    wc = E(nc.sbuf_tensor(pref + "wc", [128, NCH * 3 * C], BF16))
    wo = E(nc.sbuf_tensor(pref + "wo", [128, NCH * C], BF16))
    bqk = E(nc.sbuf_tensor(pref + "bqk_s", [128, FG], F32))
    bvr = E(nc.sbuf_tensor(pref + "bvr", [1, C], BF16))
    bor = E(nc.sbuf_tensor(pref + "bor", [1, C], BF16))
    idb = E(nc.sbuf_tensor(pref + "idb", [128, 128], BF16))
    idf = E(nc.sbuf_tensor(pref + "idf", [128, 128], F32))
    ones = E(nc.sbuf_tensor(pref + "ones_s", [1, 128], BF16))
    xti = E(nc.sbuf_tensor(pref + "xti", [128, 4 * C], F32))
    xbf = E(nc.sbuf_tensor(pref + "xbf", [128, 2 * C], BF16))
    xfm = E(nc.sbuf_tensor(pref + "xfm", [128, 2 * C], BF16))
    qkA = E(nc.sbuf_tensor(pref + "qkA", [128, T], BF16))
    qkB = E(nc.sbuf_tensor(pref + "qkB", [128, T], BF16))
    kA = E(nc.sbuf_tensor(pref + "kA", [64, T], BF16))
    qB = E(nc.sbuf_tensor(pref + "qB", [128, T], BF16))
    v2 = E(nc.sbuf_tensor(pref + "v2", [128, 2 * FG * 65], BF16))
    pt = E(nc.sbuf_tensor(pref + "pt", [128, 4 * 1024], BF16))
    pvf = E(nc.sbuf_tensor(pref + "pvf", [65, 2 * 512], F32))
    vals = E(nc.sbuf_tensor(pref + "vals", [128, 2 * C], BF16))
    vfm = E(nc.sbuf_tensor(pref + "vfm", [128, C], BF16))
    zt = E(nc.sbuf_tensor(pref + "zt", [128, C], F32))
    scr = E(nc.sbuf_tensor(pref + "scr", [128, C], F32))
    st = E(nc.sbuf_tensor(pref + "st", [128, 8], F32))
    yst = E(nc.sbuf_tensor(pref + "yst", [128, 2 * C], F32))

    pool = E(nc.psum_tensor(pref + "pool", [128, 4096], F32))
    poolb = pool[:].bitcast(BF16)
    psc = [pool[:, r * 1024:(r + 1) * 1024] for r in range(2)]
    pxT = poolb[:, 5120:6144]
    pqk = pool[:, 2048:2560]
    pv_ = pool[:, 2560:3072]
    ppv = [pool[0:65, 3072 + hh * 512: 3072 + hh * 512 + 512] for hh in range(2)]
    pvT = [pool[0:128, 3072 + hh * 512: 3072 + hh * 512 + 260] for hh in range(2)]
    pvalT = poolb[:, 6144:7168]
    po = pool[:, 3072:4096]

    sW = Sem(ses, nc, pref + "sW")
    sC = Sem(ses, nc, pref + "sC")
    sXa = Sem(ses, nc, pref + "sXa")
    sXb = Sem(ses, nc, pref + "sXb")
    sXh = [sXa, sXb]
    sX3a = Sem(ses, nc, pref + "sX3a")
    sX3b = Sem(ses, nc, pref + "sX3b")
    sX3h = [sX3a, sX3b]
    sT = Sem(ses, nc, pref + "sT")
    sV = Sem(ses, nc, pref + "sV")
    sS = Sem(ses, nc, pref + "sS")
    sG = Sem(ses, nc, pref + "sG")
    sAO0 = Sem(ses, nc, pref + "sAO0")
    sAO1 = Sem(ses, nc, pref + "sAO1")
    sAOp = [sAO0, sAO1]
    sAOr = Sem(ses, nc, pref + "sAOr")
    sO = Sem(ses, nc, pref + "sO")
    sLN = Sem(ses, nc, pref + "sLN")

    def g0(g):
        g.dma_start(wc[:].rearrange("p (ch d) -> p ch d", d=3 * C),
                    w_d.rearrange("(ch p) d -> p ch d", p=128)).then_inc(sW.dinc(), 16)
        g.dma_start(wo[:].rearrange("p (ch d) -> p ch d", d=C),
                    wo_d.rearrange("(ch p) d -> p ch d", p=128)).then_inc(sW.dinc(), 16)
        g.dma_start(bvr[:], bv_d).then_inc(sW.dinc(), 16)
        g.dma_start(bor[:], bo_d).then_inc(sW.dinc(), 16)
    blk.gpsimd(g0)
    W_ALL = sW.v

    def s0(sync):
        sync.dma_start(bqk[:], bqk_d).then_inc(sC.dinc(), 16)
        sync.dma_start(idb[:], id_d).then_inc(sC.dinc(), 16)
        sync.dma_start(ones[:], on_d).then_inc(sC.dinc(), 16)
    blk.sync(s0)
    CIN = sC.v

    def v00(v):
        v.wait_ge(sC.h, CIN)
        v.tensor_copy(idf[:], idb[:])
        v.memset(v2[:].rearrange("p (a fg e) -> p a fg e", fg=FG, e=65)[:, :, :, 64:65], 1.0).then_inc(sV.inc(), 1)
    blk.vector(v00)
    prod["a_init"] = (sV.h, sV.v)

    def emit_ph1(kpair):
        b, hp = divmod(kpair, NP)
        for hh in range(2):
            h = 2 * hp + hh

            def sx(sync, b=b, h=h, hh=hh, kpair=kpair):
                if kpair >= 1:
                    sync.wait_ge(*prod[("a_xT", kpair - 1, hh)])
                sync.dma_start(xti[:, hh * C:(hh + 1) * C],
                               x_ap[b].rearrange("(n p) c -> p n c", p=128)[:, h]).then_inc(sXh[hh].dinc(), 16)
            blk.sync(sx)
            prod[("a_x", kpair, hh)] = (sXh[hh].h, sXh[hh].v)

            def vc(v, hh=hh, kpair=kpair):
                v.wait_ge(*prod[("a_x", kpair, hh)])
                if kpair >= 1:
                    v.wait_ge(*prod[("a_xT", kpair - 1, hh)])
                v.tensor_copy(xbf[:, hh * C:(hh + 1) * C], xti[:, hh * C:(hh + 1) * C]).then_inc(sV.inc(), 1)
            blk.vector(vc)
            prod[("a_cast", kpair, hh)] = (sV.h, sV.v)

            def t_xT(t, hh=hh, kpair=kpair):
                t.wait_ge(sC.h, CIN)
                t.wait_ge(*prod[("a_cast", kpair, hh)])
                pk = (kpair, 0) if hh == 1 else ((kpair - 1, 1) if kpair >= 1 else None)
                if pk is not None:
                    t.wait_ge(*prod[("a_vev", pk[0], pk[1], 1)])
                ins = None
                for ch in range(NCH):
                    ins = t.transpose(pxT[:, ch * 128:(ch + 1) * 128],
                                      xbf[:, hh * C + ch * 128: hh * C + (ch + 1) * 128], idb[:])
                ins.then_inc(sT.inc(), 1)
            blk.tensor(t_xT)
            prod[("a_xT", kpair, hh)] = (sT.h, sT.v)

            def vev(v, hh=hh, kpair=kpair):
                v.wait_ge(*prod[("a_xT", kpair, hh)])
                if kpair >= 1:
                    t = prod.get(("a_vmm", kpair - 1, hh, 1))
                    if t:
                        v.wait_ge(*t)
                v.tensor_copy(xfm[:, hh * C:(hh + 1) * C], pxT[:]).then_inc(sV.inc(), 1)
            blk.vector(vev)
            prod[("a_evxT", kpair, hh)] = (sV.h, sV.v)

            for grp in range(4):
                def t_qk(t, hh=hh, kpair=kpair, grp=grp):
                    t.wait_ge(sW.h, W_ALL)
                    t.wait_ge(*prod[("a_evxT", kpair, hh)])
                    pk = (kpair, hh, grp - 1) if grp >= 1 else (
                        (kpair, 0, 3) if hh == 1 else ((kpair - 1, 1, 3) if kpair >= 1 else None))
                    if pk is not None:
                        t.wait_ge(*prod[("a_qkev", *pk)])
                    ins = None
                    for fgi in range(4):
                        fg = grp * 4 + fgi
                        for ch in range(NCH):
                            ins = t.matmul(pqk[:, fgi * 128:(fgi + 1) * 128],
                                           wc[:, ch * 3 * C + fg * 128: ch * 3 * C + (fg + 1) * 128],
                                           xfm[:, hh * C + ch * 128: hh * C + (ch + 1) * 128],
                                           start=(ch == 0), stop=(ch == NCH - 1))
                    ins.then_inc(sT.inc(), 1)
                blk.tensor(t_qk)
                prod[("a_qkmm", kpair, hh, grp)] = (sT.h, sT.v)

                def v_qkev(v, hh=hh, kpair=kpair, grp=grp):
                    v.wait_ge(*prod[("a_qkmm", kpair, hh, grp)])
                    if kpair >= 1:
                        v.wait_ge(*prod[("a_scdone", kpair - 1, hh)])
                    ins = None
                    for fgi in range(4):
                        fg = grp * 4 + fgi
                        src = pqk[:, fgi * 128:(fgi + 1) * 128]
                        dst = (qkA if hh == 0 else qkB)[:, fg * 128:(fg + 1) * 128]
                        ins = v.tensor_scalar(dst, src, bqk[:, fg:fg + 1], None, ALU.add)
                    ins.then_inc(sV.inc(), 1)
                blk.vector(v_qkev)
                prod[("a_qkev", kpair, hh, grp)] = (sV.h, sV.v)

            def g_shift(g, hh=hh, kpair=kpair):
                for grp in range(4):
                    g.wait_ge(*prod[("a_qkev", kpair, hh, grp)])
                if kpair >= 1:
                    g.wait_ge(*prod[("a_scdone", kpair - 1, hh)])
                if hh == 0:
                    g.dma_start(kA[0:64, :], qkA[64:128, :]).then_inc(sG.dinc(), 16)
                else:
                    g.dma_start(qB[64:128, :], qkB[0:64, :]).then_inc(sG.dinc(), 16)
            blk.gpsimd(g_shift)
            prod[("a_shift", kpair, hh)] = (sG.h, sG.v)

            for half in range(2):
                def t_v(t, hh=hh, kpair=kpair, half=half):
                    t.wait_ge(sW.h, W_ALL)
                    t.wait_ge(*prod[("a_evxT", kpair, hh)])
                    if half == 1:
                        t.wait_ge(*prod[("a_vev", kpair, hh, 0)])
                    elif hh == 1:
                        t.wait_ge(*prod[("a_vev", kpair, 0, 1)])
                    elif kpair >= 1:
                        t.wait_ge(*prod[("a_vev", kpair - 1, 1, 1)])
                    for ch in range(NCH):
                        t.matmul(pv_[:], xfm[:, hh * C + ch * 128: hh * C + (ch + 1) * 128],
                                 wc[:, ch * 3 * C + 2 * C + half * 512: ch * 3 * C + 2 * C + (half + 1) * 512],
                                 start=(ch == 0), stop=False)
                    ins = t.matmul(pv_[:], ones[:], bvr[0:1, half * 512:(half + 1) * 512],
                                   start=False, stop=True)
                    ins.then_inc(sT.inc(), 1)
                blk.tensor(t_v)
                prod[("a_vmm", kpair, hh, half)] = (sT.h, sT.v)

                def v_vev(v, hh=hh, kpair=kpair, half=half):
                    v.wait_ge(*prod[("a_vmm", kpair, hh, half)])
                    v.wait_ge(*prod["a_init"])
                    if kpair >= 1:
                        v.wait_ge(*prod[("a_pvmm", kpair - 1, 3, hh, 7)])
                    dst = v2[:, hh * FG * 65:(hh + 1) * FG * 65].rearrange(
                        "p (fg e) -> p fg e", e=65)[:, half * 8:(half + 1) * 8, 0:64]
                    v.tensor_copy(dst, pv_[:].rearrange("p (fg e) -> p fg e", e=64)).then_inc(sV.inc(), 1)
                blk.vector(v_vev)
                prod[("a_vev", kpair, hh, half)] = (sV.h, sV.v)

    def emit_ph2(kpair):
        pv_emitters = []
        for qc in range(4):
            for hh in range(2):
                for fgp in range(8):
                    idx = qc * 16 + hh * 8 + fgp
                    rot = idx % 2

                    def t_sc(t, hh=hh, kpair=kpair, qc=qc, fgp=fgp, rot=rot, idx=idx):
                        if idx == 0:
                            t.wait_ge(*prod[("a_shift", kpair, 1)])
                        if idx >= 2:
                            pi = idx - 2
                            t.wait_ge(*prod[("a_exp", kpair, pi // 16, (pi % 16) // 8, pi % 8)])
                        elif kpair >= 1:
                            t.wait_ge(*prod[("a_exp", kpair - 1, 3, 1, 6 + idx)])
                        ins = None
                        for fgi in range(2):
                            fg = fgp * 2 + fgi
                            if hh == 0:
                                lhsT = kA[0:64, fg * 128:(fg + 1) * 128]
                                rhs = qkA[0:64, qc * 512:(qc + 1) * 512]
                            else:
                                lhsT = qkB[64:128, fg * 128:(fg + 1) * 128]
                                rhs = qB[64:128, qc * 512:(qc + 1) * 512]
                            ins = t.matmul(psc[rot][:, fgi * 512:(fgi + 1) * 512],
                                           lhsT, rhs, start=True, stop=True)
                        ins.then_inc(sT.inc(), 1)
                    blk.tensor(t_sc)
                    prod[("a_scmm", kpair, qc, hh, fgp)] = (sT.h, sT.v)

                    def s_exp(s, hh=hh, kpair=kpair, qc=qc, fgp=fgp, rot=rot):
                        s.wait_ge(*prod[("a_scmm", kpair, qc, hh, fgp)])
                        if fgp >= 2:
                            s.wait_ge(*prod[("a_pvmm", kpair, qc, hh, fgp - 2)])
                        elif qc >= 1:
                            s.wait_ge(*prod[("a_pvmm", kpair, qc - 1, hh, 6 + fgp)])
                        elif kpair >= 1:
                            s.wait_ge(*prod[("a_pvmm", kpair - 1, 3, hh, 6 + fgp)])
                        s.activation(pt[:, (hh * 2 + fgp % 2) * 1024:(hh * 2 + fgp % 2 + 1) * 1024],
                                     psc[rot][:], AF.Exp, scale=0.125).then_inc(sS.inc(), 1)
                    blk.scalar(s_exp)
                    prod[("a_exp", kpair, qc, hh, fgp)] = (sS.h, sS.v)

                    def mk_pv(hh=hh, kpair=kpair, qc=qc, fgp=fgp):
                      def t_pv(t, hh=hh, kpair=kpair, qc=qc, fgp=fgp):
                        t.wait_ge(*prod[("a_exp", kpair, qc, hh, fgp)])
                        if fgp == 0:
                            if qc == 0:
                                if kpair >= 1:
                                    t.wait_ge(*prod[("a_oz", kpair - 1, 1)])
                            else:
                                t.wait_ge(*prod[("a_pvev", kpair, qc - 1, hh)])
                                t.wait_ge(*prod[("a_norm", kpair, qc - 1, hh)])
                        ins = None
                        for fgi in range(2):
                            fg = fgp * 2 + fgi
                            vslab = v2[:, hh * FG * 65 + fg * 65:hh * FG * 65 + (fg + 1) * 65]
                            ins = t.matmul(ppv[hh], vslab,
                                           pt[:, (hh * 2 + fgp % 2) * 1024 + fgi * 512:(hh * 2 + fgp % 2) * 1024 + (fgi + 1) * 512],
                                           start=(fgp == 0 and fgi == 0), stop=(fgp == 7 and fgi == 1))
                        ins.then_inc(sT.inc(), 1)
                      t_pv._key = (kpair, qc, hh, fgp)
                      return t_pv
                    pv_emitters.append(mk_pv())
                    if len(pv_emitters) >= 2:
                        em = pv_emitters.pop(0)
                        blk.tensor(em)
                        prod[("a_pvmm",) + em._key] = (sT.h, sT.v)

                while pv_emitters:
                    em = pv_emitters.pop(0)
                    blk.tensor(em)
                    prod[("a_pvmm",) + em._key] = (sT.h, sT.v)

                def v_pvev(v, hh=hh, kpair=kpair, qc=qc):
                    v.wait_ge(*prod[("a_pvmm", kpair, qc, hh, 7)])
                    if qc >= 2:
                        v.wait_ge(*prod[("a_pvT", kpair, qc - 2, hh)])
                    elif kpair >= 1:
                        v.wait_ge(*prod[("a_pvT", kpair - 1, qc + 2, hh)])
                    v.tensor_copy(pvf[:, (qc % 2) * 512:(qc % 2 + 1) * 512], ppv[hh]).then_inc(sV.inc(), 1)
                blk.vector(v_pvev)
                prod[("a_pvev", kpair, qc, hh)] = (sV.h, sV.v)

                def t_pvT(t, hh=hh, kpair=kpair, qc=qc):
                    t.wait_ge(*prod[("a_pvev", kpair, qc, hh)])
                    t.wait_ge(*prod["a_init"])
                    ins = None
                    for sub in range(4):
                        ins = t.transpose(pvT[hh][:, sub * 65:(sub + 1) * 65],
                                          pvf[0:65, (qc % 2) * 512 + sub * 128:(qc % 2) * 512 + (sub + 1) * 128],
                                          idf[0:65, 0:65])
                    ins.then_inc(sT.inc(), 1)
                blk.tensor(t_pvT)
                prod[("a_pvT", kpair, qc, hh)] = (sT.h, sT.v)

                def v_norm(v, hh=hh, kpair=kpair, qc=qc):
                    v.wait_ge(*prod[("a_pvT", kpair, qc, hh)])
                    if qc == 0 and kpair >= 1:
                        v.wait_ge(*prod[("a_vT", kpair - 1, hh)])
                    ins = None
                    for sub in range(4):
                        ins = v.reciprocal(st[:, sub:sub + 1], pvT[hh][:, sub * 65 + 64:sub * 65 + 65])
                    ins.then_inc(sV.inc(), 1)
                    v.wait_ge(sV.h, sV.v)
                    for sub in range(4):
                        fgq = qc * 4 + sub
                        ins = v.tensor_scalar(vals[:, hh * C + fgq * 64: hh * C + (fgq + 1) * 64],
                                              pvT[hh][:, sub * 65:sub * 65 + 64], st[:, sub:sub + 1], None, ALU.mult)
                    ins.then_inc(sV.inc(), 1)
                blk.vector(v_norm)
                prod[("a_norm", kpair, qc, hh)] = (sV.h, sV.v)

        prod[("a_scdone", kpair, 0)] = prod[("a_scmm", kpair, 3, 0, 7)]
        prod[("a_scdone", kpair, 1)] = prod[("a_scmm", kpair, 3, 1, 7)]

    def emit_ph3(kpair):
        b, hp = divmod(kpair, NP)
        for hh in range(2):
            h = 2 * hp + hh
            px = 2 + hh

            def sx3(sync, b=b, h=h, hh=hh, kpair=kpair, px=px):
                if kpair >= 1:
                    sync.wait_ge(*prod[("a_oz", kpair - 1, hh)])
                sync.dma_start(xti[:, px * C:(px + 1) * C],
                               x_ap[b].rearrange("(n p) c -> p n c", p=128)[:, h]).then_inc(sX3h[hh].dinc(), 16)
            blk.sync(sx3)
            prod[("a_x3", kpair, hh)] = (sX3h[hh].h, sX3h[hh].v)

            def t_vT(t, hh=hh, kpair=kpair):
                for qcx in range(4):
                    t.wait_ge(*prod[("a_norm", kpair, qcx, hh)])
                t.wait_ge(*prod[("a_pvev", kpair, 3, hh)])
                if hh == 1:
                    t.wait_ge(*prod[("a_oz", kpair, 0)])
                elif kpair >= 1:
                    t.wait_ge(*prod[("a_oz", kpair - 1, 1)])
                ins = None
                for ch in range(NCH):
                    ins = t.transpose(pvalT[:, ch * 128:(ch + 1) * 128],
                                      vals[:, hh * C + ch * 128: hh * C + (ch + 1) * 128], idb[:])
                ins.then_inc(sT.inc(), 1)
            blk.tensor(t_vT)
            prod[("a_vT", kpair, hh)] = (sT.h, sT.v)

            def v_vfm(v, hh=hh, kpair=kpair):
                v.wait_ge(*prod[("a_vT", kpair, hh)])
                if hh == 1:
                    v.wait_ge(*prod[("a_omm", kpair, 0)])
                elif kpair >= 1:
                    v.wait_ge(*prod[("a_omm", kpair - 1, 1)])
                v.tensor_copy(vfm[:], pvalT[:]).then_inc(sV.inc(), 1)
            blk.vector(v_vfm)
            prod[("a_ovfm", kpair, hh)] = (sV.h, sV.v)

            def t_o(t, hh=hh, kpair=kpair):
                t.wait_ge(sW.h, W_ALL)
                t.wait_ge(*prod[("a_ovfm", kpair, hh)])
                if hh == 0:
                    t.wait_ge(*prod[("a_norm", kpair, 3, 1)])
                ins = None
                for half in range(2):
                    for ch in range(NCH):
                        t.matmul(po[:, half * 512:(half + 1) * 512],
                                 vfm[:, ch * 128:(ch + 1) * 128],
                                 wo[:, ch * C + half * 512: ch * C + (half + 1) * 512],
                                 start=(ch == 0), stop=False)
                    ins = t.matmul(po[:, half * 512:(half + 1) * 512], ones[:],
                                   bor[0:1, half * 512:(half + 1) * 512], start=False, stop=True)
                ins.then_inc(sT.inc(), 1)
            blk.tensor(t_o)
            prod[("a_omm", kpair, hh)] = (sT.h, sT.v)

            def v_z(v, hh=hh, kpair=kpair, px=px):
                v.wait_ge(*prod[("a_omm", kpair, hh)])
                v.wait_ge(*prod[("a_x3", kpair, hh)])
                if hh == 1:
                    v.wait_ge(*prod[("a_lnout", kpair, 0)])
                elif kpair >= 1:
                    v.wait_ge(*prod[("a_lnout", kpair - 1, 1)])
                v.tensor_tensor(zt[:], po[:], xti[:, px * C:(px + 1) * C], ALU.add).then_inc(sV.inc(), 1)
            blk.vector(v_z)
            prod[("a_oz", kpair, hh)] = (sV.h, sV.v)

            ews = [prod[("a_oz", kpair, hh)]]
            if kpair >= 1 and hh == 0:
                ews.append(prod[("a_ydma", kpair - 1)])
            ln_emit(blk, sLN, zt[:], yst[:, hh * C:(hh + 1) * C], scr[:], st[:, 4:8], C,
                    prod, ("a_lnout", kpair, hh), extra_wait=ews)

        def sydma(sync, b=b, hp=hp, kpair=kpair):
            sync.wait_ge(*prod[("a_lnout", kpair, 0)])
            sync.wait_ge(*prod[("a_lnout", kpair, 1)])
            sync.dma_start(y_ap[b].rearrange("(n p) c -> p n c", p=128)[:, 2 * hp:2 * hp + 2],
                           yst[:].rearrange("p (n c) -> p n c", c=C)).then_inc(sO.dinc(), 16)
        blk.sync(sydma)
        prod[("a_ydma", kpair)] = (sO.h, sO.v)

    emit_ph1(0)
    for k in range(NPAIR):
        emit_ph2(k)
        if k + 1 < NPAIR:
            emit_ph1(k + 1)
        emit_ph3(k)

    def sfin(sync):
        sync.wait_ge(sO.h, sO.v)
    blk.sync(sfin)
    return [v for v in locals().values() if isinstance(v, Sem)]


def build_l0moe(nc, es, blk, prod, B, pref="m_", x_ap=None, y_ap=None, ses=None):
    ses = ses or es
    """L0 MoE: y = LN(2u + routed + shared(+biases)). u = x_ap [B, 2048, 1024] fp32.
    Host inputs: m_WrT [16, 1024, 1024] f32 (each Wr[e].T), m_WsT [2, 1024, 1024] f32,
    m_br [16, 1024] f32, m_bs [1, 2048] f32 (two rows concat), m_centT [1024, 16] f32,
    m_eb [1, 16] f32, m_ident [128,128] bf16, m_onesb [1,128] bf16, m_onesf [1,128] f32."""
    C = 1024
    NT = 16
    T = NT * 128
    NE = 16
    NCH = 8
    wr_d = nc.dram_tensor(pref + "WrT", [NE, C, C], F32, kind="ExternalInput").ap()
    ws_d = nc.dram_tensor(pref + "WsT", [2, C, C], F32, kind="ExternalInput").ap()
    br_d = nc.dram_tensor(pref + "br", [NE, C], F32, kind="ExternalInput").ap()
    bs_d = nc.dram_tensor(pref + "bs", [1, 2 * C], F32, kind="ExternalInput").ap()
    ct_d = nc.dram_tensor(pref + "centT", [C, NE], F32, kind="ExternalInput").ap()
    eb_d = nc.dram_tensor(pref + "eb", [1, NE], F32, kind="ExternalInput").ap()
    id_d = nc.dram_tensor(pref + "ident", [128, 128], BF16, kind="ExternalInput").ap()
    ob_d = nc.dram_tensor(pref + "onesb", [1, 128], BF16, kind="ExternalInput").ap()
    of_d = nc.dram_tensor(pref + "onesf", [1, 128], F32, kind="ExternalInput").ap()

    E = es.enter_context
    ut = E(nc.sbuf_tensor(pref + "ut", [128, NT * C], F32))
    ufm = E(nc.sbuf_tensor(pref + "ufm", [128, NT * C], BF16))
    zbf = E(nc.sbuf_tensor(pref + "zbf", [128, NT * C], BF16))
    wre = E(nc.sbuf_tensor(pref + "wre", [128, 2 * NCH * C], BF16))
    ubf = E(nc.sbuf_tensor(pref + "ubf", [128, C], BF16))
    uf32 = E(nc.sbuf_tensor(pref + "uf32", [128, C], F32))
    gal = E(nc.sbuf_tensor(pref + "gal", [128, NT * NE], F32))
    ct = E(nc.sbuf_tensor(pref + "ct", [128, NCH * NE], F32))
    ebr = E(nc.sbuf_tensor(pref + "ebr", [1, NE], F32))
    brp = E(nc.sbuf_tensor(pref + "brp", [1, 2 * C], BF16))
    bsr = E(nc.sbuf_tensor(pref + "bsr", [1, 2 * C], BF16))
    idb = E(nc.sbuf_tensor(pref + "idb", [128, 128], BF16))
    idf = E(nc.sbuf_tensor(pref + "idf", [128, 128], F32))
    onb = E(nc.sbuf_tensor(pref + "onb", [1, 128], BF16))
    onf = E(nc.sbuf_tensor(pref + "onf", [1, 128], F32))
    stx = E(nc.sbuf_tensor(pref + "stx", [128, 16], F32))
    zt = E(nc.sbuf_tensor(pref + "zt", [128, C], F32))
    scr = E(nc.sbuf_tensor(pref + "scr", [128, C], F32))
    yst = E(nc.sbuf_tensor(pref + "yst", [128, 2 * C], F32))

    pool = E(nc.psum_tensor(pref + "pool", [128, 4096], F32))
    poolb = pool[:].bitcast(BF16)
    pTf = pool[:, 0:1024]
    pTb = poolb[:, 2048:3072]
    pg = pool[:, 1536:1552]
    pe = [pool[:, 2048 + i * 1024: 2048 + (i + 1) * 1024] for i in range(2)]

    sW0 = Sem(ses, nc, pref + "sW0")
    sW1 = Sem(ses, nc, pref + "sW1")
    sWs = [sW0, sW1]
    sB = Sem(ses, nc, pref + "sB")
    sC = Sem(ses, nc, pref + "sC")
    sX = Sem(ses, nc, pref + "sX")
    sT = Sem(ses, nc, pref + "sT")
    sV = Sem(ses, nc, pref + "sV")
    sS = Sem(ses, nc, pref + "sS")
    sAO0 = Sem(ses, nc, pref + "sAO0")
    sAO1 = Sem(ses, nc, pref + "sAO1")
    sAOp = [sAO0, sAO1]
    sAOr = Sem(ses, nc, pref + "sAOr")
    sO = Sem(ses, nc, pref + "sO")
    sLN = Sem(ses, nc, pref + "sLN")

    def s0(sync):
        sync.dma_start(ct[:].rearrange("p (ch e) -> p ch e", e=NE),
                       ct_d.rearrange("(ch p) e -> p ch e", p=128)).then_inc(sC.dinc(), 16)
        sync.dma_start(ebr[:], eb_d).then_inc(sC.dinc(), 16)
        sync.dma_start(idb[:], id_d).then_inc(sC.dinc(), 16)
        sync.dma_start(onb[:], ob_d).then_inc(sC.dinc(), 16)
        sync.dma_start(onf[:], of_d).then_inc(sC.dinc(), 16)
    blk.sync(s0)
    CIN = sC.v

    def g0(g):
        g.dma_start(bsr[:], bs_d).then_inc(sB.dinc(), 16)
    blk.gpsimd(g0)

    def v0(v):
        v.wait_ge(sC.h, CIN)
        v.tensor_copy(idf[:], idb[:]).then_inc(sV.inc(), 1)
    blk.vector(v0)
    prod["m_init"] = (sV.h, sV.v)

    for b in range(B):
        def gw(g, b=b):
            g.wait_ge(sC.h, CIN)
            if b > 0:
                g.wait_ge(*prod[("m_emm", b - 1, NE - 1, NT - 1)])
                g.wait_ge(*prod[("m_eev", b - 1, NE - 1, NT - 1)])
            g.dma_start(wre[:, 0:NCH * C].rearrange("p (ch d) -> p ch d", d=C),
                        ws_d[0].rearrange("(ch p) d -> p ch d", p=128)).then_inc(sW0.dinc(), 16)
            g.dma_start(wre[:, NCH * C:].rearrange("p (ch d) -> p ch d", d=C),
                        ws_d[1].rearrange("(ch p) d -> p ch d", p=128)).then_inc(sW1.dinc(), 16)
        blk.gpsimd(gw)
        WS_IN = (sW0.v, sW1.v)
        def sxb(sync, b=b):
            if b > 0:
                sync.wait_ge(*prod[("m_sev", b - 1, NT - 1)])
            sync.dma_start(ut[:].rearrange("p (n c) -> p n c", c=C),
                           x_ap[b].rearrange("(n p) c -> p n c", p=128)).then_inc(sX.dinc(), 16)
        blk.sync(sxb)
        prod[("m_x", b)] = (sX.h, sX.v)

        for i in range(NT):
            def vc(v, b=b, i=i):
                v.wait_ge(*prod[("m_x", b)])
                if b > 0 and i == 0:
                    v.wait_ge(*prod[("m_emm", b - 1, NE - 1, NT - 1)])
                v.tensor_copy(ubf[:], ut[:, i * C:(i + 1) * C]).then_inc(sV.inc(), 1)
            blk.vector(vc)
            prod[("m_cast", b, i)] = (sV.h, sV.v)

            def t_bT(t, b=b, i=i):
                t.wait_ge(sC.h, CIN)
                t.wait_ge(*prod[("m_cast", b, i)])
                if i >= 1:
                    t.wait_ge(*prod[("m_evb", b, i - 1)])
                elif b > 0:
                    t.wait_ge(*prod[("m_evb", b - 1, NT - 1)])
                ins = None
                for ch in range(NCH):
                    ins = t.transpose(pTb[:, ch * 128:(ch + 1) * 128],
                                      ubf[:, ch * 128:(ch + 1) * 128], idb[:])
                ins.then_inc(sT.inc(), 1)
            blk.tensor(t_bT)
            prod[("m_bT", b, i)] = (sT.h, sT.v)

            def vevb(v, b=b, i=i):
                v.wait_ge(*prod[("m_bT", b, i)])
                v.tensor_copy(ufm[:, i * C:(i + 1) * C], pTb[:]).then_inc(sV.inc(), 1)
            blk.vector(vevb)
            prod[("m_evb", b, i)] = (sV.h, sV.v)

            def t_fT(t, b=b, i=i):
                t.wait_ge(*prod["m_init"])
                t.wait_ge(*prod[("m_x", b)])
                if i >= 1:
                    t.wait_ge(*prod[("m_evf", b, i - 1)])
                elif b > 0:
                    t.wait_ge(*prod[("m_evf", b - 1, NT - 1)])
                ins = None
                for ch in range(NCH):
                    ins = t.transpose(pTf[:, ch * 128:(ch + 1) * 128],
                                      ut[:, i * C + ch * 128: i * C + (ch + 1) * 128], idf[:])
                ins.then_inc(sT.inc(), 1)
            blk.tensor(t_fT)
            prod[("m_fT", b, i)] = (sT.h, sT.v)

            def vevf(v, b=b, i=i):
                v.wait_ge(*prod[("m_fT", b, i)])
                if i >= 1:
                    v.wait_ge(*prod[("m_gmm", b, i - 1)])
                v.tensor_copy(uf32[:], pTf[:]).then_inc(sV.inc(), 1)
            blk.vector(vevf)
            prod[("m_evf", b, i)] = (sV.h, sV.v)

            def t_g(t, b=b, i=i):
                t.wait_ge(sC.h, CIN)
                t.wait_ge(*prod[("m_evf", b, i)])
                if i >= 1:
                    t.wait_ge(*prod[("m_sm", b, i - 1)])
                elif b > 0:
                    t.wait_ge(*prod[("m_sm", b - 1, NT - 1)])
                for ch in range(NCH):
                    t.matmul(pg[:], uf32[:, ch * 128:(ch + 1) * 128],
                             ct[:, ch * NE:(ch + 1) * NE], start=(ch == 0), stop=False)
                t.matmul(pg[:], onf[:], ebr[:], start=False, stop=True).then_inc(sT.inc(), 1)
            blk.tensor(t_g)
            prod[("m_gmm", b, i)] = (sT.h, sT.v)

            def v_sm1(v, b=b, i=i):
                v.wait_ge(*prod[("m_gmm", b, i)])
                v.reduce_max(stx[:, 0:1], pg[:], axis=AX, negate=True).then_inc(sV.inc(), 1)
            blk.vector(v_sm1)
            prod[("m_nm", b, i)] = (sV.h, sV.v)

            def s_sm(s, b=b, i=i):
                s.wait_ge(*prod[("m_nm", b, i)])
                if b > 0 and i == 0:
                    s.wait_ge(*prod[("m_eev", b - 1, NE - 1, NT - 1)])
                s.activation(gal[:, i * NE:(i + 1) * NE], pg[:], AF.Exp,
                             bias=stx[:, 0:1], accum_out=stx[:, 1:2]).then_inc(sS.inc(), 1)
            blk.scalar(s_sm)
            prod[("m_exp", b, i)] = (sS.h, sS.v)

            def v_sm2(v, b=b, i=i):
                g = gal[:, i * NE:(i + 1) * NE]
                v.wait_ge(*prod[("m_exp", b, i)])
                if b > 0 and i == 0:
                    v.wait_ge(*prod[("m_fln", b - 1, NT - 1)])
                v.reciprocal(stx[:, 2:3], stx[:, 1:2]).then_inc(sV.inc(), 1)
                v.wait_ge(sV.h, sV.v)
                v.tensor_scalar(g, g, stx[:, 2:3], None, ALU.mult).then_inc(sV.inc(), 1)
                v.wait_ge(sV.h, sV.v)
                v.reduce_max(stx[:, 3:4], g, axis=AX).then_inc(sV.inc(), 1)
                v.wait_ge(sV.h, sV.v)
                v.tensor_scalar(scr[:, 0:NE], g, stx[:, 3:4], None, ALU.is_equal).then_inc(sV.inc(), 1)
                v.wait_ge(sV.h, sV.v)
                v.scalar_tensor_tensor(scr[:, NE:2 * NE], scr[:, 0:NE], -1e9, g, ALU.mult, ALU.add).then_inc(sV.inc(), 1)
                v.wait_ge(sV.h, sV.v)
                v.reduce_max(stx[:, 4:5], scr[:, NE:2 * NE], axis=AX).then_inc(sV.inc(), 1)
                v.wait_ge(sV.h, sV.v)
                v.tensor_scalar(scr[:, 0:NE], g, stx[:, 4:5], None, ALU.is_ge).then_inc(sV.inc(), 1)
                v.wait_ge(sV.h, sV.v)
                v.tensor_tensor(g, g, scr[:, 0:NE], ALU.mult).then_inc(sV.inc(), 1)
            blk.vector(v_sm2)
            prod[("m_sm", b, i)] = (sV.h, sV.v)

        for i in range(NT):
            def t_s(t, b=b, i=i):
                t.wait_ge(sW0.h, WS_IN[0])
                t.wait_ge(sW1.h, WS_IN[1])
                t.wait_ge(sB.h, 16)
                t.wait_ge(*prod[("m_evb", b, i)])
                if i >= 2:
                    t.wait_ge(*prod[("m_sev", b, i - 2)])
                elif b > 0:
                    t.wait_ge(*prod[("m_fln", b - 1, NT - 1)])
                ins = None
                for half in range(2):
                    dst = pe[i % 2][:, half * 512:(half + 1) * 512]
                    for w in range(2):
                        for ch in range(NCH):
                            t.matmul(dst, ufm[:, i * C + ch * 128: i * C + (ch + 1) * 128],
                                     wre[:, w * NCH * C + ch * C + half * 512: w * NCH * C + ch * C + (half + 1) * 512],
                                     start=(w == 0 and ch == 0), stop=False)
                    t.matmul(dst, onb[:], bsr[0:1, half * 512:(half + 1) * 512], start=False, stop=False)
                    ins = t.matmul(dst, onb[:], bsr[0:1, C + half * 512:C + (half + 1) * 512], start=False, stop=True)
                ins.then_inc(sT.inc(), 1)
            blk.tensor(t_s)
            prod[("m_smm", b, i)] = (sT.h, sT.v)

            def v_sev(v, b=b, i=i):
                v.wait_ge(*prod[("m_smm", b, i)])
                if i >= 1:
                    v.wait_ge(*prod[("m_sev", b, i - 1)])
                elif b > 0:
                    v.wait_ge(*prod[("m_fln", b - 1, NT - 1)])
                v.scalar_tensor_tensor(zt[:], pe[i % 2][:], 0.5, ut[:, i * C:(i + 1) * C],
                                       ALU.mult, ALU.add).then_inc(sV.inc(), 1)
                v.wait_ge(sV.h, sV.v)
                v.tensor_tensor(zbf[:, i * C:(i + 1) * C], zt[:], ut[:, i * C:(i + 1) * C], ALU.add).then_inc(sV.inc(), 1)
            blk.vector(v_sev)
            prod[("m_sev", b, i)] = (sV.h, sV.v)

        for e in range(NE):
            sl = e % 2
            def g_we(g, b=b, e=e, sl=sl):
                if e >= 2 or b > 0:
                    pk = ("m_emm", b, e - 2, NT - 1) if e >= 2 else ("m_smm", b, NT - 1)
                    g.wait_ge(*prod[pk])
                    if e < 2 and b > 0:
                        pass
                if e < 2:
                    g.wait_ge(*prod[("m_smm", b, NT - 1)])
                g.dma_start(wre[:, sl * NCH * C:(sl + 1) * NCH * C].rearrange("p (ch d) -> p ch d", d=C),
                            wr_d[e].rearrange("(ch p) d -> p ch d", p=128)).then_inc(sWs[sl].dinc(), 16)
                g.dma_start(brp[0:1, sl * C:(sl + 1) * C], br_d[e:e + 1, :]).then_inc(sWs[sl].dinc(), 16)
            blk.gpsimd(g_we)
            prod[("m_we", b, e)] = (sWs[sl].h, sWs[sl].v)

            for i in range(NT):
                def t_e(t, b=b, e=e, i=i, sl=sl):
                    t.wait_ge(*prod[("m_we", b, e)])
                    if i >= 2:
                        t.wait_ge(*prod[("m_eev", b, e, i - 2)])
                    elif e >= 1:
                        t.wait_ge(*prod[("m_eev", b, e - 1, NT - 2 + i)])
                    else:
                        t.wait_ge(*prod[("m_sev", b, NT - 2 + i)])
                    ins = None
                    for half in range(2):
                        dst = pe[i % 2][:, half * 512:(half + 1) * 512]
                        for ch in range(NCH):
                            t.matmul(dst, ufm[:, i * C + ch * 128: i * C + (ch + 1) * 128],
                                     wre[:, sl * NCH * C + ch * C + half * 512: sl * NCH * C + ch * C + (half + 1) * 512],
                                     start=(ch == 0), stop=False)
                        ins = t.matmul(dst, onb[:], brp[0:1, sl * C + half * 512: sl * C + (half + 1) * 512],
                                       start=False, stop=True)
                    ins.then_inc(sT.inc(), 1)
                blk.tensor(t_e)
                prod[("m_emm", b, e, i)] = (sT.h, sT.v)

                def v_eev(v, b=b, e=e, i=i):
                    v.wait_ge(*prod[("m_emm", b, e, i)])
                    v.scalar_tensor_tensor(zbf[:, i * C:(i + 1) * C], pe[i % 2][:],
                                           gal[:, i * NE + e:i * NE + e + 1],
                                           zbf[:, i * C:(i + 1) * C], ALU.mult, ALU.add).then_inc(sV.inc(), 1)
                blk.vector(v_eev)
                prod[("m_eev", b, e, i)] = (sV.h, sV.v)

        for i in range(NT):
            def v_f(v, b=b, i=i):
                v.wait_ge(*prod[("m_eev", b, NE - 1, i)])
                if i >= 1:
                    v.wait_ge(*prod[("m_fln", b, i - 1)])
                elif b > 0:
                    v.wait_ge(*prod[("m_fln", b - 1, NT - 1)])
                v.tensor_copy(zt[:], zbf[:, i * C:(i + 1) * C]).then_inc(sV.inc(), 1)
            blk.vector(v_f)
            prod[("m_z", b, i)] = (sV.h, sV.v)

            ews = [prod[("m_z", b, i)]]
            if i >= 2:
                ews.append(prod[("m_ydma", b, i // 2 - 1)])
            elif b > 0:
                ews.append(prod[("m_ydma", b - 1, NT // 2 - 1)])
            ln_emit(blk, sLN, zt[:], yst[:, (i % 2) * C:(i % 2 + 1) * C], scr[:], stx[:, 8:12], C,
                    prod, ("m_fln", b, i), extra_wait=ews)

            if i % 2 == 1:
                def sydma(sync, b=b, i=i):
                    sync.wait_ge(*prod[("m_fln", b, i - 1)])
                    sync.wait_ge(*prod[("m_fln", b, i)])
                    sync.dma_start(y_ap[b].rearrange("(n p) c -> p n c", p=128)[:, i - 1:i + 1],
                                   yst[:].rearrange("p (n c) -> p n c", c=C)).then_inc(sO.dinc(), 16)
                blk.sync(sydma)
                prod[("m_ydma", b, i // 2)] = (sO.h, sO.v)

    def sfin(sync):
        sync.wait_ge(sO.h, sO.v)
    blk.sync(sfin)
    return [v for v in locals().values() if isinstance(v, Sem)]


def l0moe_host_inputs(centroids, expert_biases, Wr, br, Ws, bs, pref="m_"):
    import ml_dtypes
    C = 1024
    return {
        pref + "WrT": np.ascontiguousarray(np.transpose(Wr, (0, 2, 1))),
        pref + "WsT": np.ascontiguousarray(np.transpose(Ws, (0, 2, 1))),
        pref + "br": np.ascontiguousarray(br),
        pref + "bs": np.ascontiguousarray(bs.reshape(1, 2 * C)),
        pref + "centT": np.ascontiguousarray(centroids.T),
        pref + "eb": expert_biases.reshape(1, -1).astype(np.float32),
        pref + "ident": np.eye(128, dtype=ml_dtypes.bfloat16),
        pref + "onesb": np.ones((1, 128), dtype=ml_dtypes.bfloat16),
        pref + "onesf": np.ones((1, 128), dtype=np.float32),
    }


def build_l1attn(nc, es, blk, prod, B, pref="w_", x_ap=None, y_ap=None, ses=None):
    ses = ses or es
    """L1 windowed bond/damage attention + residual + postLN.
    Host inputs (see l1attn_host_inputs): w_wdisp [1024,256], w_bdisp2 [128,2],
    w_wval [1024,1024], w_bval [1,1024], w_wcproj [1024,1024], w_bcproj [1,1024],
    w_bd [128, 640] (5 blockdiag 16x16 mats: state, strain, state_i, state_j, dmg),
    w_bdout [128, 16] (bond-out | dmg-out), w_relT [16,32], w_wpos [16,16],
    w_rep16 [16,128], w_reps [128, 8] (bstate,bstr,bdmg,bbond,bdout cols),
    w_rc [1? bf16 [128, 2048] replicated 1/cnt, w_ident [128,128] bf16, w_ones [1,128] bf16.
    """
    C = 1024
    NT = 16
    T = NT * 128
    NH = 16
    NCH = 8
    PAD = 31
    TP = T + PAD
    J = 32

    wd_d = nc.dram_tensor(pref + "wdisp", [C, 256], F32, kind="ExternalInput").ap()
    bd2_d = nc.dram_tensor(pref + "bdisp2", [128, 2], F32, kind="ExternalInput").ap()
    wv_d = nc.dram_tensor(pref + "wval", [C, C], F32, kind="ExternalInput").ap()
    bv_d = nc.dram_tensor(pref + "bval", [1, C], F32, kind="ExternalInput").ap()
    wc_d = nc.dram_tensor(pref + "wcproj", [C, C], F32, kind="ExternalInput").ap()
    bc_d = nc.dram_tensor(pref + "bcproj", [1, C], F32, kind="ExternalInput").ap()
    bd_d = nc.dram_tensor(pref + "bd", [128, 640], F32, kind="ExternalInput").ap()
    bo_d = nc.dram_tensor(pref + "bdout", [128, 16], F32, kind="ExternalInput").ap()
    rt_d = nc.dram_tensor(pref + "relT", [16, 32], F32, kind="ExternalInput").ap()
    wp_d = nc.dram_tensor(pref + "wpos", [16, 16], F32, kind="ExternalInput").ap()
    rp_d = nc.dram_tensor(pref + "rep16", [16, 128], F32, kind="ExternalInput").ap()
    rs_d = nc.dram_tensor(pref + "reps", [128, 8], F32, kind="ExternalInput").ap()
    rc_d = nc.dram_tensor(pref + "rc", [128, 32], F32, kind="ExternalInput").ap()
    id_d = nc.dram_tensor(pref + "ident", [128, 128], BF16, kind="ExternalInput").ap()
    on_d = nc.dram_tensor(pref + "ones", [1, 128], BF16, kind="ExternalInput").ap()
    skew_d = nc.dram_tensor(pref + "skew", [2, 128, 2048], BF16, kind="Internal").ap()
    ao_d = nc.dram_tensor(pref + "aod", [B, T, C], BF16, kind="Internal").ap()

    E = es.enter_context
    wdsp = E(nc.sbuf_tensor(pref + "wdsp", [128, NCH * 256], BF16))
    wval = E(nc.sbuf_tensor(pref + "wval_s", [128, NCH * C], BF16))
    wcpj = E(nc.sbuf_tensor(pref + "wcpj", [128, NCH * C], BF16))
    bd2 = E(nc.sbuf_tensor(pref + "bd2", [128, 2], F32))
    bvr = E(nc.sbuf_tensor(pref + "bvr", [1, C], BF16))
    bcr = E(nc.sbuf_tensor(pref + "bcr", [1, C], BF16))
    wbd = E(nc.sbuf_tensor(pref + "wbd", [128, 640], BF16))
    wbo = E(nc.sbuf_tensor(pref + "wbo", [128, 16], BF16))
    relT = E(nc.sbuf_tensor(pref + "relT_s", [16, 32], BF16))
    wpos = E(nc.sbuf_tensor(pref + "wpos_s", [16, 16], BF16))
    rep16 = E(nc.sbuf_tensor(pref + "rep16_s", [16, 128], BF16))
    reps = E(nc.sbuf_tensor(pref + "reps_s", [128, 8], F32))
    rc = E(nc.sbuf_tensor(pref + "rc_s", [128, 32], BF16))
    idb = E(nc.sbuf_tensor(pref + "idb", [128, 128], BF16))
    ones = E(nc.sbuf_tensor(pref + "ones_s", [1, 128], BF16))
    gsm = E(nc.sbuf_tensor(pref + "gsm", [16, 32], BF16))
    G = E(nc.sbuf_tensor(pref + "G", [128, 32], F32))
    negGx = E(nc.sbuf_tensor(pref + "negGx", [128, 4096], BF16))
    xti = E(nc.sbuf_tensor(pref + "xti", [128, 2 * C], F32))
    xbf = E(nc.sbuf_tensor(pref + "xbf", [128, 2 * C], BF16))
    xfm = E(nc.sbuf_tensor(pref + "xfm", [128, 2 * C], BF16))
    vsb = E(nc.sbuf_tensor(pref + "vsb", [128, NT * NH * 65], BF16))
    aow = E(nc.sbuf_tensor(pref + "aow", [128, 2 * 512], BF16))
    aot = E(nc.sbuf_tensor(pref + "aot", [128, 2 * C], BF16))
    dsp = E(nc.sbuf_tensor(pref + "dsp", [128, TP], BF16))
    Ast = E(nc.sbuf_tensor(pref + "Ast", [128, TP], BF16))
    A1 = E(nc.sbuf_tensor(pref + "A1", [128, TP], BF16))
    Adm = E(nc.sbuf_tensor(pref + "Adm", [128, TP], BF16))
    stt_ = None
    Ebuf = E(nc.sbuf_tensor(pref + "Ebuf", [128, TP], BF16))
    Fbuf = E(nc.sbuf_tensor(pref + "Fbuf", [128, T], BF16))
    Sa = E(nc.sbuf_tensor(pref + "Sa", [128, TP], BF16))
    Sb = Ebuf
    grid = E(nc.sbuf_tensor(pref + "grid", [128, 2 * 8192], BF16))
    BLs = E(nc.sbuf_tensor(pref + "BLs", [128, 8 * 256], BF16))
    DLs = E(nc.sbuf_tensor(pref + "DLs", [128, 8 * 256], BF16))
    ptk = E(nc.sbuf_tensor(pref + "ptk", [128, 2 * 256], BF16))
    p2d = E(nc.sbuf_tensor(pref + "p2d", [128, 2 * 2048], BF16))
    Pt = E(nc.sbuf_tensor(pref + "Pt", [128, 2 * 2048], BF16))
    zt = E(nc.sbuf_tensor(pref + "zt", [128, C], F32))
    scr = E(nc.sbuf_tensor(pref + "scr", [128, C], F32))
    stx = E(nc.sbuf_tensor(pref + "stx", [128, 8], F32))
    yst = E(nc.sbuf_tensor(pref + "yst", [128, C], F32))

    pool = E(nc.psum_tensor(pref + "pool", [128, 4096], F32))
    poolb = pool[:].bitcast(BF16)
    pT = poolb[:, 0:1024]
    pP = [poolb[:, 0:1024], poolb[:, 1024:2048]]
    pj = pool[:, 512:1536]
    pRedB = pool[:, 1024:1280]
    pRedD = pool[:, 1536:1792]
    pBD = pool[:, 3072:4096]
    pO = [pool[0:128, 2048 + i * 512: 2048 + i * 512 + 65] for i in range(2)]
    pGm = pool[0:16, 2048:2080]
    pGr = pool[:, 2560:2592]

    sW = Sem(ses, nc, pref + "sW")
    sC = Sem(ses, nc, pref + "sC")
    sX0 = Sem(ses, nc, pref + "sX0")
    sX1 = Sem(ses, nc, pref + "sX1")
    sXp = [sX0, sX1]
    sT = Sem(ses, nc, pref + "sT")
    sV = Sem(ses, nc, pref + "sV")
    sS = Sem(ses, nc, pref + "sS")
    sG = Sem(ses, nc, pref + "sG")
    sD0 = Sem(ses, nc, pref + "sD0")
    sD1 = Sem(ses, nc, pref + "sD1")
    sDp = [sD0, sD1]
    sAO0 = Sem(ses, nc, pref + "sAO0")
    sAO1 = Sem(ses, nc, pref + "sAO1")
    sAOp = [sAO0, sAO1]
    sAOr = Sem(ses, nc, pref + "sAOr")
    sO = Sem(ses, nc, pref + "sO")
    sLN = Sem(ses, nc, pref + "sLN")

    def g0(g):
        g.dma_start(wdsp[:].rearrange("p (ch d) -> p ch d", d=256),
                    wd_d.rearrange("(ch p) d -> p ch d", p=128)).then_inc(sW.dinc(), 16)
        g.dma_start(wval[:].rearrange("p (ch d) -> p ch d", d=C),
                    wv_d.rearrange("(ch p) d -> p ch d", p=128)).then_inc(sW.dinc(), 16)
        g.dma_start(wcpj[:].rearrange("p (ch d) -> p ch d", d=C),
                    wc_d.rearrange("(ch p) d -> p ch d", p=128)).then_inc(sW.dinc(), 16)
        g.dma_start(bvr[:], bv_d).then_inc(sW.dinc(), 16)
        g.dma_start(bcr[:], bc_d).then_inc(sW.dinc(), 16)
        g.dma_start(wbd[:], bd_d).then_inc(sW.dinc(), 16)
        g.dma_start(wbo[:], bo_d).then_inc(sW.dinc(), 16)
        g.dma_start(relT[:], rt_d).then_inc(sW.dinc(), 16)
        g.dma_start(wpos[:], wp_d).then_inc(sW.dinc(), 16)
        g.dma_start(rep16[:], rp_d).then_inc(sW.dinc(), 16)
        g.dma_start(rc[:], rc_d).then_inc(sW.dinc(), 16)
    blk.gpsimd(g0)
    W_ALL = sW.v

    def s0(sync):
        sync.dma_start(bd2[:], bd2_d).then_inc(sC.dinc(), 16)
        sync.dma_start(reps[:], rs_d).then_inc(sC.dinc(), 16)
        sync.dma_start(idb[:], id_d).then_inc(sC.dinc(), 16)
        sync.dma_start(ones[:], on_d).then_inc(sC.dinc(), 16)
    blk.sync(s0)
    CIN = sC.v

    def v_init(v):
        v.wait_ge(sC.h, CIN)
        v.memset(grid[:, 0:2048], 0.0)
        v.memset(vsb[:].rearrange("p (n h e) -> p n h e", h=NH, e=65)[:, :, :, 64:65], 1.0)
        v.memset(dsp[:, 0:PAD], 0.0)
        v.memset(Ast[:, 0:PAD], 0.0)
        v.memset(A1[:, 0:PAD], 0.0)
        v.memset(Adm[:, 0:PAD], 0.0)
        v.memset(Ebuf[:, 0:PAD], 0.0)
        v.memset(Sa[:, 0:PAD], 0.0).then_inc(sV.inc(), 1)
    blk.vector(v_init)
    prod["w_init"] = (sV.h, sV.v)

    def s_zskew(sync):
        sync.wait_ge(*prod["w_init"])
        sync.dma_start(skew_d[0], grid[:, 0:2048]).then_inc(sD0.dinc(), 16)
        sync.dma_start(skew_d[1], grid[:, 0:2048]).then_inc(sD1.dinc(), 16)
    blk.sync(s_zskew)
    prod["w_zskew"] = None

    def t_G(t):
        t.wait_ge(sW.h, W_ALL)
        t.matmul(pGm[:], wpos[:], relT[:], start=True, stop=True).then_inc(sT.inc(), 1)
    blk.tensor(t_G)
    prod["w_G1"] = (sT.h, sT.v)

    def v_G1(v):
        v.wait_ge(*prod[("w_G1")])
        v.tensor_copy(gsm[:], pGm[:]).then_inc(sV.inc(), 1)
    blk.vector(v_G1)
    prod["w_G2"] = (sV.h, sV.v)

    def t_G2(t):
        t.wait_ge(*prod["w_G2"])
        t.matmul(pGr[:], rep16[:], gsm[:], start=True, stop=True).then_inc(sT.inc(), 1)
    blk.tensor(t_G2)
    prod["w_G3"] = (sT.h, sT.v)

    def v_G2(v):
        v.wait_ge(*prod["w_G3"])
        v.tensor_scalar(G[:], pGr[:], reps[:, 1:2], None, ALU.add).then_inc(sV.inc(), 1)
    blk.vector(v_G2)
    prod["w_G"] = (sV.h, sV.v)

    def v_Gx(v):
        v.wait_ge(*prod["w_G"])
        Gb = G[:].unsqueeze(2).broadcast_to([128, 32, 128])
        v.tensor_scalar(negGx[:].rearrange("p (j t) -> p j t", t=128),
                        Gb, -1.0, None, ALU.mult).then_inc(sV.inc(), 1)
    blk.vector(v_Gx)
    prod["w_Gx"] = (sV.h, sV.v)

    def v_reps2(v):
        v.wait_ge(sC.h, CIN)
        v.tensor_scalar(reps[:, 5:6], reps[:, 4:5], 0.5, None, ALU.mult)
        v.tensor_scalar(reps[:, 6:7], reps[:, 3:4], -5.0, None, ALU.add).then_inc(sV.inc(), 1)
    blk.vector(v_reps2)
    prod["w_reps2"] = (sV.h, sV.v)

    for b in range(B):
        for i in range(NT):
            k = b * NT + i
            pp = k % 2

            def s_x(sync, b=b, i=i, pp=pp, k=k):
                if i < 2 and b > 0:
                    sync.wait_ge(*prod[("w_z", b - 1, NT - 1)])
                if k >= 2:
                    sync.wait_ge(*prod[("w_xT", k - 2)])
                sync.dma_start(xti[:, pp * C:(pp + 1) * C],
                               x_ap[b].rearrange("(n p) c -> p n c", p=128)[:, i]).then_inc(sXp[pp].dinc(), 16)
            blk.sync(s_x)
            prod[("w_x", k)] = (sXp[pp].h, sXp[pp].v)

            def v_c(v, b=b, i=i, pp=pp, k=k):
                v.wait_ge(*prod[("w_x", k)])
                if k >= 2:
                    v.wait_ge(*prod[("w_xT", k - 2)])
                v.tensor_copy(xbf[:, pp * C:(pp + 1) * C], xti[:, pp * C:(pp + 1) * C]).then_inc(sV.inc(), 1)
            blk.vector(v_c)
            prod[("w_cast", k)] = (sV.h, sV.v)

            def t_xT(t, pp=pp, k=k, b=b):
                t.wait_ge(sC.h, CIN)
                t.wait_ge(*prod[("w_cast", k)])
                if k % NT == 0 and b > 0:
                    t.wait_ge(*prod[("w_aev", b - 1, NT - 1)])
                    t.wait_ge(*prod[("w_Ptev", 2 * b - 1, NT - 1)])
                if k >= 1:
                    t.wait_ge(*prod[("w_evx", k - 1)])
                ins = None
                for ch in range(NCH):
                    ins = t.transpose(pT[:, ch * 128:(ch + 1) * 128],
                                      xbf[:, pp * C + ch * 128: pp * C + (ch + 1) * 128], idb[:])
                ins.then_inc(sT.inc(), 1)
            blk.tensor(t_xT)
            prod[("w_xT", k)] = (sT.h, sT.v)

            def v_evx(v, pp=pp, k=k):
                v.wait_ge(*prod[("w_xT", k)])
                if k >= 2:
                    v.wait_ge(*prod[("w_vmm", k - 2)])
                v.tensor_copy(xfm[:, pp * C:(pp + 1) * C], pT[:]).then_inc(sV.inc(), 1)
            blk.vector(v_evx)
            prod[("w_evx", k)] = (sV.h, sV.v)

            def t_v(t, pp=pp, k=k, b=b, i=i):
                t.wait_ge(sW.h, W_ALL)
                t.wait_ge(*prod[("w_evx", k)])
                if i == 0 and b > 0:
                    t.wait_ge(*prod[("w_z", b - 1, NT - 1)])
                    t.wait_ge(*prod[("w_Ptev", 2 * b - 1, NT - 1)])
                if k >= 1:
                    t.wait_ge(*prod[("w_vev", k - 1)])
                ins = None
                for half in range(2):
                    dst = pj[:, half * 512:(half + 1) * 512]
                    for ch in range(NCH):
                        t.matmul(dst, xfm[:, pp * C + ch * 128: pp * C + (ch + 1) * 128],
                                 wval[:, ch * C + half * 512: ch * C + (half + 1) * 512],
                                 start=(ch == 0), stop=False)
                    ins = t.matmul(dst, ones[:], bvr[0:1, half * 512:(half + 1) * 512],
                                   start=False, stop=True)
                ins.then_inc(sT.inc(), 1)
            blk.tensor(t_v)
            prod[("w_vmm", k)] = (sT.h, sT.v)

            def v_vev(v, k=k, b=b, i=i):
                v.wait_ge(*prod[("w_vmm", k)])
                v.wait_ge(*prod["w_init"])
                if b > 0 and i == 0:
                    v.wait_ge(*prod[("w_bandmm", (b - 1) * 2 + 1, NT - 1, 7)])
                dst = vsb[:, i * NH * 65:(i + 1) * NH * 65].rearrange("p (h e) -> p h e", e=65)[:, :, 0:64]
                v.tensor_copy(dst, pj[:].rearrange("p (h e) -> p h e", e=64)).then_inc(sV.inc(), 1)
            blk.vector(v_vev)
            prod[("w_vev", k)] = (sV.h, sV.v)

        for grp in range(2):
            gk = b * 2 + grp
            for i in range(NT):
                k = b * NT + i

                def s_x2(sync, b=b, i=i, grp=grp, gk=gk):
                    pp2 = i % 2
                    if i < 2 and gk % 2 == 0:
                        sync.wait_ge(*prod[("w_cast", b * NT + NT - 1)])
                    if i >= 2:
                        sync.wait_ge(*prod[("w_xT2", gk, i - 2)])
                    elif gk > 0:
                        sync.wait_ge(*prod[("w_xT2", gk - 1, NT - 2 + i)])
                    sync.dma_start(xti[:, pp2 * C:(pp2 + 1) * C],
                                   x_ap[b].rearrange("(n p) c -> p n c", p=128)[:, i]).then_inc(sXp[pp2].dinc(), 16)
                blk.sync(s_x2)
                prod[("w_x2", gk, i)] = (sXp[i % 2].h, sXp[i % 2].v)

                def v_c2(v, i=i, gk=gk):
                    pp2 = i % 2
                    v.wait_ge(*prod[("w_x2", gk, i)])
                    if i >= 2:
                        v.wait_ge(*prod[("w_xT2", gk, i - 2)])
                    elif gk > 0:
                        v.wait_ge(*prod[("w_xT2", gk - 1, NT - 2 + i)])
                    v.tensor_copy(xbf[:, pp2 * C:(pp2 + 1) * C], xti[:, pp2 * C:(pp2 + 1) * C]).then_inc(sV.inc(), 1)
                blk.vector(v_c2)
                prod[("w_cast2", gk, i)] = (sV.h, sV.v)

                def t_xT2(t, i=i, gk=gk):
                    pp2 = i % 2
                    t.wait_ge(*prod[("w_cast2", gk, i)])
                    if i >= 1:
                        t.wait_ge(*prod[("w_evx2", gk, i - 1)])
                    elif gk > 0:
                        t.wait_ge(*prod[("w_evx2", gk - 1, NT - 1)])
                        t.wait_ge(*prod[("w_Ptev", gk - 1, NT - 1)])
                    ins = None
                    for ch in range(NCH):
                        ins = t.transpose(pT[:, ch * 128:(ch + 1) * 128],
                                          xbf[:, pp2 * C + ch * 128: pp2 * C + (ch + 1) * 128], idb[:])
                    ins.then_inc(sT.inc(), 1)
                blk.tensor(t_xT2)
                prod[("w_xT2", gk, i)] = (sT.h, sT.v)

                def v_evx2(v, i=i, gk=gk):
                    pp2 = i % 2
                    v.wait_ge(*prod[("w_xT2", gk, i)])
                    if i >= 2:
                        v.wait_ge(*prod[("w_dmm", gk, i - 2)])
                    elif gk > 0:
                        v.wait_ge(*prod[("w_dmm", gk - 1, NT - 2 + i)])
                    v.tensor_copy(xfm[:, pp2 * C:(pp2 + 1) * C], pT[:]).then_inc(sV.inc(), 1)
                blk.vector(v_evx2)
                prod[("w_evx2", gk, i)] = (sV.h, sV.v)

                def t_d(t, i=i, grp=grp, gk=gk):
                    pp2 = i % 2
                    t.wait_ge(sW.h, W_ALL)
                    t.wait_ge(*prod[("w_evx2", gk, i)])
                    if i >= 2:
                        t.wait_ge(*prod[("w_dev", gk, i - 2)])
                    elif gk > 0:
                        t.wait_ge(*prod[("w_dev", gk - 1, NT - 2 + i)])
                        t.wait_ge(*prod[("w_F", gk - 1)])
                    ins = None
                    for ch in range(NCH):
                        ins = t.matmul(pBD[:, (i % 2) * 512:(i % 2) * 512 + 128],
                                       wdsp[:, ch * 256 + grp * 128: ch * 256 + (grp + 1) * 128],
                                       xfm[:, pp2 * C + ch * 128: pp2 * C + (ch + 1) * 128],
                                       start=(ch == 0), stop=(ch == NCH - 1))
                    ins.then_inc(sT.inc(), 1)
                blk.tensor(t_d)
                prod[("w_dmm", gk, i)] = (sT.h, sT.v)

                def v_dev(v, i=i, grp=grp, gk=gk):
                    v.wait_ge(*prod[("w_dmm", gk, i)])
                    v.wait_ge(*prod["w_init"])
                    if gk > 0 and i == 0:
                        v.wait_ge(*prod[("w_bdm", gk - 1, 5)])
                    v.tensor_scalar(dsp[:, PAD + i * 128:PAD + (i + 1) * 128],
                                    pBD[:, (i % 2) * 512:(i % 2) * 512 + 128],
                                    bd2[:, grp:grp + 1], None, ALU.add).then_inc(sV.inc(), 1)
                blk.vector(v_dev)
                prod[("w_dev", gk, i)] = (sV.h, sV.v)
            prod[("w_disp", gk)] = prod[("w_dev", gk, NT - 1)]

            for bi, (wslice, dstbuf) in enumerate([(0, Ast), (1, A1), (4, Adm)]):
                for rnd in range(2):
                    def t_bd(t, bi=bi, wslice=wslice, gk=gk, rnd=rnd):
                        t.wait_ge(*prod[("w_disp", gk)])
                        if bi > 0 or rnd > 0:
                            t.wait_ge(*prod[("w_bdev", gk, bi * 2 + rnd - 1)])
                        ins = None
                        for q in range(2):
                            ins = t.matmul(pBD[:, q * 512:(q + 1) * 512],
                                           wbd[:, wslice * 128:(wslice + 1) * 128],
                                           dsp[:, PAD + (rnd * 2 + q) * 512:PAD + (rnd * 2 + q + 1) * 512],
                                           start=True, stop=True)
                        ins.then_inc(sT.inc(), 1)
                    blk.tensor(t_bd)
                    prod[("w_bdm", gk, bi * 2 + rnd)] = (sT.h, sT.v)

                    def v_bdev(v, bi=bi, dstbuf=dstbuf, gk=gk, rnd=rnd):
                        v.wait_ge(*prod[("w_bdm", gk, bi * 2 + rnd)])
                        if bi == 2 and rnd == 0 and gk > 0:
                            v.wait_ge(*prod[("w_asm", gk - 1, NT - 1, 1)])
                        v.tensor_copy(dstbuf[:, PAD + rnd * 1024:PAD + (rnd + 1) * 1024],
                                      pBD[:, 0:1024]).then_inc(sV.inc(), 1)
                    blk.vector(v_bdev)
                    prod[("w_bdev", gk, bi * 2 + rnd)] = (sV.h, sV.v)

            def v_slide(v, gk=gk):
                v.wait_ge(*prod[("w_bdev", gk, 1)])
                v.wait_ge(sW.h, W_ALL)
                v.tensor_tensor(Sa[:, PAD:PAD + T], Ast[:, PAD:PAD + T], Ast[:, PAD - 1:PAD - 1 + T], ALU.add).then_inc(sV.inc(), 1)
                v.wait_ge(sV.h, sV.v)
                v.tensor_tensor(Sb[:, PAD:PAD + T], Sa[:, PAD:PAD + T], Sa[:, PAD - 2:PAD - 2 + T], ALU.add).then_inc(sV.inc(), 1)
                v.wait_ge(sV.h, sV.v)
                v.tensor_tensor(Sa[:, PAD:PAD + T], Sb[:, PAD:PAD + T], Sb[:, PAD - 4:PAD - 4 + T], ALU.add).then_inc(sV.inc(), 1)
                v.wait_ge(sV.h, sV.v)
                v.tensor_tensor(Sb[:, PAD:PAD + T], Sa[:, PAD:PAD + T], Sa[:, PAD - 8:PAD - 8 + T], ALU.add).then_inc(sV.inc(), 1)
                v.wait_ge(sV.h, sV.v)
                v.tensor_tensor(Sa[:, PAD:PAD + T], Sb[:, PAD:PAD + T], Sb[:, PAD - 16:PAD - 16 + T], ALU.add).then_inc(sV.inc(), 1)
                v.wait_ge(sV.h, sV.v)
                v.tensor_tensor(Sb[:, PAD:PAD + 32], Sa[:, PAD:PAD + 32], rc[:], ALU.mult).then_inc(sV.inc(), 1)
                v.wait_ge(sV.h, sV.v)
                v.tensor_scalar(Sb[:, PAD + 32:PAD + T], Sa[:, PAD + 32:PAD + T], 1.0 / 32.0, None, ALU.mult).then_inc(sV.inc(), 1)
                v.wait_ge(sV.h, sV.v)
                v.scalar_tensor_tensor(Sa[:, PAD:PAD + T], Ast[:, PAD:PAD + T], -1.0, Sb[:, PAD:PAD + T], ALU.mult, ALU.add).then_inc(sV.inc(), 1)
                v.wait_ge(sV.h, sV.v)
                v.tensor_scalar(Sa[:, PAD:PAD + T], Sa[:, PAD:PAD + T], reps[:, 0:1], None, ALU.add).then_inc(sV.inc(), 1)
            blk.vector(v_slide)
            prod[("w_state", gk)] = (sV.h, sV.v)

            for rnd in range(2):
                def t_a3(t, gk=gk, rnd=rnd):
                    t.wait_ge(*prod[("w_state", gk)])
                    if rnd > 0:
                        t.wait_ge(*prod[("w_E", gk, 0)])
                    ins = None
                    for q in range(2):
                        ins = t.matmul(pBD[:, q * 512:(q + 1) * 512],
                                       wbd[:, 3 * 128:4 * 128],
                                       Sa[:, PAD + (rnd * 2 + q) * 512:PAD + (rnd * 2 + q + 1) * 512],
                                       start=True, stop=True)
                    ins.then_inc(sT.inc(), 1)
                blk.tensor(t_a3)
                prod[("w_a3m", gk, rnd)] = (sT.h, sT.v)

                def v_E(v, gk=gk, rnd=rnd):
                    v.wait_ge(*prod[("w_a3m", gk, rnd)])
                    if rnd == 0 and gk > 0:
                        v.wait_ge(*prod[("w_asm", gk - 1, NT - 1, 0)])
                    v.tensor_tensor(Ebuf[:, PAD + rnd * 1024:PAD + (rnd + 1) * 1024],
                                    A1[:, PAD + rnd * 1024:PAD + (rnd + 1) * 1024],
                                    pBD[:, 0:1024], ALU.add).then_inc(sV.inc(), 1)
                blk.vector(v_E)
                prod[("w_E", gk, rnd)] = (sV.h, sV.v)
            for rnd in range(2):
                def t_asi(t, gk=gk, rnd=rnd):
                    t.wait_ge(*prod[("w_E", gk, 1)])
                    if rnd > 0:
                        t.wait_ge(*prod[("w_F", gk, 0)])
                    ins = None
                    for q in range(2):
                        ins = t.matmul(pBD[:, q * 512:(q + 1) * 512],
                                       wbd[:, 2 * 128:3 * 128],
                                       Sa[:, PAD + (rnd * 2 + q) * 512:PAD + (rnd * 2 + q + 1) * 512],
                                       start=True, stop=True)
                    ins.then_inc(sT.inc(), 1)
                blk.tensor(t_asi)
                prod[("w_asim", gk, rnd)] = (sT.h, sT.v)

                def v_F(v, gk=gk, rnd=rnd):
                    v.wait_ge(*prod[("w_asim", gk, rnd)])
                    v.tensor_tensor(Fbuf[:, rnd * 1024:(rnd + 1) * 1024], pBD[:, 0:1024],
                                    A1[:, PAD + rnd * 1024:PAD + (rnd + 1) * 1024], ALU.subtract).then_inc(sV.inc(), 1)
                blk.vector(v_F)
                prod[("w_F", gk, rnd)] = (sV.h, sV.v)
            prod[("w_F", gk)] = prod[("w_F", gk, 1)]

            def emit_B(i, gk=gk, b=b, grp=grp):
                pp = i % 2

                def s_rb(sync, i=i, gk=gk, pp=pp):
                    sync.wait_ge(*prod[("w_skw", gk, i)])
                    if i >= 2:
                        sync.wait_ge(*prod[("w_PtT", gk, i - 2)])
                    elif gk > 0:
                        sync.wait_ge(*prod[("w_PtT", gk - 1, NT - 2 + i)])
                    sync.dma_start(p2d[:, pp * 2048:(pp + 1) * 2048], skew_d[pp]).then_inc(sDp[pp].dinc(), 16)
                blk.sync(s_rb)
                prod[("w_rb", gk, i)] = (sDp[pp].h, sDp[pp].v)

                def t_PtT(t, i=i, gk=gk, pp=pp, b=b):
                    t.wait_ge(*prod[("w_rb", gk, i)])
                    if i == 0:
                        t.wait_ge(*prod[("w_evx2", gk, NT - 1)])
                        t.wait_ge(*prod[("w_vev", b * NT + NT - 1)])
                    if i >= 1:
                        t.wait_ge(*prod[("w_Ptev", gk, i - 1)])
                    elif gk > 0:
                        t.wait_ge(*prod[("w_Ptev", gk - 1, NT - 1)])
                    ins = None
                    for h in range(8):
                        for ab in range(2):
                            srcv = p2d[:, pp * 2048:(pp + 1) * 2048].rearrange(
                                "p (tl h) -> p tl h", h=8)[:, ab * 128:(ab + 1) * 128, h]
                            idx = h * 2 + ab
                            ins = t.transpose(pP[idx // 8][:, (idx % 8) * 128:(idx % 8 + 1) * 128],
                                              srcv, idb[:])
                    ins.then_inc(sT.inc(), 1)
                blk.tensor(t_PtT)
                prod[("w_PtT", gk, i)] = (sT.h, sT.v)

                def v_Ptev(v, i=i, gk=gk, pp=pp):
                    v.wait_ge(*prod[("w_PtT", gk, i)])
                    if i >= 2:
                        v.wait_ge(*prod[("w_bandmm", gk, i - 2, 7)])
                    elif gk > 0:
                        v.wait_ge(*prod[("w_bandmm", gk - 1, NT - 2 + i, 7)])
                    v.tensor_copy(Pt[:, pp * 2048:pp * 2048 + 1024], pP[0][:]).then_inc(sV.inc(), 1)
                    v.tensor_copy(Pt[:, pp * 2048 + 1024:(pp + 1) * 2048], pP[1][:]).then_inc(sV.inc(), 1)
                blk.vector(v_Ptev)
                prod[("w_Ptev", gk, i)] = (sV.h, sV.v)

                for h in range(8):
                    hG = grp * 8 + h

                    def t_band(t, i=i, gk=gk, pp=pp, h=h, hG=hG, b=b):
                        if (gk, i, h) == (0, 0, 0):
                            t.wait_ge(*prod["w_G"])
                        t.wait_ge(*prod[("w_Ptev", gk, i)])
                        t.wait_ge(*prod[("w_vev", b * NT + i)])
                        if i > 0:
                            t.wait_ge(*prod[("w_vev", b * NT + i - 1)])
                        key = ("w_norm", gk, i, h - 2)
                        if h >= 2:
                            t.wait_ge(*prod[key])
                        elif i >= 1:
                            t.wait_ge(*prod[("w_norm", gk, i - 1, 6 + h)])
                        elif gk > 0:
                            t.wait_ge(*prod[("w_norm", gk - 1, NT - 1, 6 + h)])
                        PtA = Pt[:, pp * 2048 + h * 256: pp * 2048 + h * 256 + 128]
                        PtB = Pt[:, pp * 2048 + h * 256 + 128: pp * 2048 + h * 256 + 256]
                        ins = None
                        if i > 0:
                            vA = vsb[:, (i - 1) * NH * 65 + hG * 65:(i - 1) * NH * 65 + (hG + 1) * 65]
                            ins = t.matmul(pO[h % 2], PtA, vA, start=True, stop=False)
                        vB = vsb[:, i * NH * 65 + hG * 65:i * NH * 65 + (hG + 1) * 65]
                        ins = t.matmul(pO[h % 2], PtB, vB, start=(i == 0), stop=True)
                        ins.then_inc(sT.inc(), 1)
                    blk.tensor(t_band)
                    prod[("w_bandmm", gk, i, h)] = (sT.h, sT.v)

                    def v_norm(v, i=i, gk=gk, h=h, hG=hG, b=b):
                        v.wait_ge(*prod[("w_bandmm", gk, i, h)])
                        if h == 0:
                            if i >= 2:
                                v.wait_ge(*prod[("w_aodma", gk, i - 2)])
                            elif gk > 0:
                                v.wait_ge(*prod[("w_aodma", gk - 1, NT - 2 + i)])
                        v.reciprocal(stx[:, h % 2:h % 2 + 1], pO[h % 2][:, 64:65]).then_inc(sV.inc(), 1)
                        v.wait_ge(sV.h, sV.v)
                        v.tensor_scalar(aow[:, (i % 2) * 512 + h * 64:(i % 2) * 512 + (h + 1) * 64],
                                        pO[h % 2][:, 0:64], stx[:, h % 2:h % 2 + 1], None, ALU.mult).then_inc(sV.inc(), 1)
                    blk.vector(v_norm)
                    prod[("w_norm", gk, i, h)] = (sV.h, sV.v)

            def emit_B_aow(i, gk=gk, b=b, grp=grp):
                def s_aow(sync, i=i, gk=gk, grp=grp, b=b):
                    sync.wait_ge(*prod[("w_norm", gk, i, 7)])
                    sync.dma_start(ao_d[b].rearrange("(n p) c -> p n c", p=128)[:, i, grp * 512:(grp + 1) * 512],
                                   aow[:, (i % 2) * 512:(i % 2 + 1) * 512]).then_inc(sAOp[i % 2].dinc(), 16)
                blk.sync(s_aow)
                prod[("w_aodma", gk, i)] = (sAOp[i % 2].h, sAOp[i % 2].v)

            def emit_lev(j, gk=gk):
                sl2 = j % 8

                def v_levB(v, i=j, gk=gk, sl=sl2):
                    v.wait_ge(*prod[("w_redm", gk, i, 0)])
                    if i >= 8:
                        v.wait_ge(*prod[("w_exp", gk, i - 8)])
                    elif gk > 0:
                        v.wait_ge(*prod[("w_exp", gk - 1, i + 8)])
                    v.tensor_copy(BLs[:, sl * 256:(sl + 1) * 256], pRedB[:]).then_inc(sV.inc(), 1)
                blk.vector(v_levB)
                prod[("w_levB", gk, j)] = (sV.h, sV.v)

                def v_lg(v, i=j, gk=gk, sl=sl2):
                    v.wait_ge(*prod[("w_tanh", gk, i)])
                    v.wait_ge(*prod[("w_levB", gk, i)])
                    v.scalar_tensor_tensor(BLs[:, sl * 256:(sl + 1) * 256],
                                           DLs[:, sl * 256:(sl + 1) * 256], -5.0,
                                           BLs[:, sl * 256:(sl + 1) * 256],
                                           ALU.mult, ALU.add).then_inc(sV.inc(), 1)
                blk.vector(v_lg)
                prod[("w_lg", gk, j)] = (sV.h, sV.v)

            def emit_exp(j, gk=gk):
                sl2 = j % 8
                pp2 = j % 2

                def s_exp(s, i=j, gk=gk, pp=pp2, sl=sl2):
                    s.wait_ge(*prod[("w_lg", gk, i)])
                    if i >= 2:
                        s.wait_ge(*prod[("w_skw", gk, i - 2)])
                    elif gk > 0:
                        s.wait_ge(*prod[("w_skw", gk - 1, NT - 2 + i)])
                    s.activation(ptk[:, pp * 256:(pp + 1) * 256], BLs[:, sl * 256:(sl + 1) * 256], AF.Exp,
                                 bias=reps[:, 6:7]).then_inc(sS.inc(), 1)
                blk.scalar(s_exp)
                prod[("w_exp", gk, j)] = (sS.h, sS.v)

                if j == 0:
                    def v_mask(v, gk=gk):
                        v.wait_ge(*prod[("w_exp", gk, 0)])
                        ins = None
                        for jj in range(J - 1):
                            ins = v.memset(ptk[0:PAD - jj, jj * 8:(jj + 1) * 8], 0.0)
                        ins.then_inc(sV.inc(), 1)
                    blk.vector(v_mask)
                    prod[("w_msk", gk, 0)] = (sV.h, sV.v)
                else:
                    prod[("w_msk", gk, j)] = prod[("w_exp", gk, j)]

                def s_skw(sync, i=j, gk=gk, pp=pp2):
                    sync.wait_ge(*prod[("w_msk", gk, i)])
                    if gk == 0 and i < 2:
                        sync.wait_ge(sDp[i % 2].h, 16)
                    if i >= 2:
                        sync.wait_ge(*prod[("w_rb", gk, i - 2)])
                    elif gk > 0:
                        sync.wait_ge(*prod[("w_rb", gk - 1, NT - 2 + i)])
                    flat = skew_d[pp].rearrange("p n -> (p n)")
                    dst = bass.AP(flat.tensor, flat.offset + 776,
                                  [[2056, 128], [1, 256]])
                    sync.dma_start(dst, ptk[:, pp * 256:(pp + 1) * 256]).then_inc(sDp[pp].dinc(), 16)
                blk.sync(s_skw)
                prod[("w_skw", gk, j)] = (sDp[pp2].h, sDp[pp2].v)

            for i in range(NT):
                pp = i % 2
                sl = i % 8
                gb = pp * 8192

                def v_asm(v, i=i, gk=gk, gb=gb):
                    v.wait_ge(*prod[("w_F", gk)])
                    if gk == 0 and i == 0:
                        v.wait_ge(sD0.h, 16)
                        v.wait_ge(sD1.h, 16)
                    if i >= 2:
                        v.wait_ge(*prod[("w_redm", gk, i - 2, 1)])
                    elif gk > 0:
                        v.wait_ge(*prod[("w_redm", gk - 1, NT - 2 + i, 1)])
                    g3 = grid[:, gb:gb + 4096].rearrange("p (j t) -> p j t", t=128)
                    Ei = bass.AP(Ebuf[:].tensor, Ebuf[:].offset + i * 128, [[TP, 128], [1, J], [1, 128]])
                    Fb = Fbuf[:, i * 128:(i + 1) * 128].unsqueeze(1).broadcast_to([128, J, 128])
                    v.tensor_tensor(g3, Ei, Fb, ALU.add).then_inc(sV.inc(), 1)
                    v.wait_ge(sV.h, sV.v)
                    if gk == 0 and i == 0:
                        v.wait_ge(*prod["w_Gx"])
                    gB = grid[:, gb:gb + 4096]
                    v.tensor_tensor(gB, gB, negGx[:], ALU.subtract).then_inc(sV.inc(), 1)
                    v.wait_ge(sV.h, sV.v)
                    g3d = grid[:, gb + 4096:gb + 8192].rearrange("p (j t) -> p j t", t=128)
                    Ai2 = bass.AP(Adm[:].tensor, Adm[:].offset + i * 128, [[TP, 128], [1, J], [1, 128]])
                    Ab2 = Adm[:, PAD + i * 128:PAD + (i + 1) * 128].unsqueeze(1).broadcast_to([128, J, 128])
                    v.tensor_tensor(g3d, Ai2, Ab2, ALU.subtract).then_inc(sV.inc(), 1)
                blk.vector(v_asm)
                prod[("w_asm", gk, i, 0)] = (sV.h, sV.v - 1)
                prod[("w_asm", gk, i, 1)] = (sV.h, sV.v)

                if i >= 1:
                    emit_lev(i - 1)
                if i >= 2:
                    emit_B(i - 2)
                if i >= 3:
                    emit_B_aow(i - 3)

                for gi in range(2):
                    def s_gelu(s, i=i, gk=gk, gi=gi):
                        s.wait_ge(*prod[("w_asm", gk, i, gi)])
                        gb = (i % 2) * 8192
                        s.activation(grid[:, gb + gi * 4096:gb + (gi + 1) * 4096],
                                     grid[:, gb + gi * 4096:gb + (gi + 1) * 4096], AF.Gelu,
                                     bias=(0.0 if gi == 0 else reps[:, 2:3])).then_inc(sS.inc(), 1)
                    blk.scalar(s_gelu)
                    prod[("w_gelu", gk, i, gi)] = (sS.h, sS.v)

                    def t_red(t, i=i, gk=gk, gi=gi, b=b):
                        t.wait_ge(*prod[("w_gelu", gk, i, gi)])
                        if gi == 0 and i == 0 and gk % 2 == 0:
                            t.wait_ge(*prod[("w_vev", b * NT + NT - 1)])
                        if gi == 0:
                            if i >= 1:
                                t.wait_ge(*prod[("w_levB", gk, i - 1)])
                                t.wait_ge(*prod[("w_tanh", gk, i - 1)])
                            elif gk > 0:
                                t.wait_ge(*prod[("w_levB", gk - 1, NT - 1)])
                                t.wait_ge(*prod[("w_tanh", gk - 1, NT - 1)])
                        ins = None
                        gb = (i % 2) * 8192
                        pR = pRedB if gi == 0 else pRedD
                        for j in range(J):
                            ins = t.matmul(pR[:, j * 8:(j + 1) * 8],
                                           grid[:, gb + gi * 4096 + j * 128: gb + gi * 4096 + (j + 1) * 128],
                                           wbo[:, gi * 8:(gi + 1) * 8], start=True, stop=True)
                        ins.then_inc(sT.inc(), 1)
                    blk.tensor(t_red)
                    prod[("w_redm", gk, i, gi)] = (sT.h, sT.v)

                def s_tanh(s, i=i, gk=gk, sl=sl):
                    s.wait_ge(*prod[("w_redm", gk, i, 1)])
                    s.wait_ge(*prod["w_reps2"])
                    if i >= 8:
                        s.wait_ge(*prod[("w_lg", gk, i - 8)])
                    elif gk > 0:
                        s.wait_ge(*prod[("w_lg", gk - 1, i + 8)])
                    s.activation(DLs[:, sl * 256:(sl + 1) * 256], pRedD[:], AF.Tanh,
                                 scale=0.5, bias=reps[:, 5:6]).then_inc(sS.inc(), 1)
                blk.scalar(s_tanh)
                prod[("w_tanh", gk, i)] = (sS.h, sS.v)

                if i >= 1:
                    emit_exp(i - 1)

            emit_lev(NT - 1)
            emit_exp(NT - 1)
            emit_B(NT - 2)
            emit_B_aow(NT - 3)
            emit_B(NT - 1)
            emit_B_aow(NT - 2)
            emit_B_aow(NT - 1)

        for i in range(NT):
            k = b * NT + i
            pp = i % 2

            def s_x3(sync, b=b, i=i, pp=pp):
                if i < 2:
                    sync.wait_ge(*prod[("w_cast2", b * 2 + 1, NT - 1)])
                if i >= 2:
                    sync.wait_ge(*prod[("w_z", b, i - 2)])
                sync.dma_start(xti[:, pp * C:(pp + 1) * C],
                               x_ap[b].rearrange("(n p) c -> p n c", p=128)[:, i]).then_inc(sXp[pp].dinc(), 16)
            blk.sync(s_x3)
            prod[("w_x3", b, i)] = (sXp[pp].h, sXp[pp].v)

            def s_aor(sync, b=b, i=i, pp=pp):
                sync.wait_ge(sAO0.h, prod[("w_aodma", b * 2 + 1, NT - 2)][1])
                sync.wait_ge(sAO1.h, prod[("w_aodma", b * 2 + 1, NT - 1)][1])
                if i >= 2:
                    sync.wait_ge(*prod[("w_aoT", b, i - 2)])
                sync.dma_start(aot[:, pp * C:(pp + 1) * C],
                               ao_d[b].rearrange("(n p) c -> p n c", p=128)[:, i]).then_inc(sAOr.dinc(), 16)
            blk.sync(s_aor)
            prod[("w_aor", b, i)] = (sAOr.h, sAOr.v)

            def t_aoT(t, b=b, i=i, pp=pp):
                t.wait_ge(*prod[("w_aor", b, i)])
                if i == 0:
                    t.wait_ge(*prod[("w_Ptev", b * 2 + 1, NT - 1)])
                if i >= 1:
                    t.wait_ge(*prod[("w_aev", b, i - 1)])
                elif b > 0:
                    t.wait_ge(*prod[("w_aev", b - 1, NT - 1)])
                ins = None
                for ch in range(NCH):
                    ins = t.transpose(pT[:, ch * 128:(ch + 1) * 128],
                                      aot[:, pp * C + ch * 128: pp * C + (ch + 1) * 128], idb[:])
                ins.then_inc(sT.inc(), 1)
            blk.tensor(t_aoT)
            prod[("w_aoT", b, i)] = (sT.h, sT.v)

            def v_aev(v, b=b, i=i, pp=pp):
                v.wait_ge(*prod[("w_aoT", b, i)])
                if i >= 2:
                    v.wait_ge(*prod[("w_cmm", b, i - 2)])
                v.tensor_copy(xfm[:, pp * C:(pp + 1) * C], pT[:]).then_inc(sV.inc(), 1)
            blk.vector(v_aev)
            prod[("w_aev", b, i)] = (sV.h, sV.v)

            def t_c(t, b=b, i=i, pp=pp):
                t.wait_ge(*prod[("w_aev", b, i)])
                if i >= 1:
                    t.wait_ge(*prod[("w_z", b, i - 1)])
                elif b > 0:
                    t.wait_ge(*prod[("w_z", b - 1, NT - 1)])
                ins = None
                for half in range(2):
                    dst = pj[:, half * 512:(half + 1) * 512]
                    for ch in range(NCH):
                        t.matmul(dst, xfm[:, pp * C + ch * 128: pp * C + (ch + 1) * 128],
                                 wcpj[:, ch * C + half * 512: ch * C + (half + 1) * 512],
                                 start=(ch == 0), stop=False)
                    ins = t.matmul(dst, ones[:], bcr[0:1, half * 512:(half + 1) * 512],
                                   start=False, stop=True)
                ins.then_inc(sT.inc(), 1)
            blk.tensor(t_c)
            prod[("w_cmm", b, i)] = (sT.h, sT.v)

            def v_z(v, b=b, i=i, pp=pp):
                v.wait_ge(*prod[("w_cmm", b, i)])
                v.wait_ge(*prod[("w_x3", b, i)])
                if i >= 1:
                    v.wait_ge(*prod[("w_fln", b, i - 1)])
                elif b > 0:
                    v.wait_ge(*prod[("w_fln", b - 1, NT - 1)])
                v.tensor_tensor(zt[:], pj[:], xti[:, pp * C:(pp + 1) * C], ALU.add).then_inc(sV.inc(), 1)
            blk.vector(v_z)
            prod[("w_z", b, i)] = (sV.h, sV.v)

            ews = [prod[("w_z", b, i)]]
            if i >= 1:
                ews.append(prod[("w_ydma", b, i - 1)])
            elif b > 0:
                ews.append(prod[("w_ydma", b - 1, NT - 1)])
            ln_emit(blk, sLN, zt[:], yst[:], scr[:], stx[:, 4:8], C,
                    prod, ("w_fln", b, i), extra_wait=ews)

            def sydma(sync, b=b, i=i):
                sync.wait_ge(*prod[("w_fln", b, i)])
                sync.dma_start(y_ap[b].rearrange("(n p) c -> p n c", p=128)[:, i],
                               yst[:]).then_inc(sO.dinc(), 16)
            blk.sync(sydma)
            prod[("w_ydma", b, i)] = (sO.h, sO.v)

    def sfin(sync):
        sync.wait_ge(sO.h, sO.v)
    blk.sync(sfin)
    return [v for v in locals().values() if isinstance(v, Sem)]


def l1attn_host_inputs(w_disp, b_disp, w_val, b_val, rel_pos_emb, w_state, b_state,
                       w_strain, b_strain, w_state_i, w_state_j, w_pos, w_bond_out,
                       b_bond_out, w_damage, b_damage, w_damage_out, b_damage_out,
                       w_cproj, b_cproj, pref="w_"):
    import ml_dtypes
    C = 1024
    bd = np.zeros((128, 640), np.float32)
    for bi, w in enumerate([w_state, w_strain, w_state_i, w_state_j, w_damage]):
        for h in range(8):
            bd[h * 16:(h + 1) * 16, bi * 128 + h * 16:bi * 128 + (h + 1) * 16] = w
    bdout = np.zeros((128, 16), np.float32)
    for h in range(8):
        bdout[h * 16:(h + 1) * 16, h] = w_bond_out[:, 0]
        bdout[h * 16:(h + 1) * 16, 8 + h] = w_damage_out[:, 0]
    rep16 = np.zeros((16, 128), np.float32)
    for h in range(8):
        rep16[:, h * 16:(h + 1) * 16] = np.eye(16)
    reps = np.zeros((128, 8), np.float32)
    reps[:, 0] = np.tile(b_state, 8)
    reps[:, 1] = np.tile(b_strain, 8)
    reps[:, 2] = np.tile(b_damage, 8)
    reps[:, 3] = b_bond_out[0]
    reps[:, 4] = b_damage_out[0]
    cnt = np.minimum(np.arange(32) + 1, 32).astype(np.float32)
    rc = np.broadcast_to((1.0 / cnt)[None, :], (128, 32)).copy()
    bdisp2 = np.stack([b_disp[0:128], b_disp[128:256]], axis=1).astype(np.float32)
    return {
        pref + "wdisp": np.ascontiguousarray(w_disp),
        pref + "bdisp2": bdisp2,
        pref + "wval": np.ascontiguousarray(w_val),
        pref + "bval": b_val.reshape(1, C).astype(np.float32),
        pref + "wcproj": np.ascontiguousarray(w_cproj),
        pref + "bcproj": b_cproj.reshape(1, C).astype(np.float32),
        pref + "bd": bd,
        pref + "bdout": bdout,
        pref + "relT": np.ascontiguousarray(rel_pos_emb.T),
        pref + "wpos": np.ascontiguousarray(w_pos),
        pref + "rep16": rep16,
        pref + "reps": reps,
        pref + "rc": rc,
        pref + "ident": np.eye(128, dtype=ml_dtypes.bfloat16),
        pref + "ones": np.ones((1, 128), dtype=ml_dtypes.bfloat16),
    }


class _RecIns:
    __slots__ = ("thens",)

    def __init__(self):
        self.thens = []

    def then_inc(self, sem, n):
        self.thens.append((sem, n))
        return self


class _RecEng:
    def __init__(self):
        self.q = []

    def __getattr__(self, name):
        if name.startswith("_"):
            raise AttributeError(name)

        def call(*a, **k):
            ri = _RecIns()
            self.q.append((name, a, k, ri))
            return ri
        return call


class MergeBlk:
    ENGS = ("gpsimd", "sync", "tensor", "vector", "scalar")

    def __init__(self, blk):
        self._blk = blk
        self._recs = {n: _RecEng() for n in self.ENGS}
        for n in self.ENGS:
            setattr(self, n, (lambda rec: lambda fn: fn(rec))(self._recs[n]))

    def flush(self):
        for n in self.ENGS:
            rec = self._recs[n]
            if not rec.q:
                continue
            q = rec.q
            rec.q = []

            def run(eng, q=q):
                for (nm, a, k, ri) in q:
                    ins = getattr(eng, nm)(*a, **k)
                    for (s, cnt) in ri.thens:
                        ins.then_inc(s, cnt)
            getattr(self._blk, n)(run)


from concourse.bass_utils import run_bass_kernel_spmd

N_CORES = 8
B_CORE = 2
_LAYERS = [build_l0attn, build_l0moe, build_l1attn, build_l1ffn]


def _build_fused():
    nc = bass.Bass("TRN2", target_bir_lowering=False, debug=False)
    x_d = nc.dram_tensor("x_in", [B_CORE, 2048, 1024], F32, kind="ExternalInput").ap()
    y1_d = nc.dram_tensor("y1_s", [B_CORE, 2048, 1024], F32, kind="Internal").ap()
    y2_d = nc.dram_tensor("y2_s", [B_CORE, 2048, 1024], F32, kind="Internal").ap()
    y3_d = nc.dram_tensor("y3_s", [B_CORE, 2048, 1024], F32, kind="Internal").ap()
    out_d = nc.dram_tensor("y_out", [B_CORE, 2048, 1024], F32, kind="ExternalOutput").ap()
    es = ExitStack()
    rblk = es.enter_context(nc.Block())
    blk = MergeBlk(rblk)
    bar = Sem(es, nc, "bar")
    prod = {}
    chain = [(build_l0attn, x_d, y1_d), (build_l0moe, y1_d, y2_d),
             (build_l1attn, y2_d, y3_d), (build_l1ffn, y3_d, out_d)]
    for li, (builder, xa, ya) in enumerate(chain):
        if li > 0:
            for engsec in (blk.gpsimd, blk.sync, blk.tensor, blk.vector, blk.scalar):
                engsec(lambda e, li=li: e.wait_ge(bar.h, li))
        with ExitStack() as les:
            lsems = builder(nc, les, blk, prod, B_CORE, x_ap=xa, y_ap=ya, ses=es)
        def _barrier(s, lsems=lsems):
            for sm in lsems:
                s.wait_ge(sm.h, sm.v)
            s.sem_inc(bar.h, 1)
        bar.v += 1
        blk.sync(_barrier)
        blk.flush()
    es.close()
    return nc


def _host_all(ins):
    h = {}
    h.update(l0attn_host_inputs(ins["L0_attn_Wqkv"], ins["L0_attn_bqkv"],
                                ins["L0_attn_Wo"], ins["L0_attn_bo"]))
    h.update(l0moe_host_inputs(ins["L0_ffn_centroids"], ins["L0_ffn_expert_biases"],
                               ins["L0_ffn_Wr"], ins["L0_ffn_br"],
                               ins["L0_ffn_Ws"], ins["L0_ffn_bs"]))
    h.update(l1attn_host_inputs(ins["L1_attn_w_disp"], ins["L1_attn_b_disp"],
                                ins["L1_attn_w_val"], ins["L1_attn_b_val"],
                                ins["L1_attn_rel_pos_emb"], ins["L1_attn_w_state"],
                                ins["L1_attn_b_state"], ins["L1_attn_w_strain"],
                                ins["L1_attn_b_strain"], ins["L1_attn_w_state_i"],
                                ins["L1_attn_w_state_j"], ins["L1_attn_w_pos"],
                                ins["L1_attn_w_bond_out"], ins["L1_attn_b_bond_out"],
                                ins["L1_attn_w_damage"], ins["L1_attn_b_damage"],
                                ins["L1_attn_w_damage_out"], ins["L1_attn_b_damage_out"],
                                ins["L1_attn_w_cproj"], ins["L1_attn_b_cproj"]))
    h.update(l1ffn_host_inputs(ins["L1_ffn_W_r"], ins["L1_ffn_W_k"], ins["L1_ffn_W_v"],
                               ins["L1_ffn_mu_r"], ins["L1_ffn_mu_k"]))
    return h


def kernel(**ins):
    for k in list(ins):
        ins[k] = np.asarray(ins[k])
    x = np.asarray(ins["L0_attn_x"], dtype=np.float32)
    host = _host_all(ins)
    nc = _build_fused()
    shards = np.split(np.ascontiguousarray(x), N_CORES, axis=0)
    in_maps = [{"x_in": s, **host} for s in shards]
    res = run_bass_kernel_spmd(nc, in_maps, core_ids=list(range(N_CORES)))
    return np.concatenate([np.asarray(r["y_out"]) for r in res.results], axis=0)


# version 16 (final)
# speedup vs baseline: 1.1816x; 1.1816x over previous
import numpy as np
import concourse.bass as bass
import concourse.mybir as mybir
from contextlib import ExitStack

F32 = mybir.dt.float32
BF16 = mybir.dt.bfloat16
AF = mybir.ActivationFunctionType
ALU = mybir.AluOpType
AX = mybir.AxisListType.X

LN_EPS = 1e-5


class Sem:
    def __init__(self, es, nc, name):
        self.h = es.enter_context(nc.semaphore(name))
        self.v = 0

    def inc(self, n=1):
        self.v += n
        return self.h

    def dinc(self):
        self.v += 16
        return self.h

    @property
    def cur(self):
        return (self.h, self.v)


def ln_emit(blk, sv, z, y_out, scr, st, n_free, prod, key, extra_wait=None):
    def v1(v):
        for w in (extra_wait or []):
            v.wait_ge(*w)
        v.reduce_sum(st[:, 0:1], z[:], axis=AX).then_inc(sv.inc(), 1)
        v.wait_ge(sv.h, sv.v)
        v.tensor_scalar(st[:, 1:2], st[:, 0:1], 1.0 / n_free, None, ALU.mult).then_inc(sv.inc(), 1)
        v.wait_ge(sv.h, sv.v)
        v.tensor_scalar(z[:], z[:], st[:, 1:2], None, ALU.subtract).then_inc(sv.inc(), 1)
    blk.vector(v1)

    def s1(s):
        s.wait_ge(sv.h, sv.v)
        s.activation(scr[:], z[:], AF.Square, accum_out=st[:, 2:3]).then_inc(sv.inc(), 1)
    blk.scalar(s1)

    def v2(v):
        v.wait_ge(sv.h, sv.v)
        v.tensor_scalar(st[:, 3:4], st[:, 2:3], 1.0 / n_free, LN_EPS, ALU.mult, ALU.add).then_inc(sv.inc(), 1)
    blk.vector(v2)

    def s2(s):
        s.wait_ge(sv.h, sv.v)
        s.activation(st[:, 0:1], st[:, 3:4], AF.Ln).then_inc(sv.inc(), 1)
        s.wait_ge(sv.h, sv.v)
        s.activation(st[:, 1:2], st[:, 0:1], AF.Exp, scale=-0.5).then_inc(sv.inc(), 1)
    blk.scalar(s2)

    def v3(v):
        v.wait_ge(sv.h, sv.v)
        v.tensor_scalar(y_out[:], z[:], st[:, 1:2], None, ALU.mult).then_inc(sv.inc(), 1)
        prod[key] = (sv.h, sv.v)
    blk.vector(v3)


def build_l1ffn(nc, es, blk, prod, B, T=2048, pref="f_", x_ap=None, y_ap=None, first=True, ses=None):
    ses = ses or es
    """L1 ffn + residual + postLN: y = LN(x + ffn(x)). x_ap/y_ap: DRAM [B,T,1024] fp32."""
    C = 1024
    NT = T // 128
    NCH = C // 128
    wrt_d = nc.dram_tensor(pref + "WrT", [C, C], F32, kind="ExternalInput").ap()
    wkt_d = nc.dram_tensor(pref + "WkT", [C, C], F32, kind="ExternalInput").ap()
    wvt_d = nc.dram_tensor(pref + "WvT", [C, C], F32, kind="ExternalInput").ap()
    mur_d = nc.dram_tensor(pref + "mur", [128, NCH], F32, kind="ExternalInput").ap()
    muk_d = nc.dram_tensor(pref + "muk", [128, NCH], F32, kind="ExternalInput").ap()
    id_d = nc.dram_tensor(pref + "ident", [128, 128], BF16, kind="ExternalInput").ap()

    E = es.enter_context
    xt = E(nc.sbuf_tensor(pref + "xt", [128, NT * C], F32))
    wr = E(nc.sbuf_tensor(pref + "wr", [128, NCH * C], BF16))
    wk = E(nc.sbuf_tensor(pref + "wk", [128, NCH * C], BF16))
    wv = E(nc.sbuf_tensor(pref + "wv", [128, NCH * C], BF16))
    mur = E(nc.sbuf_tensor(pref + "mur_s", [128, NCH], F32))
    muk = E(nc.sbuf_tensor(pref + "muk_s", [128, NCH], F32))
    murn = E(nc.sbuf_tensor(pref + "murn", [128, NCH], F32))
    mukn = E(nc.sbuf_tensor(pref + "mukn", [128, NCH], F32))
    idb = E(nc.sbuf_tensor(pref + "idb", [128, 128], BF16))
    xbf = E(nc.sbuf_tensor(pref + "xbf", [128, 2 * C], BF16))
    NE = NCH * 129
    xfm = E(nc.sbuf_tensor(pref + "xfm", [128, 2 * NE], BF16))
    xr = E(nc.sbuf_tensor(pref + "xr", [128, 2 * C], BF16))
    xk = E(nc.sbuf_tensor(pref + "xk", [128, 2 * C], BF16))
    rt = E(nc.sbuf_tensor(pref + "rt", [128, 2 * C], F32))
    dif = E(nc.sbuf_tensor(pref + "dif", [128, 2 * C], BF16))
    kab = E(nc.sbuf_tensor(pref + "kab", [128, 2 * C], BF16))
    kafm = E(nc.sbuf_tensor(pref + "kafm", [128, 2 * C], BF16))
    zt = E(nc.sbuf_tensor(pref + "zt", [128, 2 * C], F32))
    scr = E(nc.sbuf_tensor(pref + "scr", [128, 2 * C], F32))
    st = E(nc.sbuf_tensor(pref + "st", [128, 16], F32))
    yst = E(nc.sbuf_tensor(pref + "yst", [128, 4 * C], F32))
    pxT = E(nc.psum_tensor(pref + "pxT", [128, 1024], BF16))
    pr = E(nc.psum_tensor(pref + "pr", [128, 1024], F32))
    pk = E(nc.psum_tensor(pref + "pk", [128, 1024], F32))
    pkT = E(nc.psum_tensor(pref + "pkT", [128, 1024], BF16))
    pv = E(nc.psum_tensor(pref + "pv", [128, 1024], F32))

    sW = Sem(ses, nc, pref + "sW")
    sC = Sem(ses, nc, pref + "sC")
    sX = Sem(ses, nc, pref + "sX")
    sT = Sem(ses, nc, pref + "sT")
    sV = Sem(ses, nc, pref + "sV")
    sS = Sem(ses, nc, pref + "sS")
    sAO0 = Sem(ses, nc, pref + "sAO0")
    sAO1 = Sem(ses, nc, pref + "sAO1")
    sAOp = [sAO0, sAO1]
    sAOr = Sem(ses, nc, pref + "sAOr")
    sO = Sem(ses, nc, pref + "sO")
    sLN = Sem(ses, nc, pref + "sLN")

    def g0(g):
        g.dma_start(wr[:].rearrange("p (ch d) -> p ch d", d=C), wrt_d.rearrange("(ch p) d -> p ch d", p=128)).then_inc(sW.dinc(), 16)
        g.dma_start(wk[:].rearrange("p (ch d) -> p ch d", d=C), wkt_d.rearrange("(ch p) d -> p ch d", p=128)).then_inc(sW.dinc(), 16)
        g.dma_start(wv[:].rearrange("p (ch d) -> p ch d", d=C), wvt_d.rearrange("(ch p) d -> p ch d", p=128)).then_inc(sW.dinc(), 16)
    blk.gpsimd(g0)

    def s0(sync):
        sync.dma_start(mur[:], mur_d).then_inc(sC.dinc(), 16)
        sync.dma_start(muk[:], muk_d).then_inc(sC.dinc(), 16)
        sync.dma_start(idb[:], id_d).then_inc(sC.dinc(), 16)
    blk.sync(s0)
    consts_in = sC.v

    def v0(v):
        v.wait_ge(sC.h, consts_in)
        v.tensor_scalar(murn[:], mur[:], -1.0, 1.0, ALU.mult, ALU.add)
        v.tensor_scalar(mukn[:], muk[:], -1.0, 1.0, ALU.mult, ALU.add).then_inc(sV.inc(), 1)
    blk.vector(v0)
    prod["mu_ready"] = (sV.h, sV.v)

    for b in range(B):
        def sxb(sync, b=b):
            if b > 0:
                sync.wait_ge(*prod[("ffn_zdone", b - 1, NT - 1)])
            sync.dma_start(xt[:].rearrange("p (n c) -> p n c", c=C), x_ap[b].rearrange("(n p) c -> p n c", p=128)).then_inc(sX.dinc(), 16)
        blk.sync(sxb)
        prod[("ffn_x", b)] = (sX.h, sX.v)

        for i in range(NT):
            k = b * NT + i
            pp = k % 2
            xtile = xt[:, i * C:(i + 1) * C]

            def vcast(v, b=b, i=i, pp=pp, xtile=xtile):
                v.wait_ge(*prod[("ffn_x", b)])
                if b * NT + i >= 2:
                    kk = b * NT + i - 2
                    v.wait_ge(*prod[("ffn_xT", kk // NT, kk % NT)])
                v.tensor_copy(xbf[:, pp * C:(pp + 1) * C], xtile).then_inc(sV.inc(), 1)
            blk.vector(vcast)
            prod[("ffn_cast", b, i)] = (sV.h, sV.v)

            def t1(t, b=b, i=i, pp=pp):
                t.wait_ge(sC.h, 48)
                t.wait_ge(*prod[("ffn_cast", b, i)])
                if b * NT + i >= 1:
                    kk = b * NT + i - 1
                    t.wait_ge(*prod[("ffn_evx", kk // NT, kk % NT)])
                ins = None
                for ch in range(NCH):
                    ins = t.transpose(pxT[:, ch * 128:(ch + 1) * 128],
                                      xbf[:, pp * C + ch * 128: pp * C + (ch + 1) * 128], idb[:])
                ins.then_inc(sT.inc(), 1)
            blk.tensor(t1)
            prod[("ffn_xT", b, i)] = (sT.h, sT.v)

            def vevac(v, b=b, i=i, pp=pp, k=k):
                v.wait_ge(*prod[("ffn_xT", b, i)])
                if k >= 2:
                    kk = k - 2
                    v.wait_ge(*prod[("ffn_mix", kk // NT, kk % NT)])
                if i == 0:
                    v.memset(xfm[:, pp * NE:(pp + 1) * NE].rearrange("p (ch e) -> p ch e", e=129)[:, :, 0:1], 0.0).then_inc(sV.inc(), 1)
                else:
                    v.tensor_copy(
                        xfm[:, pp * NE:(pp + 1) * NE].rearrange("p (ch e) -> p ch e", e=129)[:, :, 0:1],
                        xfm[:, (1 - pp) * NE:(2 - pp) * NE].rearrange("p (ch e) -> p ch e", e=129)[:, :, 128:129]).then_inc(sV.inc(), 1)
                v.wait_ge(sV.h, sV.v)
                v.tensor_copy(
                    xfm[:, pp * NE:(pp + 1) * NE].rearrange("p (ch e) -> p ch e", e=129)[:, :, 1:129],
                    pxT[:].rearrange("p (ch e) -> p ch e", e=128)).then_inc(sV.inc(), 1)
            blk.vector(vevac)
            prod[("ffn_evx", b, i)] = (sV.h, sV.v)

            def vmix(v, b=b, i=i, pp=pp, k=k):
                v.wait_ge(*prod[("ffn_evx", b, i)])
                v.wait_ge(*prod["mu_ready"])
                if k >= 2:
                    kk = k - 2
                    v.wait_ge(*prod[("ffn_kmm", kk // NT, kk % NT)])
                ext = xfm[:, pp * NE:(pp + 1) * NE].rearrange("p (ch e) -> p ch e", e=129)
                cur = ext[:, :, 1:129]
                prv = ext[:, :, 0:128]
                dif3 = dif[:, pp * C:(pp + 1) * C].rearrange("p (ch e) -> p ch e", e=128)
                v.tensor_tensor(dif3, cur, prv, ALU.subtract).then_inc(sV.inc(), 1)
                v.wait_ge(sV.h, sV.v)
                ins = None
                for ch in range(NCH):
                    v.scalar_tensor_tensor(xr[:, pp * C + ch * 128:pp * C + (ch + 1) * 128],
                                           dif3[:, ch], mur[:, ch:ch + 1],
                                           prv[:, ch], ALU.mult, ALU.add)
                    ins = v.scalar_tensor_tensor(xk[:, pp * C + ch * 128:pp * C + (ch + 1) * 128],
                                           dif3[:, ch], muk[:, ch:ch + 1],
                                           prv[:, ch], ALU.mult, ALU.add)
                ins.then_inc(sV.inc(), 1)
            blk.vector(vmix)
            prod[("ffn_mix", b, i)] = (sV.h, sV.v)

            def t2(t, b=b, i=i, pp=pp, k=k):
                t.wait_ge(sW.h, 48)
                t.wait_ge(*prod[("ffn_mix", b, i)])
                if k >= 1:
                    kk = k - 1
                    t.wait_ge(*prod[("ffn_sig", kk // NT, kk % NT)])
                    t.wait_ge(*prod[("ffn_relu", kk // NT, kk % NT)])
                ins = None
                for ch in range(NCH):
                    lhs = xr[:, pp * C + ch * 128: pp * C + (ch + 1) * 128]
                    t.matmul(pr[:, 0:512], lhs, wr[:, ch * C: ch * C + 512], start=(ch == 0), stop=(ch == NCH - 1))
                    ins = t.matmul(pr[:, 512:1024], lhs, wr[:, ch * C + 512:(ch + 1) * C], start=(ch == 0), stop=(ch == NCH - 1))
                ins.then_inc(sT.inc(), 1)
                for ch in range(NCH):
                    lhs = xk[:, pp * C + ch * 128: pp * C + (ch + 1) * 128]
                    t.matmul(pk[:, 0:512], lhs, wk[:, ch * C: ch * C + 512], start=(ch == 0), stop=(ch == NCH - 1))
                    ins = t.matmul(pk[:, 512:1024], lhs, wk[:, ch * C + 512:(ch + 1) * C], start=(ch == 0), stop=(ch == NCH - 1))
                ins.then_inc(sT.inc(), 1)
            blk.tensor(t2)
            prod[("ffn_rmm", b, i)] = (sT.h, sT.v - 1)
            prod[("ffn_kmm", b, i)] = (sT.h, sT.v)

            def ssig(s, b=b, i=i, pp=pp):
                s.wait_ge(*prod[("ffn_rmm", b, i)])
                s.activation(rt[:, pp * C:(pp + 1) * C], pr[:], AF.Exp, scale=-1.0).then_inc(sS.inc(), 1)
            blk.scalar(ssig)
            prod[("ffn_sige", b, i)] = (sS.h, sS.v)

            def vsig(v, b=b, i=i, pp=pp):
                v.wait_ge(*prod[("ffn_sige", b, i)])
                v.tensor_scalar(rt[:, pp * C:(pp + 1) * C], rt[:, pp * C:(pp + 1) * C], 1.0, None, ALU.add).then_inc(sV.inc(), 1)
                v.wait_ge(sV.h, sV.v)
                v.reciprocal(rt[:, pp * C:(pp + 1) * C], rt[:, pp * C:(pp + 1) * C]).then_inc(sV.inc(), 1)
            blk.vector(vsig)
            prod[("ffn_sig", b, i)] = (sV.h, sV.v)

            def vrelu(v, b=b, i=i, pp=pp, k=k):
                v.wait_ge(*prod[("ffn_kmm", b, i)])
                if k >= 1:
                    kk = k - 1
                    v.wait_ge(*prod[("ffn_kaT", kk // NT, kk % NT)])
                v.tensor_scalar(kab[:, pp * C:(pp + 1) * C], pk[:], 0.0, None, ALU.max).then_inc(sV.inc(), 1)
                v.wait_ge(sV.h, sV.v)
                v.tensor_tensor(kab[:, pp * C:(pp + 1) * C], kab[:, pp * C:(pp + 1) * C],
                                kab[:, pp * C:(pp + 1) * C], ALU.mult).then_inc(sV.inc(), 1)
            blk.vector(vrelu)
            prod[("ffn_relu", b, i)] = (sV.h, sV.v)

            def t3(t, b=b, i=i, pp=pp, k=k):
                t.wait_ge(*prod[("ffn_relu", b, i)])
                if k >= 1:
                    kk = k - 1
                    t.wait_ge(*prod[("ffn_evk", kk // NT, kk % NT)])
                ins = None
                for ch in range(NCH):
                    ins = t.transpose(pkT[:, ch * 128:(ch + 1) * 128],
                                      kab[:, pp * C + ch * 128: pp * C + (ch + 1) * 128], idb[:])
                ins.then_inc(sT.inc(), 1)
            blk.tensor(t3)
            prod[("ffn_kaT", b, i)] = (sT.h, sT.v)

            def vevk(v, b=b, i=i, pp=pp, k=k):
                v.wait_ge(*prod[("ffn_kaT", b, i)])
                if k >= 2:
                    kk = k - 2
                    v.wait_ge(*prod[("ffn_vmm", kk // NT, kk % NT)])
                v.tensor_copy(kafm[:, pp * C:(pp + 1) * C], pkT[:]).then_inc(sV.inc(), 1)
            blk.vector(vevk)
            prod[("ffn_evk", b, i)] = (sV.h, sV.v)

            def t4(t, b=b, i=i, pp=pp, k=k):
                t.wait_ge(*prod[("ffn_evk", b, i)])
                if k >= 1:
                    kk = k - 1
                    t.wait_ge(*prod[("ffn_rv", kk // NT, kk % NT)])
                ins = None
                for ch in range(NCH):
                    lhs = kafm[:, pp * C + ch * 128: pp * C + (ch + 1) * 128]
                    t.matmul(pv[:, 0:512], lhs, wv[:, ch * C: ch * C + 512], start=(ch == 0), stop=(ch == NCH - 1))
                    ins = t.matmul(pv[:, 512:1024], lhs, wv[:, ch * C + 512:(ch + 1) * C], start=(ch == 0), stop=(ch == NCH - 1))
                ins.then_inc(sT.inc(), 1)
            blk.tensor(t4)
            prod[("ffn_vmm", b, i)] = (sT.h, sT.v)

            def vrv(v, b=b, i=i, pp=pp, xtile=xtile, k=k):
                v.wait_ge(*prod[("ffn_vmm", b, i)])
                v.wait_ge(*prod[("ffn_sig", b, i)])
                if k >= 2:
                    kk = k - 2
                    v.wait_ge(*prod[("ffn_lnout", kk // NT, kk % NT)])
                v.tensor_tensor(zt[:, pp * C:(pp + 1) * C], rt[:, pp * C:(pp + 1) * C], pv[:], ALU.mult).then_inc(sV.inc(), 1)
                v.wait_ge(sV.h, sV.v)
                v.tensor_tensor(zt[:, pp * C:(pp + 1) * C], zt[:, pp * C:(pp + 1) * C], xtile, ALU.add).then_inc(sV.inc(), 1)
            blk.vector(vrv)
            prod[("ffn_rv", b, i)] = (sV.h, sV.v)
            prod[("ffn_zdone", b, i)] = (sV.h, sV.v)

            slot = i % 4
            gkey = ("ffn_ydma", b, i // 4 - 1) if i // 4 >= 1 else (("ffn_ydma", b - 1, (NT - 1) // 4) if b > 0 else None)
            ews = [prod[("ffn_rv", b, i)]]
            if gkey is not None and gkey in prod:
                ews.append(prod[gkey])
            ln_emit(blk, sLN, zt[:, pp * C:(pp + 1) * C], yst[:, slot * C:(slot + 1) * C],
                    scr[:, pp * C:(pp + 1) * C], st[:, 8:12], C, prod, ("ffn_lnout", b, i),
                    extra_wait=ews)

            if slot == 3 or i == NT - 1:
                g0i = i - slot
                def sydma(sync, b=b, i=i, g0i=g0i, slot=slot):
                    for jj in range(g0i, i + 1):
                        sync.wait_ge(*prod[("ffn_lnout", b, jj)])
                    sync.dma_start(
                        y_ap[b].rearrange("(n p) c -> p n c", p=128)[:, g0i:i + 1],
                        yst[:, 0:(slot + 1) * C].rearrange("p (n c) -> p n c", c=C)).then_inc(sO.dinc(), 16)
                blk.sync(sydma)
                prod[("ffn_ydma", b, i // 4)] = (sO.h, sO.v)

    def sfin(sync):
        sync.wait_ge(sO.h, sO.v)
    blk.sync(sfin)
    return [v for v in locals().values() if isinstance(v, Sem)]


def l1ffn_host_inputs(W_r, W_k, W_v, mu_r, mu_k):
    C = W_r.shape[0]
    NCH = C // 128
    import ml_dtypes
    return {
        "f_WrT": np.ascontiguousarray(W_r.T),
        "f_WkT": np.ascontiguousarray(W_k.T),
        "f_WvT": np.ascontiguousarray(W_v.T),
        "f_mur": np.ascontiguousarray(mu_r.reshape(NCH, 128).T),
        "f_muk": np.ascontiguousarray(mu_k.reshape(NCH, 128).T),
        "f_ident": np.eye(128, dtype=ml_dtypes.bfloat16),
    }


def l0attn_host_inputs(Wqkv, bqkv, Wo, bo, pref="a_"):
    import ml_dtypes
    C = 1024
    FG = 16
    WqkvT = Wqkv.T
    cols_qk = []
    cols_v = []
    for fg in range(FG):
        cols_qk.extend(range(fg * 192, fg * 192 + 128))
        cols_v.extend(range(fg * 192 + 128, fg * 192 + 192))
    Wcat = np.ascontiguousarray(np.concatenate([WqkvT[:, cols_qk], WqkvT[:, cols_v]], axis=1))
    bqk = np.ascontiguousarray(np.stack([bqkv[cols_qk[fg * 128:(fg + 1) * 128]] for fg in range(FG)], axis=1))
    bv = bqkv[np.array(cols_v)].reshape(1, C).astype(np.float32)
    return {
        pref + "Wcat": Wcat.astype(np.float32),
        pref + "bqk": bqk.astype(np.float32),
        pref + "bv": bv,
        pref + "WoT": np.ascontiguousarray(Wo.T),
        pref + "bo": bo.reshape(1, C).astype(np.float32),
        pref + "ident": np.eye(128, dtype=ml_dtypes.bfloat16),
        pref + "ones": np.ones((1, 128), dtype=ml_dtypes.bfloat16),
    }


def build_l0attn(nc, es, blk, prod, B, pref="a_", x_ap=None, y_ap=None, ses=None):
    ses = ses or es
    C = 1024
    NH = 16
    T = NH * 128
    FG = 16
    NCH = 8
    NP = NH // 2
    NPAIR = B * NP
    w_d = nc.dram_tensor(pref + "Wcat", [C, 3 * C], F32, kind="ExternalInput").ap()
    bqk_d = nc.dram_tensor(pref + "bqk", [128, FG], F32, kind="ExternalInput").ap()
    bv_d = nc.dram_tensor(pref + "bv", [1, C], F32, kind="ExternalInput").ap()
    wo_d = nc.dram_tensor(pref + "WoT", [C, C], F32, kind="ExternalInput").ap()
    bo_d = nc.dram_tensor(pref + "bo", [1, C], F32, kind="ExternalInput").ap()
    id_d = nc.dram_tensor(pref + "ident", [128, 128], BF16, kind="ExternalInput").ap()
    on_d = nc.dram_tensor(pref + "ones", [1, 128], BF16, kind="ExternalInput").ap()

    E = es.enter_context
    wc = E(nc.sbuf_tensor(pref + "wc", [128, NCH * 3 * C], BF16))
    wo = E(nc.sbuf_tensor(pref + "wo", [128, NCH * C], BF16))
    bqk = E(nc.sbuf_tensor(pref + "bqk_s", [128, FG], F32))
    bvr = E(nc.sbuf_tensor(pref + "bvr", [1, C], BF16))
    bor = E(nc.sbuf_tensor(pref + "bor", [1, C], BF16))
    idb = E(nc.sbuf_tensor(pref + "idb", [128, 128], BF16))
    idf = E(nc.sbuf_tensor(pref + "idf", [128, 128], F32))
    ones = E(nc.sbuf_tensor(pref + "ones_s", [1, 128], BF16))
    xti = E(nc.sbuf_tensor(pref + "xti", [128, 4 * C], F32))
    xbf = E(nc.sbuf_tensor(pref + "xbf", [128, 2 * C], BF16))
    xfm = E(nc.sbuf_tensor(pref + "xfm", [128, 2 * C], BF16))
    qkA = E(nc.sbuf_tensor(pref + "qkA", [128, T], BF16))
    qkB = E(nc.sbuf_tensor(pref + "qkB", [128, T], BF16))
    kA = E(nc.sbuf_tensor(pref + "kA", [64, T], BF16))
    qB = E(nc.sbuf_tensor(pref + "qB", [128, T], BF16))
    v2 = E(nc.sbuf_tensor(pref + "v2", [128, 2 * FG * 65], BF16))
    pt = E(nc.sbuf_tensor(pref + "pt", [128, 4 * 1024], BF16))
    pvf = E(nc.sbuf_tensor(pref + "pvf", [65, 2 * 512], F32))
    vals = E(nc.sbuf_tensor(pref + "vals", [128, 2 * C], BF16))
    vfm = E(nc.sbuf_tensor(pref + "vfm", [128, C], BF16))
    zt = E(nc.sbuf_tensor(pref + "zt", [128, C], F32))
    scr = E(nc.sbuf_tensor(pref + "scr", [128, C], F32))
    st = E(nc.sbuf_tensor(pref + "st", [128, 8], F32))
    yst = E(nc.sbuf_tensor(pref + "yst", [128, 2 * C], F32))

    pool = E(nc.psum_tensor(pref + "pool", [128, 4096], F32))
    poolb = pool[:].bitcast(BF16)
    psc = [pool[:, r * 1024:(r + 1) * 1024] for r in range(2)]
    pxT = poolb[:, 5120:6144]
    pqk = pool[:, 2048:2560]
    pv_ = pool[:, 2560:3072]
    ppv = [pool[0:65, 3072 + hh * 512: 3072 + hh * 512 + 512] for hh in range(2)]
    pvT = [pool[0:128, 3072 + hh * 512: 3072 + hh * 512 + 260] for hh in range(2)]
    pvalT = poolb[:, 6144:7168]
    po = pool[:, 3072:4096]

    sW = Sem(ses, nc, pref + "sW")
    sC = Sem(ses, nc, pref + "sC")
    sXa = Sem(ses, nc, pref + "sXa")
    sXb = Sem(ses, nc, pref + "sXb")
    sXh = [sXa, sXb]
    sX3a = Sem(ses, nc, pref + "sX3a")
    sX3b = Sem(ses, nc, pref + "sX3b")
    sX3h = [sX3a, sX3b]
    sT = Sem(ses, nc, pref + "sT")
    sV = Sem(ses, nc, pref + "sV")
    sS = Sem(ses, nc, pref + "sS")
    sG = Sem(ses, nc, pref + "sG")
    sAO0 = Sem(ses, nc, pref + "sAO0")
    sAO1 = Sem(ses, nc, pref + "sAO1")
    sAOp = [sAO0, sAO1]
    sAOr = Sem(ses, nc, pref + "sAOr")
    sO = Sem(ses, nc, pref + "sO")
    sLN = Sem(ses, nc, pref + "sLN")

    def g0(g):
        g.dma_start(wc[:].rearrange("p (ch d) -> p ch d", d=3 * C),
                    w_d.rearrange("(ch p) d -> p ch d", p=128)).then_inc(sW.dinc(), 16)
        g.dma_start(wo[:].rearrange("p (ch d) -> p ch d", d=C),
                    wo_d.rearrange("(ch p) d -> p ch d", p=128)).then_inc(sW.dinc(), 16)
        g.dma_start(bvr[:], bv_d).then_inc(sW.dinc(), 16)
        g.dma_start(bor[:], bo_d).then_inc(sW.dinc(), 16)
    blk.gpsimd(g0)
    W_ALL = sW.v

    def s0(sync):
        sync.dma_start(bqk[:], bqk_d).then_inc(sC.dinc(), 16)
        sync.dma_start(idb[:], id_d).then_inc(sC.dinc(), 16)
        sync.dma_start(ones[:], on_d).then_inc(sC.dinc(), 16)
    blk.sync(s0)
    CIN = sC.v

    def v00(v):
        v.wait_ge(sC.h, CIN)
        v.tensor_copy(idf[:], idb[:])
        v.memset(v2[:].rearrange("p (a fg e) -> p a fg e", fg=FG, e=65)[:, :, :, 64:65], 1.0).then_inc(sV.inc(), 1)
    blk.vector(v00)
    prod["a_init"] = (sV.h, sV.v)

    def emit_ph1(kpair):
        b, hp = divmod(kpair, NP)
        for hh in range(2):
            h = 2 * hp + hh

            def sx(sync, b=b, h=h, hh=hh, kpair=kpair):
                if kpair >= 1:
                    sync.wait_ge(*prod[("a_xT", kpair - 1, hh)])
                sync.dma_start(xti[:, hh * C:(hh + 1) * C],
                               x_ap[b].rearrange("(n p) c -> p n c", p=128)[:, h]).then_inc(sXh[hh].dinc(), 16)
            blk.sync(sx)
            prod[("a_x", kpair, hh)] = (sXh[hh].h, sXh[hh].v)

            def vc(v, hh=hh, kpair=kpair):
                v.wait_ge(*prod[("a_x", kpair, hh)])
                if kpair >= 1:
                    v.wait_ge(*prod[("a_xT", kpair - 1, hh)])
                v.tensor_copy(xbf[:, hh * C:(hh + 1) * C], xti[:, hh * C:(hh + 1) * C]).then_inc(sV.inc(), 1)
            blk.vector(vc)
            prod[("a_cast", kpair, hh)] = (sV.h, sV.v)

            def t_xT(t, hh=hh, kpair=kpair):
                t.wait_ge(sC.h, CIN)
                t.wait_ge(*prod[("a_cast", kpair, hh)])
                pk = (kpair, 0) if hh == 1 else ((kpair - 1, 1) if kpair >= 1 else None)
                if pk is not None:
                    t.wait_ge(*prod[("a_vev", pk[0], pk[1], 1)])
                ins = None
                for ch in range(NCH):
                    ins = t.transpose(pxT[:, ch * 128:(ch + 1) * 128],
                                      xbf[:, hh * C + ch * 128: hh * C + (ch + 1) * 128], idb[:])
                ins.then_inc(sT.inc(), 1)
            blk.tensor(t_xT)
            prod[("a_xT", kpair, hh)] = (sT.h, sT.v)

            def vev(v, hh=hh, kpair=kpair):
                v.wait_ge(*prod[("a_xT", kpair, hh)])
                if kpair >= 1:
                    t = prod.get(("a_vmm", kpair - 1, hh, 1))
                    if t:
                        v.wait_ge(*t)
                v.tensor_copy(xfm[:, hh * C:(hh + 1) * C], pxT[:]).then_inc(sV.inc(), 1)
            blk.vector(vev)
            prod[("a_evxT", kpair, hh)] = (sV.h, sV.v)

            for grp in range(4):
                def t_qk(t, hh=hh, kpair=kpair, grp=grp):
                    t.wait_ge(sW.h, W_ALL)
                    t.wait_ge(*prod[("a_evxT", kpair, hh)])
                    pk = (kpair, hh, grp - 1) if grp >= 1 else (
                        (kpair, 0, 3) if hh == 1 else ((kpair - 1, 1, 3) if kpair >= 1 else None))
                    if pk is not None:
                        t.wait_ge(*prod[("a_qkev", *pk)])
                    ins = None
                    for fgi in range(4):
                        fg = grp * 4 + fgi
                        for ch in range(NCH):
                            ins = t.matmul(pqk[:, fgi * 128:(fgi + 1) * 128],
                                           wc[:, ch * 3 * C + fg * 128: ch * 3 * C + (fg + 1) * 128],
                                           xfm[:, hh * C + ch * 128: hh * C + (ch + 1) * 128],
                                           start=(ch == 0), stop=(ch == NCH - 1))
                    ins.then_inc(sT.inc(), 1)
                blk.tensor(t_qk)
                prod[("a_qkmm", kpair, hh, grp)] = (sT.h, sT.v)

                def v_qkev(v, hh=hh, kpair=kpair, grp=grp):
                    v.wait_ge(*prod[("a_qkmm", kpair, hh, grp)])
                    if kpair >= 1:
                        v.wait_ge(*prod[("a_scdone", kpair - 1, hh)])
                    ins = None
                    for fgi in range(4):
                        fg = grp * 4 + fgi
                        src = pqk[:, fgi * 128:(fgi + 1) * 128]
                        dst = (qkA if hh == 0 else qkB)[:, fg * 128:(fg + 1) * 128]
                        ins = v.tensor_scalar(dst, src, bqk[:, fg:fg + 1], None, ALU.add)
                    ins.then_inc(sV.inc(), 1)
                blk.vector(v_qkev)
                prod[("a_qkev", kpair, hh, grp)] = (sV.h, sV.v)

            def g_shift(g, hh=hh, kpair=kpair):
                for grp in range(4):
                    g.wait_ge(*prod[("a_qkev", kpair, hh, grp)])
                if kpair >= 1:
                    g.wait_ge(*prod[("a_scdone", kpair - 1, hh)])
                if hh == 0:
                    g.dma_start(kA[0:64, :], qkA[64:128, :]).then_inc(sG.dinc(), 16)
                else:
                    g.dma_start(qB[64:128, :], qkB[0:64, :]).then_inc(sG.dinc(), 16)
            blk.gpsimd(g_shift)
            prod[("a_shift", kpair, hh)] = (sG.h, sG.v)

            for half in range(2):
                def t_v(t, hh=hh, kpair=kpair, half=half):
                    t.wait_ge(sW.h, W_ALL)
                    t.wait_ge(*prod[("a_evxT", kpair, hh)])
                    if half == 1:
                        t.wait_ge(*prod[("a_vev", kpair, hh, 0)])
                    elif hh == 1:
                        t.wait_ge(*prod[("a_vev", kpair, 0, 1)])
                    elif kpair >= 1:
                        t.wait_ge(*prod[("a_vev", kpair - 1, 1, 1)])
                    for ch in range(NCH):
                        t.matmul(pv_[:], xfm[:, hh * C + ch * 128: hh * C + (ch + 1) * 128],
                                 wc[:, ch * 3 * C + 2 * C + half * 512: ch * 3 * C + 2 * C + (half + 1) * 512],
                                 start=(ch == 0), stop=False)
                    ins = t.matmul(pv_[:], ones[:], bvr[0:1, half * 512:(half + 1) * 512],
                                   start=False, stop=True)
                    ins.then_inc(sT.inc(), 1)
                blk.tensor(t_v)
                prod[("a_vmm", kpair, hh, half)] = (sT.h, sT.v)

                def v_vev(v, hh=hh, kpair=kpair, half=half):
                    v.wait_ge(*prod[("a_vmm", kpair, hh, half)])
                    v.wait_ge(*prod["a_init"])
                    if kpair >= 1:
                        v.wait_ge(*prod[("a_pvmm", kpair - 1, 3, hh, 7)])
                    dst = v2[:, hh * FG * 65:(hh + 1) * FG * 65].rearrange(
                        "p (fg e) -> p fg e", e=65)[:, half * 8:(half + 1) * 8, 0:64]
                    v.tensor_copy(dst, pv_[:].rearrange("p (fg e) -> p fg e", e=64)).then_inc(sV.inc(), 1)
                blk.vector(v_vev)
                prod[("a_vev", kpair, hh, half)] = (sV.h, sV.v)

    def emit_ph2(kpair):
        pv_emitters = []
        for qc in range(4):
            for hh in range(2):
                for fgp in range(8):
                    idx = qc * 16 + hh * 8 + fgp
                    rot = idx % 2

                    def t_sc(t, hh=hh, kpair=kpair, qc=qc, fgp=fgp, rot=rot, idx=idx):
                        if idx == 0:
                            t.wait_ge(*prod[("a_shift", kpair, 1)])
                        if idx >= 2:
                            pi = idx - 2
                            t.wait_ge(*prod[("a_exp", kpair, pi // 16, (pi % 16) // 8, pi % 8)])
                        elif kpair >= 1:
                            t.wait_ge(*prod[("a_exp", kpair - 1, 3, 1, 6 + idx)])
                        ins = None
                        for fgi in range(2):
                            fg = fgp * 2 + fgi
                            if hh == 0:
                                lhsT = kA[0:64, fg * 128:(fg + 1) * 128]
                                rhs = qkA[0:64, qc * 512:(qc + 1) * 512]
                            else:
                                lhsT = qkB[64:128, fg * 128:(fg + 1) * 128]
                                rhs = qB[64:128, qc * 512:(qc + 1) * 512]
                            ins = t.matmul(psc[rot][:, fgi * 512:(fgi + 1) * 512],
                                           lhsT, rhs, start=True, stop=True)
                        ins.then_inc(sT.inc(), 1)
                    blk.tensor(t_sc)
                    prod[("a_scmm", kpair, qc, hh, fgp)] = (sT.h, sT.v)

                    def s_exp(s, hh=hh, kpair=kpair, qc=qc, fgp=fgp, rot=rot):
                        s.wait_ge(*prod[("a_scmm", kpair, qc, hh, fgp)])
                        if fgp >= 2:
                            s.wait_ge(*prod[("a_pvmm", kpair, qc, hh, fgp - 2)])
                        elif qc >= 1:
                            s.wait_ge(*prod[("a_pvmm", kpair, qc - 1, hh, 6 + fgp)])
                        elif kpair >= 1:
                            s.wait_ge(*prod[("a_pvmm", kpair - 1, 3, hh, 6 + fgp)])
                        s.activation(pt[:, (hh * 2 + fgp % 2) * 1024:(hh * 2 + fgp % 2 + 1) * 1024],
                                     psc[rot][:], AF.Exp, scale=0.125).then_inc(sS.inc(), 1)
                    blk.scalar(s_exp)
                    prod[("a_exp", kpair, qc, hh, fgp)] = (sS.h, sS.v)

                    def mk_pv(hh=hh, kpair=kpair, qc=qc, fgp=fgp):
                      def t_pv(t, hh=hh, kpair=kpair, qc=qc, fgp=fgp):
                        t.wait_ge(*prod[("a_exp", kpair, qc, hh, fgp)])
                        if fgp == 0:
                            if qc == 0:
                                if kpair >= 1:
                                    t.wait_ge(*prod[("a_oz", kpair - 1, 1)])
                            else:
                                t.wait_ge(*prod[("a_pvev", kpair, qc - 1, hh)])
                                t.wait_ge(*prod[("a_norm", kpair, qc - 1, hh)])
                        ins = None
                        for fgi in range(2):
                            fg = fgp * 2 + fgi
                            vslab = v2[:, hh * FG * 65 + fg * 65:hh * FG * 65 + (fg + 1) * 65]
                            ins = t.matmul(ppv[hh], vslab,
                                           pt[:, (hh * 2 + fgp % 2) * 1024 + fgi * 512:(hh * 2 + fgp % 2) * 1024 + (fgi + 1) * 512],
                                           start=(fgp == 0 and fgi == 0), stop=(fgp == 7 and fgi == 1))
                        ins.then_inc(sT.inc(), 1)
                      t_pv._key = (kpair, qc, hh, fgp)
                      return t_pv
                    pv_emitters.append(mk_pv())
                    if len(pv_emitters) >= 2:
                        em = pv_emitters.pop(0)
                        blk.tensor(em)
                        prod[("a_pvmm",) + em._key] = (sT.h, sT.v)

                while pv_emitters:
                    em = pv_emitters.pop(0)
                    blk.tensor(em)
                    prod[("a_pvmm",) + em._key] = (sT.h, sT.v)

                def v_pvev(v, hh=hh, kpair=kpair, qc=qc):
                    v.wait_ge(*prod[("a_pvmm", kpair, qc, hh, 7)])
                    if qc >= 2:
                        v.wait_ge(*prod[("a_pvT", kpair, qc - 2, hh)])
                    elif kpair >= 1:
                        v.wait_ge(*prod[("a_pvT", kpair - 1, qc + 2, hh)])
                    v.tensor_copy(pvf[:, (qc % 2) * 512:(qc % 2 + 1) * 512], ppv[hh]).then_inc(sV.inc(), 1)
                blk.vector(v_pvev)
                prod[("a_pvev", kpair, qc, hh)] = (sV.h, sV.v)

                def t_pvT(t, hh=hh, kpair=kpair, qc=qc):
                    t.wait_ge(*prod[("a_pvev", kpair, qc, hh)])
                    t.wait_ge(*prod["a_init"])
                    ins = None
                    for sub in range(4):
                        ins = t.transpose(pvT[hh][:, sub * 65:(sub + 1) * 65],
                                          pvf[0:65, (qc % 2) * 512 + sub * 128:(qc % 2) * 512 + (sub + 1) * 128],
                                          idf[0:65, 0:65])
                    ins.then_inc(sT.inc(), 1)
                blk.tensor(t_pvT)
                prod[("a_pvT", kpair, qc, hh)] = (sT.h, sT.v)

                def v_norm(v, hh=hh, kpair=kpair, qc=qc):
                    v.wait_ge(*prod[("a_pvT", kpair, qc, hh)])
                    if qc == 0 and kpair >= 1:
                        v.wait_ge(*prod[("a_vT", kpair - 1, hh)])
                    ins = None
                    for sub in range(4):
                        ins = v.reciprocal(st[:, sub:sub + 1], pvT[hh][:, sub * 65 + 64:sub * 65 + 65])
                    ins.then_inc(sV.inc(), 1)
                    v.wait_ge(sV.h, sV.v)
                    for sub in range(4):
                        fgq = qc * 4 + sub
                        ins = v.tensor_scalar(vals[:, hh * C + fgq * 64: hh * C + (fgq + 1) * 64],
                                              pvT[hh][:, sub * 65:sub * 65 + 64], st[:, sub:sub + 1], None, ALU.mult)
                    ins.then_inc(sV.inc(), 1)
                blk.vector(v_norm)
                prod[("a_norm", kpair, qc, hh)] = (sV.h, sV.v)

        prod[("a_scdone", kpair, 0)] = prod[("a_scmm", kpair, 3, 0, 7)]
        prod[("a_scdone", kpair, 1)] = prod[("a_scmm", kpair, 3, 1, 7)]

    def emit_ph3(kpair):
        b, hp = divmod(kpair, NP)
        for hh in range(2):
            h = 2 * hp + hh
            px = 2 + hh

            def sx3(sync, b=b, h=h, hh=hh, kpair=kpair, px=px):
                if kpair >= 1:
                    sync.wait_ge(*prod[("a_oz", kpair - 1, hh)])
                sync.dma_start(xti[:, px * C:(px + 1) * C],
                               x_ap[b].rearrange("(n p) c -> p n c", p=128)[:, h]).then_inc(sX3h[hh].dinc(), 16)
            blk.sync(sx3)
            prod[("a_x3", kpair, hh)] = (sX3h[hh].h, sX3h[hh].v)

            def t_vT(t, hh=hh, kpair=kpair):
                for qcx in range(4):
                    t.wait_ge(*prod[("a_norm", kpair, qcx, hh)])
                t.wait_ge(*prod[("a_pvev", kpair, 3, hh)])
                if hh == 1:
                    t.wait_ge(*prod[("a_oz", kpair, 0)])
                elif kpair >= 1:
                    t.wait_ge(*prod[("a_oz", kpair - 1, 1)])
                ins = None
                for ch in range(NCH):
                    ins = t.transpose(pvalT[:, ch * 128:(ch + 1) * 128],
                                      vals[:, hh * C + ch * 128: hh * C + (ch + 1) * 128], idb[:])
                ins.then_inc(sT.inc(), 1)
            blk.tensor(t_vT)
            prod[("a_vT", kpair, hh)] = (sT.h, sT.v)

            def v_vfm(v, hh=hh, kpair=kpair):
                v.wait_ge(*prod[("a_vT", kpair, hh)])
                if hh == 1:
                    v.wait_ge(*prod[("a_omm", kpair, 0)])
                elif kpair >= 1:
                    v.wait_ge(*prod[("a_omm", kpair - 1, 1)])
                v.tensor_copy(vfm[:], pvalT[:]).then_inc(sV.inc(), 1)
            blk.vector(v_vfm)
            prod[("a_ovfm", kpair, hh)] = (sV.h, sV.v)

            def t_o(t, hh=hh, kpair=kpair):
                t.wait_ge(sW.h, W_ALL)
                t.wait_ge(*prod[("a_ovfm", kpair, hh)])
                if hh == 0:
                    t.wait_ge(*prod[("a_norm", kpair, 3, 1)])
                ins = None
                for half in range(2):
                    for ch in range(NCH):
                        t.matmul(po[:, half * 512:(half + 1) * 512],
                                 vfm[:, ch * 128:(ch + 1) * 128],
                                 wo[:, ch * C + half * 512: ch * C + (half + 1) * 512],
                                 start=(ch == 0), stop=False)
                    ins = t.matmul(po[:, half * 512:(half + 1) * 512], ones[:],
                                   bor[0:1, half * 512:(half + 1) * 512], start=False, stop=True)
                ins.then_inc(sT.inc(), 1)
            blk.tensor(t_o)
            prod[("a_omm", kpair, hh)] = (sT.h, sT.v)

            def v_z(v, hh=hh, kpair=kpair, px=px):
                v.wait_ge(*prod[("a_omm", kpair, hh)])
                v.wait_ge(*prod[("a_x3", kpair, hh)])
                if hh == 1:
                    v.wait_ge(*prod[("a_lnout", kpair, 0)])
                elif kpair >= 1:
                    v.wait_ge(*prod[("a_lnout", kpair - 1, 1)])
                v.tensor_tensor(zt[:], po[:], xti[:, px * C:(px + 1) * C], ALU.add).then_inc(sV.inc(), 1)
            blk.vector(v_z)
            prod[("a_oz", kpair, hh)] = (sV.h, sV.v)

            ews = [prod[("a_oz", kpair, hh)]]
            if kpair >= 1 and hh == 0:
                ews.append(prod[("a_ydma", kpair - 1)])
            ln_emit(blk, sLN, zt[:], yst[:, hh * C:(hh + 1) * C], scr[:], st[:, 4:8], C,
                    prod, ("a_lnout", kpair, hh), extra_wait=ews)

        def sydma(sync, b=b, hp=hp, kpair=kpair):
            sync.wait_ge(*prod[("a_lnout", kpair, 0)])
            sync.wait_ge(*prod[("a_lnout", kpair, 1)])
            sync.dma_start(y_ap[b].rearrange("(n p) c -> p n c", p=128)[:, 2 * hp:2 * hp + 2],
                           yst[:].rearrange("p (n c) -> p n c", c=C)).then_inc(sO.dinc(), 16)
        blk.sync(sydma)
        prod[("a_ydma", kpair)] = (sO.h, sO.v)

    emit_ph1(0)
    for k in range(NPAIR):
        emit_ph2(k)
        if k + 1 < NPAIR:
            emit_ph1(k + 1)
        emit_ph3(k)

    def sfin(sync):
        sync.wait_ge(sO.h, sO.v)
    blk.sync(sfin)
    return [v for v in locals().values() if isinstance(v, Sem)]


def build_l0moe(nc, es, blk, prod, B, pref="m_", x_ap=None, y_ap=None, ses=None):
    ses = ses or es
    """L0 MoE: y = LN(2u + routed + shared(+biases)). u = x_ap [B, 2048, 1024] fp32.
    Host inputs: m_WrT [16, 1024, 1024] f32 (each Wr[e].T), m_WsT [2, 1024, 1024] f32,
    m_br [16, 1024] f32, m_bs [1, 2048] f32 (two rows concat), m_centT [1024, 16] f32,
    m_eb [1, 16] f32, m_ident [128,128] bf16, m_onesb [1,128] bf16, m_onesf [1,128] f32."""
    C = 1024
    NT = 16
    T = NT * 128
    NE = 16
    NCH = 8
    wr_d = nc.dram_tensor(pref + "WrT", [NE, C, C], F32, kind="ExternalInput").ap()
    ws_d = nc.dram_tensor(pref + "WsT", [2, C, C], F32, kind="ExternalInput").ap()
    br_d = nc.dram_tensor(pref + "br", [NE, C], F32, kind="ExternalInput").ap()
    bs_d = nc.dram_tensor(pref + "bs", [1, 2 * C], F32, kind="ExternalInput").ap()
    ct_d = nc.dram_tensor(pref + "centT", [C, NE], F32, kind="ExternalInput").ap()
    eb_d = nc.dram_tensor(pref + "eb", [1, NE], F32, kind="ExternalInput").ap()
    id_d = nc.dram_tensor(pref + "ident", [128, 128], BF16, kind="ExternalInput").ap()
    ob_d = nc.dram_tensor(pref + "onesb", [1, 128], BF16, kind="ExternalInput").ap()
    of_d = nc.dram_tensor(pref + "onesf", [1, 128], F32, kind="ExternalInput").ap()

    E = es.enter_context
    ut = E(nc.sbuf_tensor(pref + "ut", [128, NT * C], F32))
    ufm = E(nc.sbuf_tensor(pref + "ufm", [128, NT * C], BF16))
    zbf = E(nc.sbuf_tensor(pref + "zbf", [128, NT * C], BF16))
    wre = E(nc.sbuf_tensor(pref + "wre", [128, 2 * NCH * C], BF16))
    ubf = E(nc.sbuf_tensor(pref + "ubf", [128, C], BF16))
    uf32 = E(nc.sbuf_tensor(pref + "uf32", [128, C], F32))
    gal = E(nc.sbuf_tensor(pref + "gal", [128, NT * NE], F32))
    ct = E(nc.sbuf_tensor(pref + "ct", [128, NCH * NE], F32))
    ebr = E(nc.sbuf_tensor(pref + "ebr", [1, NE], F32))
    brp = E(nc.sbuf_tensor(pref + "brp", [1, 2 * C], BF16))
    bsr = E(nc.sbuf_tensor(pref + "bsr", [1, 2 * C], BF16))
    idb = E(nc.sbuf_tensor(pref + "idb", [128, 128], BF16))
    idf = E(nc.sbuf_tensor(pref + "idf", [128, 128], F32))
    onb = E(nc.sbuf_tensor(pref + "onb", [1, 128], BF16))
    onf = E(nc.sbuf_tensor(pref + "onf", [1, 128], F32))
    stx = E(nc.sbuf_tensor(pref + "stx", [128, 16], F32))
    zt = E(nc.sbuf_tensor(pref + "zt", [128, C], F32))
    scr = E(nc.sbuf_tensor(pref + "scr", [128, C], F32))
    yst = E(nc.sbuf_tensor(pref + "yst", [128, 2 * C], F32))

    pool = E(nc.psum_tensor(pref + "pool", [128, 4096], F32))
    poolb = pool[:].bitcast(BF16)
    pTf = pool[:, 0:1024]
    pTb = poolb[:, 2048:3072]
    pg = pool[:, 1536:1552]
    pe = [pool[:, 2048 + i * 1024: 2048 + (i + 1) * 1024] for i in range(2)]

    sW0 = Sem(ses, nc, pref + "sW0")
    sW1 = Sem(ses, nc, pref + "sW1")
    sWs = [sW0, sW1]
    sB = Sem(ses, nc, pref + "sB")
    sC = Sem(ses, nc, pref + "sC")
    sX = Sem(ses, nc, pref + "sX")
    sT = Sem(ses, nc, pref + "sT")
    sV = Sem(ses, nc, pref + "sV")
    sS = Sem(ses, nc, pref + "sS")
    sAO0 = Sem(ses, nc, pref + "sAO0")
    sAO1 = Sem(ses, nc, pref + "sAO1")
    sAOp = [sAO0, sAO1]
    sAOr = Sem(ses, nc, pref + "sAOr")
    sO = Sem(ses, nc, pref + "sO")
    sLN = Sem(ses, nc, pref + "sLN")

    def s0(sync):
        sync.dma_start(ct[:].rearrange("p (ch e) -> p ch e", e=NE),
                       ct_d.rearrange("(ch p) e -> p ch e", p=128)).then_inc(sC.dinc(), 16)
        sync.dma_start(ebr[:], eb_d).then_inc(sC.dinc(), 16)
        sync.dma_start(idb[:], id_d).then_inc(sC.dinc(), 16)
        sync.dma_start(onb[:], ob_d).then_inc(sC.dinc(), 16)
        sync.dma_start(onf[:], of_d).then_inc(sC.dinc(), 16)
    blk.sync(s0)
    CIN = sC.v

    def g0(g):
        g.dma_start(bsr[:], bs_d).then_inc(sB.dinc(), 16)
    blk.gpsimd(g0)

    def v0(v):
        v.wait_ge(sC.h, CIN)
        v.tensor_copy(idf[:], idb[:]).then_inc(sV.inc(), 1)
    blk.vector(v0)
    prod["m_init"] = (sV.h, sV.v)

    for b in range(B):
        def gw(g, b=b):
            g.wait_ge(sC.h, CIN)
            if b > 0:
                g.wait_ge(*prod[("m_emm", b - 1, NE - 1, NT - 1)])
                g.wait_ge(*prod[("m_eev", b - 1, NE - 1, NT - 1)])
            g.dma_start(wre[:, 0:NCH * C].rearrange("p (ch d) -> p ch d", d=C),
                        ws_d[0].rearrange("(ch p) d -> p ch d", p=128)).then_inc(sW0.dinc(), 16)
            g.dma_start(wre[:, NCH * C:].rearrange("p (ch d) -> p ch d", d=C),
                        ws_d[1].rearrange("(ch p) d -> p ch d", p=128)).then_inc(sW1.dinc(), 16)
        blk.gpsimd(gw)
        WS_IN = (sW0.v, sW1.v)
        def sxb(sync, b=b):
            if b > 0:
                sync.wait_ge(*prod[("m_sev", b - 1, NT - 1)])
            sync.dma_start(ut[:].rearrange("p (n c) -> p n c", c=C),
                           x_ap[b].rearrange("(n p) c -> p n c", p=128)).then_inc(sX.dinc(), 16)
        blk.sync(sxb)
        prod[("m_x", b)] = (sX.h, sX.v)

        for i in range(NT):
            def vc(v, b=b, i=i):
                v.wait_ge(*prod[("m_x", b)])
                if b > 0 and i == 0:
                    v.wait_ge(*prod[("m_emm", b - 1, NE - 1, NT - 1)])
                v.tensor_copy(ubf[:], ut[:, i * C:(i + 1) * C]).then_inc(sV.inc(), 1)
            blk.vector(vc)
            prod[("m_cast", b, i)] = (sV.h, sV.v)

            def t_bT(t, b=b, i=i):
                t.wait_ge(sC.h, CIN)
                t.wait_ge(*prod[("m_cast", b, i)])
                if i >= 1:
                    t.wait_ge(*prod[("m_evb", b, i - 1)])
                elif b > 0:
                    t.wait_ge(*prod[("m_evb", b - 1, NT - 1)])
                ins = None
                for ch in range(NCH):
                    ins = t.transpose(pTb[:, ch * 128:(ch + 1) * 128],
                                      ubf[:, ch * 128:(ch + 1) * 128], idb[:])
                ins.then_inc(sT.inc(), 1)
            blk.tensor(t_bT)
            prod[("m_bT", b, i)] = (sT.h, sT.v)

            def vevb(v, b=b, i=i):
                v.wait_ge(*prod[("m_bT", b, i)])
                v.tensor_copy(ufm[:, i * C:(i + 1) * C], pTb[:]).then_inc(sV.inc(), 1)
            blk.vector(vevb)
            prod[("m_evb", b, i)] = (sV.h, sV.v)

            def t_fT(t, b=b, i=i):
                t.wait_ge(*prod["m_init"])
                t.wait_ge(*prod[("m_x", b)])
                if i >= 1:
                    t.wait_ge(*prod[("m_evf", b, i - 1)])
                elif b > 0:
                    t.wait_ge(*prod[("m_evf", b - 1, NT - 1)])
                ins = None
                for ch in range(NCH):
                    ins = t.transpose(pTf[:, ch * 128:(ch + 1) * 128],
                                      ut[:, i * C + ch * 128: i * C + (ch + 1) * 128], idf[:])
                ins.then_inc(sT.inc(), 1)
            blk.tensor(t_fT)
            prod[("m_fT", b, i)] = (sT.h, sT.v)

            def vevf(v, b=b, i=i):
                v.wait_ge(*prod[("m_fT", b, i)])
                if i >= 1:
                    v.wait_ge(*prod[("m_gmm", b, i - 1)])
                v.tensor_copy(uf32[:], pTf[:]).then_inc(sV.inc(), 1)
            blk.vector(vevf)
            prod[("m_evf", b, i)] = (sV.h, sV.v)

            def t_g(t, b=b, i=i):
                t.wait_ge(sC.h, CIN)
                t.wait_ge(*prod[("m_evf", b, i)])
                if i >= 1:
                    t.wait_ge(*prod[("m_sm", b, i - 1)])
                elif b > 0:
                    t.wait_ge(*prod[("m_sm", b - 1, NT - 1)])
                for ch in range(NCH):
                    t.matmul(pg[:], uf32[:, ch * 128:(ch + 1) * 128],
                             ct[:, ch * NE:(ch + 1) * NE], start=(ch == 0), stop=False)
                t.matmul(pg[:], onf[:], ebr[:], start=False, stop=True).then_inc(sT.inc(), 1)
            blk.tensor(t_g)
            prod[("m_gmm", b, i)] = (sT.h, sT.v)

            def v_sm1(v, b=b, i=i):
                v.wait_ge(*prod[("m_gmm", b, i)])
                v.reduce_max(stx[:, 0:1], pg[:], axis=AX, negate=True).then_inc(sV.inc(), 1)
            blk.vector(v_sm1)
            prod[("m_nm", b, i)] = (sV.h, sV.v)

            def s_sm(s, b=b, i=i):
                s.wait_ge(*prod[("m_nm", b, i)])
                if b > 0 and i == 0:
                    s.wait_ge(*prod[("m_eev", b - 1, NE - 1, NT - 1)])
                s.activation(gal[:, i * NE:(i + 1) * NE], pg[:], AF.Exp,
                             bias=stx[:, 0:1], accum_out=stx[:, 1:2]).then_inc(sS.inc(), 1)
            blk.scalar(s_sm)
            prod[("m_exp", b, i)] = (sS.h, sS.v)

            def v_sm2(v, b=b, i=i):
                g = gal[:, i * NE:(i + 1) * NE]
                v.wait_ge(*prod[("m_exp", b, i)])
                if b > 0 and i == 0:
                    v.wait_ge(*prod[("m_fln", b - 1, NT - 1)])
                v.reciprocal(stx[:, 2:3], stx[:, 1:2]).then_inc(sV.inc(), 1)
                v.wait_ge(sV.h, sV.v)
                v.tensor_scalar(g, g, stx[:, 2:3], None, ALU.mult).then_inc(sV.inc(), 1)
                v.wait_ge(sV.h, sV.v)
                v.reduce_max(stx[:, 3:4], g, axis=AX).then_inc(sV.inc(), 1)
                v.wait_ge(sV.h, sV.v)
                v.tensor_scalar(scr[:, 0:NE], g, stx[:, 3:4], None, ALU.is_equal).then_inc(sV.inc(), 1)
                v.wait_ge(sV.h, sV.v)
                v.scalar_tensor_tensor(scr[:, NE:2 * NE], scr[:, 0:NE], -1e9, g, ALU.mult, ALU.add).then_inc(sV.inc(), 1)
                v.wait_ge(sV.h, sV.v)
                v.reduce_max(stx[:, 4:5], scr[:, NE:2 * NE], axis=AX).then_inc(sV.inc(), 1)
                v.wait_ge(sV.h, sV.v)
                v.tensor_scalar(scr[:, 0:NE], g, stx[:, 4:5], None, ALU.is_ge).then_inc(sV.inc(), 1)
                v.wait_ge(sV.h, sV.v)
                v.tensor_tensor(g, g, scr[:, 0:NE], ALU.mult).then_inc(sV.inc(), 1)
            blk.vector(v_sm2)
            prod[("m_sm", b, i)] = (sV.h, sV.v)

        for i in range(NT):
            def t_s(t, b=b, i=i):
                t.wait_ge(sW0.h, WS_IN[0])
                t.wait_ge(sW1.h, WS_IN[1])
                t.wait_ge(sB.h, 16)
                t.wait_ge(*prod[("m_evb", b, i)])
                if i >= 2:
                    t.wait_ge(*prod[("m_sev", b, i - 2)])
                elif b > 0:
                    t.wait_ge(*prod[("m_fln", b - 1, NT - 1)])
                ins = None
                for half in range(2):
                    dst = pe[i % 2][:, half * 512:(half + 1) * 512]
                    for w in range(2):
                        for ch in range(NCH):
                            t.matmul(dst, ufm[:, i * C + ch * 128: i * C + (ch + 1) * 128],
                                     wre[:, w * NCH * C + ch * C + half * 512: w * NCH * C + ch * C + (half + 1) * 512],
                                     start=(w == 0 and ch == 0), stop=False)
                    t.matmul(dst, onb[:], bsr[0:1, half * 512:(half + 1) * 512], start=False, stop=False)
                    ins = t.matmul(dst, onb[:], bsr[0:1, C + half * 512:C + (half + 1) * 512], start=False, stop=True)
                ins.then_inc(sT.inc(), 1)
            blk.tensor(t_s)
            prod[("m_smm", b, i)] = (sT.h, sT.v)

            def v_sev(v, b=b, i=i):
                v.wait_ge(*prod[("m_smm", b, i)])
                if i >= 1:
                    v.wait_ge(*prod[("m_sev", b, i - 1)])
                elif b > 0:
                    v.wait_ge(*prod[("m_fln", b - 1, NT - 1)])
                v.scalar_tensor_tensor(zt[:], pe[i % 2][:], 0.5, ut[:, i * C:(i + 1) * C],
                                       ALU.mult, ALU.add).then_inc(sV.inc(), 1)
                v.wait_ge(sV.h, sV.v)
                v.tensor_tensor(zbf[:, i * C:(i + 1) * C], zt[:], ut[:, i * C:(i + 1) * C], ALU.add).then_inc(sV.inc(), 1)
            blk.vector(v_sev)
            prod[("m_sev", b, i)] = (sV.h, sV.v)

        for e in range(NE):
            sl = e % 2
            def g_we(g, b=b, e=e, sl=sl):
                if e >= 2 or b > 0:
                    pk = ("m_emm", b, e - 2, NT - 1) if e >= 2 else ("m_smm", b, NT - 1)
                    g.wait_ge(*prod[pk])
                    if e < 2 and b > 0:
                        pass
                if e < 2:
                    g.wait_ge(*prod[("m_smm", b, NT - 1)])
                g.dma_start(wre[:, sl * NCH * C:(sl + 1) * NCH * C].rearrange("p (ch d) -> p ch d", d=C),
                            wr_d[e].rearrange("(ch p) d -> p ch d", p=128)).then_inc(sWs[sl].dinc(), 16)
                g.dma_start(brp[0:1, sl * C:(sl + 1) * C], br_d[e:e + 1, :]).then_inc(sWs[sl].dinc(), 16)
            blk.gpsimd(g_we)
            prod[("m_we", b, e)] = (sWs[sl].h, sWs[sl].v)

            for i in range(NT):
                def t_e(t, b=b, e=e, i=i, sl=sl):
                    t.wait_ge(*prod[("m_we", b, e)])
                    if i >= 2:
                        t.wait_ge(*prod[("m_eev", b, e, i - 2)])
                    elif e >= 1:
                        t.wait_ge(*prod[("m_eev", b, e - 1, NT - 2 + i)])
                    else:
                        t.wait_ge(*prod[("m_sev", b, NT - 2 + i)])
                    ins = None
                    for half in range(2):
                        dst = pe[i % 2][:, half * 512:(half + 1) * 512]
                        for ch in range(NCH):
                            t.matmul(dst, ufm[:, i * C + ch * 128: i * C + (ch + 1) * 128],
                                     wre[:, sl * NCH * C + ch * C + half * 512: sl * NCH * C + ch * C + (half + 1) * 512],
                                     start=(ch == 0), stop=False)
                        ins = t.matmul(dst, onb[:], brp[0:1, sl * C + half * 512: sl * C + (half + 1) * 512],
                                       start=False, stop=True)
                    ins.then_inc(sT.inc(), 1)
                blk.tensor(t_e)
                prod[("m_emm", b, e, i)] = (sT.h, sT.v)

                def v_eev(v, b=b, e=e, i=i):
                    v.wait_ge(*prod[("m_emm", b, e, i)])
                    v.scalar_tensor_tensor(zbf[:, i * C:(i + 1) * C], pe[i % 2][:],
                                           gal[:, i * NE + e:i * NE + e + 1],
                                           zbf[:, i * C:(i + 1) * C], ALU.mult, ALU.add).then_inc(sV.inc(), 1)
                blk.vector(v_eev)
                prod[("m_eev", b, e, i)] = (sV.h, sV.v)

        for i in range(NT):
            def v_f(v, b=b, i=i):
                v.wait_ge(*prod[("m_eev", b, NE - 1, i)])
                if i >= 1:
                    v.wait_ge(*prod[("m_fln", b, i - 1)])
                elif b > 0:
                    v.wait_ge(*prod[("m_fln", b - 1, NT - 1)])
                v.tensor_copy(zt[:], zbf[:, i * C:(i + 1) * C]).then_inc(sV.inc(), 1)
            blk.vector(v_f)
            prod[("m_z", b, i)] = (sV.h, sV.v)

            ews = [prod[("m_z", b, i)]]
            if i >= 2:
                ews.append(prod[("m_ydma", b, i // 2 - 1)])
            elif b > 0:
                ews.append(prod[("m_ydma", b - 1, NT // 2 - 1)])
            ln_emit(blk, sLN, zt[:], yst[:, (i % 2) * C:(i % 2 + 1) * C], scr[:], stx[:, 8:12], C,
                    prod, ("m_fln", b, i), extra_wait=ews)

            if i % 2 == 1:
                def sydma(sync, b=b, i=i):
                    sync.wait_ge(*prod[("m_fln", b, i - 1)])
                    sync.wait_ge(*prod[("m_fln", b, i)])
                    sync.dma_start(y_ap[b].rearrange("(n p) c -> p n c", p=128)[:, i - 1:i + 1],
                                   yst[:].rearrange("p (n c) -> p n c", c=C)).then_inc(sO.dinc(), 16)
                blk.sync(sydma)
                prod[("m_ydma", b, i // 2)] = (sO.h, sO.v)

    def sfin(sync):
        sync.wait_ge(sO.h, sO.v)
    blk.sync(sfin)
    return [v for v in locals().values() if isinstance(v, Sem)]


def l0moe_host_inputs(centroids, expert_biases, Wr, br, Ws, bs, pref="m_"):
    import ml_dtypes
    C = 1024
    return {
        pref + "WrT": np.ascontiguousarray(np.transpose(Wr, (0, 2, 1))),
        pref + "WsT": np.ascontiguousarray(np.transpose(Ws, (0, 2, 1))),
        pref + "br": np.ascontiguousarray(br),
        pref + "bs": np.ascontiguousarray(bs.reshape(1, 2 * C)),
        pref + "centT": np.ascontiguousarray(centroids.T),
        pref + "eb": expert_biases.reshape(1, -1).astype(np.float32),
        pref + "ident": np.eye(128, dtype=ml_dtypes.bfloat16),
        pref + "onesb": np.ones((1, 128), dtype=ml_dtypes.bfloat16),
        pref + "onesf": np.ones((1, 128), dtype=np.float32),
    }


def build_l1attn(nc, es, blk, prod, B, pref="w_", x_ap=None, y_ap=None, ses=None):
    ses = ses or es
    """L1 windowed bond/damage attention + residual + postLN.
    Host inputs (see l1attn_host_inputs): w_wdisp [1024,256], w_bdisp2 [128,2],
    w_wval [1024,1024], w_bval [1,1024], w_wcproj [1024,1024], w_bcproj [1,1024],
    w_bd [128, 640] (5 blockdiag 16x16 mats: state, strain, state_i, state_j, dmg),
    w_bdout [128, 16] (bond-out | dmg-out), w_relT [16,32], w_wpos [16,16],
    w_rep16 [16,128], w_reps [128, 8] (bstate,bstr,bdmg,bbond,bdout cols),
    w_rc [1? bf16 [128, 2048] replicated 1/cnt, w_ident [128,128] bf16, w_ones [1,128] bf16.
    """
    C = 1024
    NT = 16
    T = NT * 128
    NH = 16
    NCH = 8
    PAD = 31
    TP = T + PAD
    J = 32

    wd_d = nc.dram_tensor(pref + "wdisp", [C, 256], F32, kind="ExternalInput").ap()
    bd2_d = nc.dram_tensor(pref + "bdisp2", [128, 2], F32, kind="ExternalInput").ap()
    wv_d = nc.dram_tensor(pref + "wval", [C, C], F32, kind="ExternalInput").ap()
    bv_d = nc.dram_tensor(pref + "bval", [1, C], F32, kind="ExternalInput").ap()
    wc_d = nc.dram_tensor(pref + "wcproj", [C, C], F32, kind="ExternalInput").ap()
    bc_d = nc.dram_tensor(pref + "bcproj", [1, C], F32, kind="ExternalInput").ap()
    bd_d = nc.dram_tensor(pref + "bd", [128, 640], F32, kind="ExternalInput").ap()
    bo_d = nc.dram_tensor(pref + "bdout", [128, 16], F32, kind="ExternalInput").ap()
    rt_d = nc.dram_tensor(pref + "relT", [16, 32], F32, kind="ExternalInput").ap()
    wp_d = nc.dram_tensor(pref + "wpos", [16, 16], F32, kind="ExternalInput").ap()
    rp_d = nc.dram_tensor(pref + "rep16", [16, 128], F32, kind="ExternalInput").ap()
    rs_d = nc.dram_tensor(pref + "reps", [128, 8], F32, kind="ExternalInput").ap()
    rc_d = nc.dram_tensor(pref + "rc", [128, 32], F32, kind="ExternalInput").ap()
    id_d = nc.dram_tensor(pref + "ident", [128, 128], BF16, kind="ExternalInput").ap()
    on_d = nc.dram_tensor(pref + "ones", [1, 128], BF16, kind="ExternalInput").ap()
    skew_d = nc.dram_tensor(pref + "skew", [2, 128, 2048], BF16, kind="Internal").ap()
    ao_d = nc.dram_tensor(pref + "aod", [B, T, C], BF16, kind="Internal").ap()

    E = es.enter_context
    wdsp = E(nc.sbuf_tensor(pref + "wdsp", [128, NCH * 256], BF16))
    wval = E(nc.sbuf_tensor(pref + "wval_s", [128, NCH * C], BF16))
    wcpj = E(nc.sbuf_tensor(pref + "wcpj", [128, NCH * C], BF16))
    bd2 = E(nc.sbuf_tensor(pref + "bd2", [128, 2], F32))
    bvr = E(nc.sbuf_tensor(pref + "bvr", [1, C], BF16))
    bcr = E(nc.sbuf_tensor(pref + "bcr", [1, C], BF16))
    wbd = E(nc.sbuf_tensor(pref + "wbd", [128, 640], BF16))
    wbo = E(nc.sbuf_tensor(pref + "wbo", [128, 16], BF16))
    relT = E(nc.sbuf_tensor(pref + "relT_s", [16, 32], BF16))
    wpos = E(nc.sbuf_tensor(pref + "wpos_s", [16, 16], BF16))
    rep16 = E(nc.sbuf_tensor(pref + "rep16_s", [16, 128], BF16))
    reps = E(nc.sbuf_tensor(pref + "reps_s", [128, 8], F32))
    rc = E(nc.sbuf_tensor(pref + "rc_s", [128, 32], BF16))
    idb = E(nc.sbuf_tensor(pref + "idb", [128, 128], BF16))
    ones = E(nc.sbuf_tensor(pref + "ones_s", [1, 128], BF16))
    gsm = E(nc.sbuf_tensor(pref + "gsm", [16, 32], BF16))
    G = E(nc.sbuf_tensor(pref + "G", [128, 32], F32))
    negGx = E(nc.sbuf_tensor(pref + "negGx", [128, 4096], BF16))
    xti = E(nc.sbuf_tensor(pref + "xti", [128, 2 * C], F32))
    xbf = E(nc.sbuf_tensor(pref + "xbf", [128, 2 * C], BF16))
    xfm = E(nc.sbuf_tensor(pref + "xfm", [128, 2 * C], BF16))
    vsb = E(nc.sbuf_tensor(pref + "vsb", [128, NT * NH * 65], BF16))
    aow = E(nc.sbuf_tensor(pref + "aow", [128, 2 * 512], BF16))
    aot = E(nc.sbuf_tensor(pref + "aot", [128, 2 * C], BF16))
    dsp = E(nc.sbuf_tensor(pref + "dsp", [128, TP], BF16))
    Ast = E(nc.sbuf_tensor(pref + "Ast", [128, TP], BF16))
    A1 = E(nc.sbuf_tensor(pref + "A1", [128, TP], BF16))
    Adm = E(nc.sbuf_tensor(pref + "Adm", [128, TP], BF16))
    stt_ = None
    Ebuf = E(nc.sbuf_tensor(pref + "Ebuf", [128, TP], BF16))
    Fbuf = E(nc.sbuf_tensor(pref + "Fbuf", [128, T], BF16))
    Sa = E(nc.sbuf_tensor(pref + "Sa", [128, TP], BF16))
    Sb = Ebuf
    grid = E(nc.sbuf_tensor(pref + "grid", [128, 2 * 8192], BF16))
    BLs = E(nc.sbuf_tensor(pref + "BLs", [128, 8 * 256], BF16))
    DLs = E(nc.sbuf_tensor(pref + "DLs", [128, 8 * 256], BF16))
    ptk = E(nc.sbuf_tensor(pref + "ptk", [128, 2 * 256], BF16))
    p2d = E(nc.sbuf_tensor(pref + "p2d", [128, 2 * 2048], BF16))
    Pt = E(nc.sbuf_tensor(pref + "Pt", [128, 2 * 2048], BF16))
    zt = E(nc.sbuf_tensor(pref + "zt", [128, C], F32))
    scr = E(nc.sbuf_tensor(pref + "scr", [128, C], F32))
    stx = E(nc.sbuf_tensor(pref + "stx", [128, 8], F32))
    yst = E(nc.sbuf_tensor(pref + "yst", [128, C], F32))

    pool = E(nc.psum_tensor(pref + "pool", [128, 4096], F32))
    poolb = pool[:].bitcast(BF16)
    pT = poolb[:, 0:1024]
    pP = [poolb[:, 0:1024], poolb[:, 1024:2048]]
    pj = pool[:, 512:1536]
    pRedB = pool[:, 1024:1280]
    pRedD = pool[:, 1536:1792]
    pBD = pool[:, 3072:4096]
    pO = [pool[0:128, 2048 + i * 512: 2048 + i * 512 + 65] for i in range(2)]
    pGm = pool[0:16, 2048:2080]
    pGr = pool[:, 2560:2592]

    sW = Sem(ses, nc, pref + "sW")
    sC = Sem(ses, nc, pref + "sC")
    sX0 = Sem(ses, nc, pref + "sX0")
    sX1 = Sem(ses, nc, pref + "sX1")
    sXp = [sX0, sX1]
    sT = Sem(ses, nc, pref + "sT")
    sV = Sem(ses, nc, pref + "sV")
    sS = Sem(ses, nc, pref + "sS")
    sG = Sem(ses, nc, pref + "sG")
    sD0 = Sem(ses, nc, pref + "sD0")
    sD1 = Sem(ses, nc, pref + "sD1")
    sDp = [sD0, sD1]
    sAO0 = Sem(ses, nc, pref + "sAO0")
    sAO1 = Sem(ses, nc, pref + "sAO1")
    sAOp = [sAO0, sAO1]
    sAOr = Sem(ses, nc, pref + "sAOr")
    sO = Sem(ses, nc, pref + "sO")
    sLN = Sem(ses, nc, pref + "sLN")
    sXK = [Sem(ses, nc, pref + "sXK%d" % q) for q in range(4)]
    sAK = [Sem(ses, nc, pref + "sAK%d" % q) for q in range(4)]

    def g0(g):
        g.dma_start(wdsp[:].rearrange("p (ch d) -> p ch d", d=256),
                    wd_d.rearrange("(ch p) d -> p ch d", p=128)).then_inc(sW.dinc(), 16)
        g.dma_start(wval[:].rearrange("p (ch d) -> p ch d", d=C),
                    wv_d.rearrange("(ch p) d -> p ch d", p=128)).then_inc(sW.dinc(), 16)
        g.dma_start(wcpj[:].rearrange("p (ch d) -> p ch d", d=C),
                    wc_d.rearrange("(ch p) d -> p ch d", p=128)).then_inc(sW.dinc(), 16)
        g.dma_start(bvr[:], bv_d).then_inc(sW.dinc(), 16)
        g.dma_start(bcr[:], bc_d).then_inc(sW.dinc(), 16)
        g.dma_start(wbd[:], bd_d).then_inc(sW.dinc(), 16)
        g.dma_start(wbo[:], bo_d).then_inc(sW.dinc(), 16)
        g.dma_start(relT[:], rt_d).then_inc(sW.dinc(), 16)
        g.dma_start(wpos[:], wp_d).then_inc(sW.dinc(), 16)
        g.dma_start(rep16[:], rp_d).then_inc(sW.dinc(), 16)
        g.dma_start(rc[:], rc_d).then_inc(sW.dinc(), 16)
    blk.gpsimd(g0)
    W_ALL = sW.v

    def s0(sync):
        sync.dma_start(bd2[:], bd2_d).then_inc(sC.dinc(), 16)
        sync.dma_start(reps[:], rs_d).then_inc(sC.dinc(), 16)
        sync.dma_start(idb[:], id_d).then_inc(sC.dinc(), 16)
        sync.dma_start(ones[:], on_d).then_inc(sC.dinc(), 16)
    blk.sync(s0)
    CIN = sC.v

    def v_init(v):
        v.wait_ge(sC.h, CIN)
        v.memset(grid[:, 0:2048], 0.0)
        v.memset(vsb[:].rearrange("p (n h e) -> p n h e", h=NH, e=65)[:, :, :, 64:65], 1.0)
        v.memset(dsp[:, 0:PAD], 0.0)
        v.memset(Ast[:, 0:PAD], 0.0)
        v.memset(A1[:, 0:PAD], 0.0)
        v.memset(Adm[:, 0:PAD], 0.0)
        v.memset(Ebuf[:, 0:PAD], 0.0)
        v.memset(Sa[:, 0:PAD], 0.0).then_inc(sV.inc(), 1)
    blk.vector(v_init)
    prod["w_init"] = (sV.h, sV.v)

    def s_zskew(sync):
        sync.wait_ge(*prod["w_init"])
        sync.dma_start(skew_d[0], grid[:, 0:2048]).then_inc(sD0.dinc(), 16)
        sync.dma_start(skew_d[1], grid[:, 0:2048]).then_inc(sD1.dinc(), 16)
    blk.sync(s_zskew)
    prod["w_zskew"] = None

    def t_G(t):
        t.wait_ge(sW.h, W_ALL)
        t.matmul(pGm[:], wpos[:], relT[:], start=True, stop=True).then_inc(sT.inc(), 1)
    blk.tensor(t_G)
    prod["w_G1"] = (sT.h, sT.v)

    def v_G1(v):
        v.wait_ge(*prod[("w_G1")])
        v.tensor_copy(gsm[:], pGm[:]).then_inc(sV.inc(), 1)
    blk.vector(v_G1)
    prod["w_G2"] = (sV.h, sV.v)

    def t_G2(t):
        t.wait_ge(*prod["w_G2"])
        t.matmul(pGr[:], rep16[:], gsm[:], start=True, stop=True).then_inc(sT.inc(), 1)
    blk.tensor(t_G2)
    prod["w_G3"] = (sT.h, sT.v)

    def v_G2(v):
        v.wait_ge(*prod["w_G3"])
        v.tensor_scalar(G[:], pGr[:], reps[:, 1:2], None, ALU.add).then_inc(sV.inc(), 1)
    blk.vector(v_G2)
    prod["w_G"] = (sV.h, sV.v)

    def v_Gx(v):
        v.wait_ge(*prod["w_G"])
        Gb = G[:].unsqueeze(2).broadcast_to([128, 32, 128])
        v.tensor_scalar(negGx[:].rearrange("p (j t) -> p j t", t=128),
                        Gb, -1.0, None, ALU.mult).then_inc(sV.inc(), 1)
    blk.vector(v_Gx)
    prod["w_Gx"] = (sV.h, sV.v)

    def v_reps2(v):
        v.wait_ge(sC.h, CIN)
        v.tensor_scalar(reps[:, 5:6], reps[:, 4:5], 0.5, None, ALU.mult)
        v.tensor_scalar(reps[:, 6:7], reps[:, 3:4], -5.0, None, ALU.add).then_inc(sV.inc(), 1)
    blk.vector(v_reps2)
    prod["w_reps2"] = (sV.h, sV.v)

    for b in range(B):
        if b > 0:
            def v_ones(v):
                v.memset(vsb[:].rearrange("p (n h e) -> p n h e", h=NH, e=65)[:, :, :, 64:65],
                         1.0).then_inc(sV.inc(), 1)
            blk.vector(v_ones)

        for i in range(NT):
            k = b * NT + i
            pp = k % 2

            def s_x(sync, b=b, i=i, pp=pp, k=k):
                if i < 2 and b > 0:
                    sync.wait_ge(*prod[("w_z", b - 1, NT - 1)])
                if k >= 2:
                    sync.wait_ge(*prod[("w_xT", k - 2)])
                sync.dma_start(xti[:, pp * C:(pp + 1) * C],
                               x_ap[b].rearrange("(n p) c -> p n c", p=128)[:, i]).then_inc(sXp[pp].dinc(), 16)
            blk.sync(s_x)
            prod[("w_x", k)] = (sXp[pp].h, sXp[pp].v)

            def v_c(v, b=b, i=i, pp=pp, k=k):
                v.wait_ge(*prod[("w_x", k)])
                if k >= 2:
                    v.wait_ge(*prod[("w_xT", k - 2)])
                v.tensor_copy(xbf[:, pp * C:(pp + 1) * C], xti[:, pp * C:(pp + 1) * C]).then_inc(sV.inc(), 1)
            blk.vector(v_c)
            prod[("w_cast", k)] = (sV.h, sV.v)

            def t_xT(t, pp=pp, k=k, b=b):
                t.wait_ge(sC.h, CIN)
                t.wait_ge(*prod[("w_cast", k)])
                if k % NT == 0 and b > 0:
                    t.wait_ge(*prod[("w_aev", b - 1, NT - 1)])
                    t.wait_ge(*prod[("w_Ptev", 2 * b - 1, NT - 1)])
                if k >= 1:
                    t.wait_ge(*prod[("w_evx", k - 1)])
                ins = None
                for ch in range(NCH):
                    ins = t.transpose(pT[:, ch * 128:(ch + 1) * 128],
                                      xbf[:, pp * C + ch * 128: pp * C + (ch + 1) * 128], idb[:])
                ins.then_inc(sT.inc(), 1)
            blk.tensor(t_xT)
            prod[("w_xT", k)] = (sT.h, sT.v)

            def v_evx(v, pp=pp, k=k):
                v.wait_ge(*prod[("w_xT", k)])
                if k >= 2:
                    v.wait_ge(*prod[("w_vmm", k - 2)])
                v.tensor_copy(xfm[:, pp * C:(pp + 1) * C], pT[:]).then_inc(sV.inc(), 1)
            blk.vector(v_evx)
            prod[("w_evx", k)] = (sV.h, sV.v)

            def t_v(t, pp=pp, k=k, b=b, i=i):
                t.wait_ge(sW.h, W_ALL)
                t.wait_ge(*prod[("w_evx", k)])
                if i == 0 and b > 0:
                    t.wait_ge(*prod[("w_z", b - 1, NT - 1)])
                    t.wait_ge(*prod[("w_Ptev", 2 * b - 1, NT - 1)])
                if k >= 1:
                    t.wait_ge(*prod[("w_vev", k - 1)])
                ins = None
                for half in range(2):
                    dst = pj[:, half * 512:(half + 1) * 512]
                    for ch in range(NCH):
                        t.matmul(dst, xfm[:, pp * C + ch * 128: pp * C + (ch + 1) * 128],
                                 wval[:, ch * C + half * 512: ch * C + (half + 1) * 512],
                                 start=(ch == 0), stop=False)
                    ins = t.matmul(dst, ones[:], bvr[0:1, half * 512:(half + 1) * 512],
                                   start=False, stop=True)
                ins.then_inc(sT.inc(), 1)
            blk.tensor(t_v)
            prod[("w_vmm", k)] = (sT.h, sT.v)

            def v_vev(v, k=k, b=b, i=i):
                v.wait_ge(*prod[("w_vmm", k)])
                v.wait_ge(*prod["w_init"])
                if b > 0 and i == 0:
                    v.wait_ge(*prod[("w_bandmm", (b - 1) * 2 + 1, NT - 1, 7)])
                dst = vsb[:, i * NH * 65:(i + 1) * NH * 65].rearrange("p (h e) -> p h e", e=65)[:, :, 0:64]
                v.tensor_copy(dst, pj[:].rearrange("p (h e) -> p h e", e=64)).then_inc(sV.inc(), 1)
            blk.vector(v_vev)
            prod[("w_vev", k)] = (sV.h, sV.v)

        for grp in range(2):
            gk = b * 2 + grp
            for i in range(NT):
                k = b * NT + i

                def s_x2(sync, b=b, i=i, grp=grp, gk=gk):
                    pp2 = i % 2
                    if i < 2 and gk % 2 == 0:
                        sync.wait_ge(*prod[("w_cast", b * NT + NT - 1)])
                    if i >= 2:
                        sync.wait_ge(*prod[("w_xT2", gk, i - 2)])
                    elif gk > 0:
                        sync.wait_ge(*prod[("w_xT2", gk - 1, NT - 2 + i)])
                    sync.dma_start(xti[:, pp2 * C:(pp2 + 1) * C],
                                   x_ap[b].rearrange("(n p) c -> p n c", p=128)[:, i]).then_inc(sXp[pp2].dinc(), 16)
                blk.sync(s_x2)
                prod[("w_x2", gk, i)] = (sXp[i % 2].h, sXp[i % 2].v)

                def v_c2(v, i=i, gk=gk):
                    pp2 = i % 2
                    v.wait_ge(*prod[("w_x2", gk, i)])
                    if i >= 2:
                        v.wait_ge(*prod[("w_xT2", gk, i - 2)])
                    elif gk > 0:
                        v.wait_ge(*prod[("w_xT2", gk - 1, NT - 2 + i)])
                    v.tensor_copy(xbf[:, pp2 * C:(pp2 + 1) * C], xti[:, pp2 * C:(pp2 + 1) * C]).then_inc(sV.inc(), 1)
                blk.vector(v_c2)
                prod[("w_cast2", gk, i)] = (sV.h, sV.v)

                def t_xT2(t, i=i, gk=gk):
                    pp2 = i % 2
                    t.wait_ge(*prod[("w_cast2", gk, i)])
                    if i >= 1:
                        t.wait_ge(*prod[("w_evx2", gk, i - 1)])
                    elif gk > 0:
                        t.wait_ge(*prod[("w_evx2", gk - 1, NT - 1)])
                        t.wait_ge(*prod[("w_Ptev", gk - 1, NT - 1)])
                    ins = None
                    for ch in range(NCH):
                        ins = t.transpose(pT[:, ch * 128:(ch + 1) * 128],
                                          xbf[:, pp2 * C + ch * 128: pp2 * C + (ch + 1) * 128], idb[:])
                    ins.then_inc(sT.inc(), 1)
                blk.tensor(t_xT2)
                prod[("w_xT2", gk, i)] = (sT.h, sT.v)

                def v_evx2(v, i=i, gk=gk):
                    pp2 = i % 2
                    v.wait_ge(*prod[("w_xT2", gk, i)])
                    if i >= 2:
                        v.wait_ge(*prod[("w_dmm", gk, i - 2)])
                    elif gk > 0:
                        v.wait_ge(*prod[("w_dmm", gk - 1, NT - 2 + i)])
                    v.tensor_copy(xfm[:, pp2 * C:(pp2 + 1) * C], pT[:]).then_inc(sV.inc(), 1)
                blk.vector(v_evx2)
                prod[("w_evx2", gk, i)] = (sV.h, sV.v)

                def t_d(t, i=i, grp=grp, gk=gk):
                    pp2 = i % 2
                    t.wait_ge(sW.h, W_ALL)
                    t.wait_ge(*prod[("w_evx2", gk, i)])
                    if i >= 2:
                        t.wait_ge(*prod[("w_dev", gk, i - 2)])
                    elif gk > 0:
                        t.wait_ge(*prod[("w_dev", gk - 1, NT - 2 + i)])
                        t.wait_ge(*prod[("w_F", gk - 1)])
                    ins = None
                    for ch in range(NCH):
                        ins = t.matmul(pBD[:, (i % 2) * 512:(i % 2) * 512 + 128],
                                       wdsp[:, ch * 256 + grp * 128: ch * 256 + (grp + 1) * 128],
                                       xfm[:, pp2 * C + ch * 128: pp2 * C + (ch + 1) * 128],
                                       start=(ch == 0), stop=(ch == NCH - 1))
                    ins.then_inc(sT.inc(), 1)
                blk.tensor(t_d)
                prod[("w_dmm", gk, i)] = (sT.h, sT.v)

                def v_dev(v, i=i, grp=grp, gk=gk):
                    v.wait_ge(*prod[("w_dmm", gk, i)])
                    v.wait_ge(*prod["w_init"])
                    if gk > 0 and i == 0:
                        v.wait_ge(*prod[("w_bdm", gk - 1, 5)])
                    v.tensor_scalar(dsp[:, PAD + i * 128:PAD + (i + 1) * 128],
                                    pBD[:, (i % 2) * 512:(i % 2) * 512 + 128],
                                    bd2[:, grp:grp + 1], None, ALU.add).then_inc(sV.inc(), 1)
                blk.vector(v_dev)
                prod[("w_dev", gk, i)] = (sV.h, sV.v)
            prod[("w_disp", gk)] = prod[("w_dev", gk, NT - 1)]

            for bi, (wslice, dstbuf) in enumerate([(0, Ast), (1, A1), (4, Adm)]):
                for rnd in range(2):
                    def t_bd(t, bi=bi, wslice=wslice, gk=gk, rnd=rnd):
                        t.wait_ge(*prod[("w_disp", gk)])
                        if bi > 0 or rnd > 0:
                            t.wait_ge(*prod[("w_bdev", gk, bi * 2 + rnd - 1)])
                        ins = None
                        for q in range(2):
                            ins = t.matmul(pBD[:, q * 512:(q + 1) * 512],
                                           wbd[:, wslice * 128:(wslice + 1) * 128],
                                           dsp[:, PAD + (rnd * 2 + q) * 512:PAD + (rnd * 2 + q + 1) * 512],
                                           start=True, stop=True)
                        ins.then_inc(sT.inc(), 1)
                    blk.tensor(t_bd)
                    prod[("w_bdm", gk, bi * 2 + rnd)] = (sT.h, sT.v)

                    def v_bdev(v, bi=bi, dstbuf=dstbuf, gk=gk, rnd=rnd):
                        v.wait_ge(*prod[("w_bdm", gk, bi * 2 + rnd)])
                        if bi == 2 and rnd == 0 and gk > 0:
                            v.wait_ge(*prod[("w_asm", gk - 1, NT - 1, 1)])
                        v.tensor_copy(dstbuf[:, PAD + rnd * 1024:PAD + (rnd + 1) * 1024],
                                      pBD[:, 0:1024]).then_inc(sV.inc(), 1)
                    blk.vector(v_bdev)
                    prod[("w_bdev", gk, bi * 2 + rnd)] = (sV.h, sV.v)

            def v_slide(v, gk=gk):
                v.wait_ge(*prod[("w_bdev", gk, 1)])
                v.wait_ge(sW.h, W_ALL)
                v.tensor_tensor(Sa[:, PAD:PAD + T], Ast[:, PAD:PAD + T], Ast[:, PAD - 1:PAD - 1 + T], ALU.add).then_inc(sV.inc(), 1)
                v.wait_ge(sV.h, sV.v)
                v.tensor_tensor(Sb[:, PAD:PAD + T], Sa[:, PAD:PAD + T], Sa[:, PAD - 2:PAD - 2 + T], ALU.add).then_inc(sV.inc(), 1)
                v.wait_ge(sV.h, sV.v)
                v.tensor_tensor(Sa[:, PAD:PAD + T], Sb[:, PAD:PAD + T], Sb[:, PAD - 4:PAD - 4 + T], ALU.add).then_inc(sV.inc(), 1)
                v.wait_ge(sV.h, sV.v)
                v.tensor_tensor(Sb[:, PAD:PAD + T], Sa[:, PAD:PAD + T], Sa[:, PAD - 8:PAD - 8 + T], ALU.add).then_inc(sV.inc(), 1)
                v.wait_ge(sV.h, sV.v)
                v.tensor_tensor(Sa[:, PAD:PAD + T], Sb[:, PAD:PAD + T], Sb[:, PAD - 16:PAD - 16 + T], ALU.add).then_inc(sV.inc(), 1)
                v.wait_ge(sV.h, sV.v)
                v.tensor_tensor(Sb[:, PAD:PAD + 32], Sa[:, PAD:PAD + 32], rc[:], ALU.mult).then_inc(sV.inc(), 1)
                v.wait_ge(sV.h, sV.v)
                v.tensor_scalar(Sb[:, PAD + 32:PAD + T], Sa[:, PAD + 32:PAD + T], 1.0 / 32.0, None, ALU.mult).then_inc(sV.inc(), 1)
                v.wait_ge(sV.h, sV.v)
                v.scalar_tensor_tensor(Sa[:, PAD:PAD + T], Ast[:, PAD:PAD + T], -1.0, Sb[:, PAD:PAD + T], ALU.mult, ALU.add).then_inc(sV.inc(), 1)
                v.wait_ge(sV.h, sV.v)
                v.tensor_scalar(Sa[:, PAD:PAD + T], Sa[:, PAD:PAD + T], reps[:, 0:1], None, ALU.add).then_inc(sV.inc(), 1)
            blk.vector(v_slide)
            prod[("w_state", gk)] = (sV.h, sV.v)

            for rnd in range(2):
                def t_a3(t, gk=gk, rnd=rnd):
                    t.wait_ge(*prod[("w_state", gk)])
                    if rnd > 0:
                        t.wait_ge(*prod[("w_E", gk, 0)])
                    ins = None
                    for q in range(2):
                        ins = t.matmul(pBD[:, q * 512:(q + 1) * 512],
                                       wbd[:, 3 * 128:4 * 128],
                                       Sa[:, PAD + (rnd * 2 + q) * 512:PAD + (rnd * 2 + q + 1) * 512],
                                       start=True, stop=True)
                    ins.then_inc(sT.inc(), 1)
                blk.tensor(t_a3)
                prod[("w_a3m", gk, rnd)] = (sT.h, sT.v)

                def v_E(v, gk=gk, rnd=rnd):
                    v.wait_ge(*prod[("w_a3m", gk, rnd)])
                    if rnd == 0 and gk > 0:
                        v.wait_ge(*prod[("w_asm", gk - 1, NT - 1, 0)])
                    v.tensor_tensor(Ebuf[:, PAD + rnd * 1024:PAD + (rnd + 1) * 1024],
                                    A1[:, PAD + rnd * 1024:PAD + (rnd + 1) * 1024],
                                    pBD[:, 0:1024], ALU.add).then_inc(sV.inc(), 1)
                blk.vector(v_E)
                prod[("w_E", gk, rnd)] = (sV.h, sV.v)
            for rnd in range(2):
                def t_asi(t, gk=gk, rnd=rnd):
                    t.wait_ge(*prod[("w_E", gk, 1)])
                    if rnd > 0:
                        t.wait_ge(*prod[("w_F", gk, 0)])
                    ins = None
                    for q in range(2):
                        ins = t.matmul(pBD[:, q * 512:(q + 1) * 512],
                                       wbd[:, 2 * 128:3 * 128],
                                       Sa[:, PAD + (rnd * 2 + q) * 512:PAD + (rnd * 2 + q + 1) * 512],
                                       start=True, stop=True)
                    ins.then_inc(sT.inc(), 1)
                blk.tensor(t_asi)
                prod[("w_asim", gk, rnd)] = (sT.h, sT.v)

                def v_F(v, gk=gk, rnd=rnd):
                    v.wait_ge(*prod[("w_asim", gk, rnd)])
                    v.tensor_tensor(Fbuf[:, rnd * 1024:(rnd + 1) * 1024], pBD[:, 0:1024],
                                    A1[:, PAD + rnd * 1024:PAD + (rnd + 1) * 1024], ALU.subtract).then_inc(sV.inc(), 1)
                blk.vector(v_F)
                prod[("w_F", gk, rnd)] = (sV.h, sV.v)
            prod[("w_F", gk)] = prod[("w_F", gk, 1)]

            def emit_B(i, gk=gk, b=b, grp=grp):
                pp = i % 2

                def s_rb(sync, i=i, gk=gk, pp=pp):
                    sync.wait_ge(*prod[("w_skw", gk, i)])
                    if i >= 2:
                        sync.wait_ge(*prod[("w_PtT", gk, i - 2)])
                    elif gk > 0:
                        sync.wait_ge(*prod[("w_PtT", gk - 1, NT - 2 + i)])
                    sync.dma_start(p2d[:, pp * 2048:(pp + 1) * 2048], skew_d[pp]).then_inc(sDp[pp].dinc(), 16)
                blk.sync(s_rb)
                prod[("w_rb", gk, i)] = (sDp[pp].h, sDp[pp].v)

                def t_PtT(t, i=i, gk=gk, pp=pp, b=b):
                    t.wait_ge(*prod[("w_rb", gk, i)])
                    if i == 0:
                        t.wait_ge(*prod[("w_evx2", gk, NT - 1)])
                        t.wait_ge(*prod[("w_vev", b * NT + NT - 1)])
                    if i >= 1:
                        t.wait_ge(*prod[("w_Ptev", gk, i - 1)])
                    elif gk > 0:
                        t.wait_ge(*prod[("w_Ptev", gk - 1, NT - 1)])
                    ins = None
                    for h in range(8):
                        for ab in range(2):
                            srcv = p2d[:, pp * 2048:(pp + 1) * 2048].rearrange(
                                "p (tl h) -> p tl h", h=8)[:, ab * 128:(ab + 1) * 128, h]
                            idx = h * 2 + ab
                            ins = t.transpose(pP[idx // 8][:, (idx % 8) * 128:(idx % 8 + 1) * 128],
                                              srcv, idb[:])
                    ins.then_inc(sT.inc(), 1)
                blk.tensor(t_PtT)
                prod[("w_PtT", gk, i)] = (sT.h, sT.v)

                def v_Ptev(v, i=i, gk=gk, pp=pp):
                    v.wait_ge(*prod[("w_PtT", gk, i)])
                    if i >= 2:
                        v.wait_ge(*prod[("w_bandmm", gk, i - 2, 7)])
                    elif gk > 0:
                        v.wait_ge(*prod[("w_bandmm", gk - 1, NT - 2 + i, 7)])
                    v.tensor_copy(Pt[:, pp * 2048:pp * 2048 + 1024], pP[0][:]).then_inc(sV.inc(), 1)
                    v.tensor_copy(Pt[:, pp * 2048 + 1024:(pp + 1) * 2048], pP[1][:]).then_inc(sV.inc(), 1)
                blk.vector(v_Ptev)
                prod[("w_Ptev", gk, i)] = (sV.h, sV.v)

                for h in range(8):
                    hG = grp * 8 + h

                    def t_band(t, i=i, gk=gk, pp=pp, h=h, hG=hG, b=b):
                        if (gk, i, h) == (0, 0, 0):
                            t.wait_ge(*prod["w_G"])
                        t.wait_ge(*prod[("w_Ptev", gk, i)])
                        t.wait_ge(*prod[("w_vev", b * NT + i)])
                        if i > 0:
                            t.wait_ge(*prod[("w_vev", b * NT + i - 1)])
                        key = ("w_norm", gk, i, h - 2)
                        if h >= 2:
                            t.wait_ge(*prod[key])
                        elif i >= 1:
                            t.wait_ge(*prod[("w_norm", gk, i - 1, 6 + h)])
                        elif gk > 0:
                            t.wait_ge(*prod[("w_norm", gk - 1, NT - 1, 6 + h)])
                        PtA = Pt[:, pp * 2048 + h * 256: pp * 2048 + h * 256 + 128]
                        PtB = Pt[:, pp * 2048 + h * 256 + 128: pp * 2048 + h * 256 + 256]
                        ins = None
                        if i > 0:
                            vA = vsb[:, (i - 1) * NH * 65 + hG * 65:(i - 1) * NH * 65 + (hG + 1) * 65]
                            ins = t.matmul(pO[h % 2], PtA, vA, start=True, stop=False)
                        vB = vsb[:, i * NH * 65 + hG * 65:i * NH * 65 + (hG + 1) * 65]
                        ins = t.matmul(pO[h % 2], PtB, vB, start=(i == 0), stop=True)
                        ins.then_inc(sT.inc(), 1)
                    blk.tensor(t_band)
                    prod[("w_bandmm", gk, i, h)] = (sT.h, sT.v)

                    def v_norm(v, i=i, gk=gk, h=h, hG=hG, b=b):
                        v.wait_ge(*prod[("w_bandmm", gk, i, h)])
                        if h == 0:
                            if i >= 2:
                                v.wait_ge(*prod[("w_aodma", gk, i - 2)])
                            elif gk > 0:
                                v.wait_ge(*prod[("w_aodma", gk - 1, NT - 2 + i)])
                        v.reciprocal(stx[:, h % 2:h % 2 + 1], pO[h % 2][:, 64:65]).then_inc(sV.inc(), 1)
                        v.wait_ge(sV.h, sV.v)
                        v.tensor_scalar(aow[:, (i % 2) * 512 + h * 64:(i % 2) * 512 + (h + 1) * 64],
                                        pO[h % 2][:, 0:64], stx[:, h % 2:h % 2 + 1], None, ALU.mult).then_inc(sV.inc(), 1)
                    blk.vector(v_norm)
                    prod[("w_norm", gk, i, h)] = (sV.h, sV.v)

            def emit_B_aow(i, gk=gk, b=b, grp=grp):
                def s_aow(sync, i=i, gk=gk, grp=grp, b=b):
                    sync.wait_ge(*prod[("w_norm", gk, i, 7)])
                    sync.dma_start(ao_d[b].rearrange("(n p) c -> p n c", p=128)[:, i, grp * 512:(grp + 1) * 512],
                                   aow[:, (i % 2) * 512:(i % 2 + 1) * 512]).then_inc(sAOp[i % 2].dinc(), 16)
                blk.sync(s_aow)
                prod[("w_aodma", gk, i)] = (sAOp[i % 2].h, sAOp[i % 2].v)

            def emit_lev(j, gk=gk):
                sl2 = j % 8

                def v_levB(v, i=j, gk=gk, sl=sl2):
                    v.wait_ge(*prod[("w_redm", gk, i, 0)])
                    if i >= 8:
                        v.wait_ge(*prod[("w_exp", gk, i - 8)])
                    elif gk > 0:
                        v.wait_ge(*prod[("w_exp", gk - 1, i + 8)])
                    v.tensor_copy(BLs[:, sl * 256:(sl + 1) * 256], pRedB[:]).then_inc(sV.inc(), 1)
                blk.vector(v_levB)
                prod[("w_levB", gk, j)] = (sV.h, sV.v)

                def v_lg(v, i=j, gk=gk, sl=sl2):
                    v.wait_ge(*prod[("w_tanh", gk, i)])
                    v.wait_ge(*prod[("w_levB", gk, i)])
                    v.scalar_tensor_tensor(BLs[:, sl * 256:(sl + 1) * 256],
                                           DLs[:, sl * 256:(sl + 1) * 256], -5.0,
                                           BLs[:, sl * 256:(sl + 1) * 256],
                                           ALU.mult, ALU.add).then_inc(sV.inc(), 1)
                blk.vector(v_lg)
                prod[("w_lg", gk, j)] = (sV.h, sV.v)

            def emit_exp(j, gk=gk):
                sl2 = j % 8
                pp2 = j % 2

                def s_exp(s, i=j, gk=gk, pp=pp2, sl=sl2):
                    s.wait_ge(*prod[("w_lg", gk, i)])
                    if i >= 2:
                        s.wait_ge(*prod[("w_skw", gk, i - 2)])
                    elif gk > 0:
                        s.wait_ge(*prod[("w_skw", gk - 1, NT - 2 + i)])
                    s.activation(ptk[:, pp * 256:(pp + 1) * 256], BLs[:, sl * 256:(sl + 1) * 256], AF.Exp,
                                 bias=reps[:, 6:7]).then_inc(sS.inc(), 1)
                blk.scalar(s_exp)
                prod[("w_exp", gk, j)] = (sS.h, sS.v)

                if j == 0:
                    def v_mask(v, gk=gk):
                        v.wait_ge(*prod[("w_exp", gk, 0)])
                        ins = None
                        for jj in range(J - 1):
                            ins = v.memset(ptk[0:PAD - jj, jj * 8:(jj + 1) * 8], 0.0)
                        ins.then_inc(sV.inc(), 1)
                    blk.vector(v_mask)
                    prod[("w_msk", gk, 0)] = (sV.h, sV.v)
                else:
                    prod[("w_msk", gk, j)] = prod[("w_exp", gk, j)]

                def s_skw(sync, i=j, gk=gk, pp=pp2):
                    sync.wait_ge(*prod[("w_msk", gk, i)])
                    if gk == 0 and i < 2:
                        sync.wait_ge(sDp[i % 2].h, 16)
                    if i >= 2:
                        sync.wait_ge(*prod[("w_rb", gk, i - 2)])
                    elif gk > 0:
                        sync.wait_ge(*prod[("w_rb", gk - 1, NT - 2 + i)])
                    flat = skew_d[pp].rearrange("p n -> (p n)")
                    dst = bass.AP(flat.tensor, flat.offset + 776,
                                  [[2056, 128], [1, 256]])
                    sync.dma_start(dst, ptk[:, pp * 256:(pp + 1) * 256]).then_inc(sDp[pp].dinc(), 16)
                blk.sync(s_skw)
                prod[("w_skw", gk, j)] = (sDp[pp2].h, sDp[pp2].v)

            for i in range(NT):
                pp = i % 2
                sl = i % 8
                gb = pp * 8192

                def v_asm(v, i=i, gk=gk, gb=gb):
                    v.wait_ge(*prod[("w_F", gk)])
                    if gk == 0 and i == 0:
                        v.wait_ge(sD0.h, 16)
                        v.wait_ge(sD1.h, 16)
                    if i >= 2:
                        v.wait_ge(*prod[("w_redm", gk, i - 2, 1)])
                    elif gk > 0:
                        v.wait_ge(*prod[("w_redm", gk - 1, NT - 2 + i, 1)])
                    g3 = grid[:, gb:gb + 4096].rearrange("p (j t) -> p j t", t=128)
                    Ei = bass.AP(Ebuf[:].tensor, Ebuf[:].offset + i * 128, [[TP, 128], [1, J], [1, 128]])
                    Fb = Fbuf[:, i * 128:(i + 1) * 128].unsqueeze(1).broadcast_to([128, J, 128])
                    v.tensor_tensor(g3, Ei, Fb, ALU.add).then_inc(sV.inc(), 1)
                    v.wait_ge(sV.h, sV.v)
                    if gk == 0 and i == 0:
                        v.wait_ge(*prod["w_Gx"])
                    gB = grid[:, gb:gb + 4096]
                    v.tensor_tensor(gB, gB, negGx[:], ALU.subtract).then_inc(sV.inc(), 1)
                    v.wait_ge(sV.h, sV.v)
                    g3d = grid[:, gb + 4096:gb + 8192].rearrange("p (j t) -> p j t", t=128)
                    Ai2 = bass.AP(Adm[:].tensor, Adm[:].offset + i * 128, [[TP, 128], [1, J], [1, 128]])
                    Ab2 = Adm[:, PAD + i * 128:PAD + (i + 1) * 128].unsqueeze(1).broadcast_to([128, J, 128])
                    v.tensor_tensor(g3d, Ai2, Ab2, ALU.subtract).then_inc(sV.inc(), 1)
                blk.vector(v_asm)
                prod[("w_asm", gk, i, 0)] = (sV.h, sV.v - 1)
                prod[("w_asm", gk, i, 1)] = (sV.h, sV.v)

                if i >= 1:
                    emit_lev(i - 1)
                if i >= 2:
                    emit_B(i - 2)
                if i >= 3:
                    emit_B_aow(i - 3)

                for gi in range(2):
                    def s_gelu(s, i=i, gk=gk, gi=gi):
                        s.wait_ge(*prod[("w_asm", gk, i, gi)])
                        gb = (i % 2) * 8192
                        s.activation(grid[:, gb + gi * 4096:gb + (gi + 1) * 4096],
                                     grid[:, gb + gi * 4096:gb + (gi + 1) * 4096], AF.Gelu,
                                     bias=(0.0 if gi == 0 else reps[:, 2:3])).then_inc(sS.inc(), 1)
                    blk.scalar(s_gelu)
                    prod[("w_gelu", gk, i, gi)] = (sS.h, sS.v)

                    def t_red(t, i=i, gk=gk, gi=gi, b=b):
                        t.wait_ge(*prod[("w_gelu", gk, i, gi)])
                        if gi == 0 and i == 0 and gk % 2 == 0:
                            t.wait_ge(*prod[("w_vev", b * NT + NT - 1)])
                        if gi == 0:
                            if i >= 1:
                                t.wait_ge(*prod[("w_levB", gk, i - 1)])
                                t.wait_ge(*prod[("w_tanh", gk, i - 1)])
                            elif gk > 0:
                                t.wait_ge(*prod[("w_levB", gk - 1, NT - 1)])
                                t.wait_ge(*prod[("w_tanh", gk - 1, NT - 1)])
                        ins = None
                        gb = (i % 2) * 8192
                        pR = pRedB if gi == 0 else pRedD
                        for j in range(J):
                            ins = t.matmul(pR[:, j * 8:(j + 1) * 8],
                                           grid[:, gb + gi * 4096 + j * 128: gb + gi * 4096 + (j + 1) * 128],
                                           wbo[:, gi * 8:(gi + 1) * 8], start=True, stop=True)
                        ins.then_inc(sT.inc(), 1)
                    blk.tensor(t_red)
                    prod[("w_redm", gk, i, gi)] = (sT.h, sT.v)

                def s_tanh(s, i=i, gk=gk, sl=sl):
                    s.wait_ge(*prod[("w_redm", gk, i, 1)])
                    s.wait_ge(*prod["w_reps2"])
                    if i >= 8:
                        s.wait_ge(*prod[("w_lg", gk, i - 8)])
                    elif gk > 0:
                        s.wait_ge(*prod[("w_lg", gk - 1, i + 8)])
                    s.activation(DLs[:, sl * 256:(sl + 1) * 256], pRedD[:], AF.Tanh,
                                 scale=0.5, bias=reps[:, 5:6]).then_inc(sS.inc(), 1)
                blk.scalar(s_tanh)
                prod[("w_tanh", gk, i)] = (sS.h, sS.v)

                if i >= 1:
                    emit_exp(i - 1)

            emit_lev(NT - 1)
            emit_exp(NT - 1)
            emit_B(NT - 2)
            emit_B_aow(NT - 3)
            emit_B(NT - 1)
            emit_B_aow(NT - 2)
            emit_B_aow(NT - 1)

        xtiK = vsb[:].bitcast(F32)
        aotK = Pt
        for i in range(NT):
            k = b * NT + i
            pp = i % 2
            xs = i % 4
            asl = i % 4

            def s_x3(sync, b=b, i=i, xs=xs):
                if i < 4:
                    sync.wait_ge(*prod[("w_bandmm", b * 2 + 1, NT - 1, 7)])
                else:
                    sync.wait_ge(*prod[("w_z", b, i - 4)])
                sync.dma_start(xtiK[:, xs * C:(xs + 1) * C],
                               x_ap[b].rearrange("(n p) c -> p n c", p=128)[:, i]).then_inc(sXK[xs].dinc(), 16)
            blk.sync(s_x3)
            prod[("w_x3", b, i)] = (sXK[xs].h, sXK[xs].v)

            def s_aor(sync, b=b, i=i, asl=asl):
                sync.wait_ge(sAO0.h, prod[("w_aodma", b * 2 + 1, NT - 2)][1])
                sync.wait_ge(sAO1.h, prod[("w_aodma", b * 2 + 1, NT - 1)][1])
                if i < 4:
                    sync.wait_ge(*prod[("w_bandmm", b * 2 + 1, NT - 1, 7)])
                else:
                    sync.wait_ge(*prod[("w_aoT", b, i - 4)])
                sync.dma_start(aotK[:, asl * C:(asl + 1) * C],
                               ao_d[b].rearrange("(n p) c -> p n c", p=128)[:, i]).then_inc(sAK[asl].dinc(), 16)
            blk.sync(s_aor)
            prod[("w_aor", b, i)] = (sAK[asl].h, sAK[asl].v)

            def t_aoT(t, b=b, i=i, asl=asl):
                t.wait_ge(*prod[("w_aor", b, i)])
                if i == 0:
                    t.wait_ge(*prod[("w_Ptev", b * 2 + 1, NT - 1)])
                if i >= 1:
                    t.wait_ge(*prod[("w_aev", b, i - 1)])
                elif b > 0:
                    t.wait_ge(*prod[("w_aev", b - 1, NT - 1)])
                ins = None
                for ch in range(NCH):
                    ins = t.transpose(pT[:, ch * 128:(ch + 1) * 128],
                                      aotK[:, asl * C + ch * 128: asl * C + (ch + 1) * 128], idb[:])
                ins.then_inc(sT.inc(), 1)
            blk.tensor(t_aoT)
            prod[("w_aoT", b, i)] = (sT.h, sT.v)

            def v_aev(v, b=b, i=i, pp=pp):
                v.wait_ge(*prod[("w_aoT", b, i)])
                if i >= 2:
                    v.wait_ge(*prod[("w_cmm", b, i - 2)])
                v.tensor_copy(xfm[:, pp * C:(pp + 1) * C], pT[:]).then_inc(sV.inc(), 1)
            blk.vector(v_aev)
            prod[("w_aev", b, i)] = (sV.h, sV.v)

            def t_c(t, b=b, i=i, pp=pp):
                t.wait_ge(*prod[("w_aev", b, i)])
                if i >= 1:
                    t.wait_ge(*prod[("w_z", b, i - 1)])
                elif b > 0:
                    t.wait_ge(*prod[("w_z", b - 1, NT - 1)])
                ins = None
                for half in range(2):
                    dst = pj[:, half * 512:(half + 1) * 512]
                    for ch in range(NCH):
                        t.matmul(dst, xfm[:, pp * C + ch * 128: pp * C + (ch + 1) * 128],
                                 wcpj[:, ch * C + half * 512: ch * C + (half + 1) * 512],
                                 start=(ch == 0), stop=False)
                    ins = t.matmul(dst, ones[:], bcr[0:1, half * 512:(half + 1) * 512],
                                   start=False, stop=True)
                ins.then_inc(sT.inc(), 1)
            blk.tensor(t_c)
            prod[("w_cmm", b, i)] = (sT.h, sT.v)

            def v_z(v, b=b, i=i, pp=pp, xs=xs):
                v.wait_ge(*prod[("w_cmm", b, i)])
                v.wait_ge(*prod[("w_x3", b, i)])
                if i >= 1:
                    v.wait_ge(*prod[("w_fln", b, i - 1)])
                elif b > 0:
                    v.wait_ge(*prod[("w_fln", b - 1, NT - 1)])
                v.tensor_tensor(zt[:], pj[:], xtiK[:, xs * C:(xs + 1) * C], ALU.add).then_inc(sV.inc(), 1)
            blk.vector(v_z)
            prod[("w_z", b, i)] = (sV.h, sV.v)

            ews = [prod[("w_z", b, i)]]
            if i >= 1:
                ews.append(prod[("w_ydma", b, i - 1)])
            elif b > 0:
                ews.append(prod[("w_ydma", b - 1, NT - 1)])
            ln_emit(blk, sLN, zt[:], yst[:], scr[:], stx[:, 4:8], C,
                    prod, ("w_fln", b, i), extra_wait=ews)

            def sydma(sync, b=b, i=i):
                sync.wait_ge(*prod[("w_fln", b, i)])
                sync.dma_start(y_ap[b].rearrange("(n p) c -> p n c", p=128)[:, i],
                               yst[:]).then_inc(sO.dinc(), 16)
            blk.sync(sydma)
            prod[("w_ydma", b, i)] = (sO.h, sO.v)

    def sfin(sync):
        sync.wait_ge(sO.h, sO.v)
    blk.sync(sfin)
    return [v for v in locals().values() if isinstance(v, Sem)]


def l1attn_host_inputs(w_disp, b_disp, w_val, b_val, rel_pos_emb, w_state, b_state,
                       w_strain, b_strain, w_state_i, w_state_j, w_pos, w_bond_out,
                       b_bond_out, w_damage, b_damage, w_damage_out, b_damage_out,
                       w_cproj, b_cproj, pref="w_"):
    import ml_dtypes
    C = 1024
    bd = np.zeros((128, 640), np.float32)
    for bi, w in enumerate([w_state, w_strain, w_state_i, w_state_j, w_damage]):
        for h in range(8):
            bd[h * 16:(h + 1) * 16, bi * 128 + h * 16:bi * 128 + (h + 1) * 16] = w
    bdout = np.zeros((128, 16), np.float32)
    for h in range(8):
        bdout[h * 16:(h + 1) * 16, h] = w_bond_out[:, 0]
        bdout[h * 16:(h + 1) * 16, 8 + h] = w_damage_out[:, 0]
    rep16 = np.zeros((16, 128), np.float32)
    for h in range(8):
        rep16[:, h * 16:(h + 1) * 16] = np.eye(16)
    reps = np.zeros((128, 8), np.float32)
    reps[:, 0] = np.tile(b_state, 8)
    reps[:, 1] = np.tile(b_strain, 8)
    reps[:, 2] = np.tile(b_damage, 8)
    reps[:, 3] = b_bond_out[0]
    reps[:, 4] = b_damage_out[0]
    cnt = np.minimum(np.arange(32) + 1, 32).astype(np.float32)
    rc = np.broadcast_to((1.0 / cnt)[None, :], (128, 32)).copy()
    bdisp2 = np.stack([b_disp[0:128], b_disp[128:256]], axis=1).astype(np.float32)
    return {
        pref + "wdisp": np.ascontiguousarray(w_disp),
        pref + "bdisp2": bdisp2,
        pref + "wval": np.ascontiguousarray(w_val),
        pref + "bval": b_val.reshape(1, C).astype(np.float32),
        pref + "wcproj": np.ascontiguousarray(w_cproj),
        pref + "bcproj": b_cproj.reshape(1, C).astype(np.float32),
        pref + "bd": bd,
        pref + "bdout": bdout,
        pref + "relT": np.ascontiguousarray(rel_pos_emb.T),
        pref + "wpos": np.ascontiguousarray(w_pos),
        pref + "rep16": rep16,
        pref + "reps": reps,
        pref + "rc": rc,
        pref + "ident": np.eye(128, dtype=ml_dtypes.bfloat16),
        pref + "ones": np.ones((1, 128), dtype=ml_dtypes.bfloat16),
    }


class _RecIns:
    __slots__ = ("thens",)

    def __init__(self):
        self.thens = []

    def then_inc(self, sem, n):
        self.thens.append((sem, n))
        return self


class _RecEng:
    def __init__(self):
        self.q = []

    def __getattr__(self, name):
        if name.startswith("_"):
            raise AttributeError(name)

        def call(*a, **k):
            ri = _RecIns()
            self.q.append((name, a, k, ri))
            return ri
        return call


class MergeBlk:
    ENGS = ("gpsimd", "sync", "tensor", "vector", "scalar")

    def __init__(self, blk):
        self._blk = blk
        self._recs = {n: _RecEng() for n in self.ENGS}
        for n in self.ENGS:
            setattr(self, n, (lambda rec: lambda fn: fn(rec))(self._recs[n]))

    def flush(self):
        for n in self.ENGS:
            rec = self._recs[n]
            if not rec.q:
                continue
            q = rec.q
            rec.q = []

            def run(eng, q=q):
                for (nm, a, k, ri) in q:
                    ins = getattr(eng, nm)(*a, **k)
                    for (s, cnt) in ri.thens:
                        ins.then_inc(s, cnt)
            getattr(self._blk, n)(run)


from concourse.bass_utils import run_bass_kernel_spmd

N_CORES = 8
B_CORE = 2
_LAYERS = [build_l0attn, build_l0moe, build_l1attn, build_l1ffn]


def _build_fused():
    nc = bass.Bass("TRN2", target_bir_lowering=False, debug=False)
    x_d = nc.dram_tensor("x_in", [B_CORE, 2048, 1024], F32, kind="ExternalInput").ap()
    y1_d = nc.dram_tensor("y1_s", [B_CORE, 2048, 1024], F32, kind="Internal").ap()
    y2_d = nc.dram_tensor("y2_s", [B_CORE, 2048, 1024], F32, kind="Internal").ap()
    y3_d = nc.dram_tensor("y3_s", [B_CORE, 2048, 1024], F32, kind="Internal").ap()
    out_d = nc.dram_tensor("y_out", [B_CORE, 2048, 1024], F32, kind="ExternalOutput").ap()
    es = ExitStack()
    rblk = es.enter_context(nc.Block())
    blk = MergeBlk(rblk)
    bar = Sem(es, nc, "bar")
    prod = {}
    chain = [(build_l0attn, x_d, y1_d), (build_l0moe, y1_d, y2_d),
             (build_l1attn, y2_d, y3_d), (build_l1ffn, y3_d, out_d)]
    for li, (builder, xa, ya) in enumerate(chain):
        if li > 0:
            for engsec in (blk.gpsimd, blk.sync, blk.tensor, blk.vector, blk.scalar):
                engsec(lambda e, li=li: e.wait_ge(bar.h, li))
        with ExitStack() as les:
            lsems = builder(nc, les, blk, prod, B_CORE, x_ap=xa, y_ap=ya, ses=es)
        def _barrier(s, lsems=lsems):
            for sm in lsems:
                s.wait_ge(sm.h, sm.v)
            s.sem_inc(bar.h, 1)
        bar.v += 1
        blk.sync(_barrier)
        blk.flush()
    es.close()
    return nc


def _host_all(ins):
    h = {}
    h.update(l0attn_host_inputs(ins["L0_attn_Wqkv"], ins["L0_attn_bqkv"],
                                ins["L0_attn_Wo"], ins["L0_attn_bo"]))
    h.update(l0moe_host_inputs(ins["L0_ffn_centroids"], ins["L0_ffn_expert_biases"],
                               ins["L0_ffn_Wr"], ins["L0_ffn_br"],
                               ins["L0_ffn_Ws"], ins["L0_ffn_bs"]))
    h.update(l1attn_host_inputs(ins["L1_attn_w_disp"], ins["L1_attn_b_disp"],
                                ins["L1_attn_w_val"], ins["L1_attn_b_val"],
                                ins["L1_attn_rel_pos_emb"], ins["L1_attn_w_state"],
                                ins["L1_attn_b_state"], ins["L1_attn_w_strain"],
                                ins["L1_attn_b_strain"], ins["L1_attn_w_state_i"],
                                ins["L1_attn_w_state_j"], ins["L1_attn_w_pos"],
                                ins["L1_attn_w_bond_out"], ins["L1_attn_b_bond_out"],
                                ins["L1_attn_w_damage"], ins["L1_attn_b_damage"],
                                ins["L1_attn_w_damage_out"], ins["L1_attn_b_damage_out"],
                                ins["L1_attn_w_cproj"], ins["L1_attn_b_cproj"]))
    h.update(l1ffn_host_inputs(ins["L1_ffn_W_r"], ins["L1_ffn_W_k"], ins["L1_ffn_W_v"],
                               ins["L1_ffn_mu_r"], ins["L1_ffn_mu_k"]))
    return h


def kernel(**ins):
    for k in list(ins):
        ins[k] = np.asarray(ins[k])
    x = np.asarray(ins["L0_attn_x"], dtype=np.float32)
    host = _host_all(ins)
    nc = _build_fused()
    shards = np.split(np.ascontiguousarray(x), N_CORES, axis=0)
    in_maps = [{"x_in": s, **host} for s in shards]
    res = run_bass_kernel_spmd(nc, in_maps, core_ids=list(range(N_CORES)))
    return np.concatenate([np.asarray(r["y_out"]) for r in res.results], axis=0)


# version 17
# speedup vs baseline: 1.1853x; 1.0031x over previous
import numpy as np
import concourse.bass as bass
import concourse.mybir as mybir
from contextlib import ExitStack

F32 = mybir.dt.float32
BF16 = mybir.dt.bfloat16
AF = mybir.ActivationFunctionType
ALU = mybir.AluOpType
AX = mybir.AxisListType.X

LN_EPS = 1e-5


class Sem:
    def __init__(self, es, nc, name):
        self.h = es.enter_context(nc.semaphore(name))
        self.v = 0

    def inc(self, n=1):
        self.v += n
        return self.h

    def dinc(self):
        self.v += 16
        return self.h

    @property
    def cur(self):
        return (self.h, self.v)


def ln_emit(blk, sv, z, y_out, scr, st, n_free, prod, key, extra_wait=None):
    def v1(v):
        for w in (extra_wait or []):
            v.wait_ge(*w)
        v.reduce_sum(st[:, 0:1], z[:], axis=AX).then_inc(sv.inc(), 1)
        v.wait_ge(sv.h, sv.v)
        v.tensor_scalar(st[:, 1:2], st[:, 0:1], 1.0 / n_free, None, ALU.mult).then_inc(sv.inc(), 1)
        v.wait_ge(sv.h, sv.v)
        v.tensor_scalar(z[:], z[:], st[:, 1:2], None, ALU.subtract).then_inc(sv.inc(), 1)
    blk.vector(v1)

    def s1(s):
        s.wait_ge(sv.h, sv.v)
        s.activation(scr[:], z[:], AF.Square, accum_out=st[:, 2:3]).then_inc(sv.inc(), 1)
    blk.scalar(s1)

    def v2(v):
        v.wait_ge(sv.h, sv.v)
        v.tensor_scalar(st[:, 3:4], st[:, 2:3], 1.0 / n_free, LN_EPS, ALU.mult, ALU.add).then_inc(sv.inc(), 1)
    blk.vector(v2)

    def s2(s):
        s.wait_ge(sv.h, sv.v)
        s.activation(st[:, 0:1], st[:, 3:4], AF.Ln).then_inc(sv.inc(), 1)
        s.wait_ge(sv.h, sv.v)
        s.activation(st[:, 1:2], st[:, 0:1], AF.Exp, scale=-0.5).then_inc(sv.inc(), 1)
    blk.scalar(s2)

    def v3(v):
        v.wait_ge(sv.h, sv.v)
        v.tensor_scalar(y_out[:], z[:], st[:, 1:2], None, ALU.mult).then_inc(sv.inc(), 1)
        prod[key] = (sv.h, sv.v)
    blk.vector(v3)


def build_l1ffn(nc, es, blk, prod, B, T=2048, pref="f_", x_ap=None, y_ap=None, first=True, ses=None):
    ses = ses or es
    """L1 ffn + residual + postLN: y = LN(x + ffn(x)). x_ap/y_ap: DRAM [B,T,1024] fp32."""
    C = 1024
    NT = T // 128
    NCH = C // 128
    wrt_d = nc.dram_tensor(pref + "WrT", [C, C], F32, kind="ExternalInput").ap()
    wkt_d = nc.dram_tensor(pref + "WkT", [C, C], F32, kind="ExternalInput").ap()
    wvt_d = nc.dram_tensor(pref + "WvT", [C, C], F32, kind="ExternalInput").ap()
    mur_d = nc.dram_tensor(pref + "mur", [128, NCH], F32, kind="ExternalInput").ap()
    muk_d = nc.dram_tensor(pref + "muk", [128, NCH], F32, kind="ExternalInput").ap()
    id_d = nc.dram_tensor(pref + "ident", [128, 128], BF16, kind="ExternalInput").ap()

    E = es.enter_context
    xt = E(nc.sbuf_tensor(pref + "xt", [128, NT * C], F32))
    wr = E(nc.sbuf_tensor(pref + "wr", [128, NCH * C], BF16))
    wk = E(nc.sbuf_tensor(pref + "wk", [128, NCH * C], BF16))
    wv = E(nc.sbuf_tensor(pref + "wv", [128, NCH * C], BF16))
    mur = E(nc.sbuf_tensor(pref + "mur_s", [128, NCH], F32))
    muk = E(nc.sbuf_tensor(pref + "muk_s", [128, NCH], F32))
    murn = E(nc.sbuf_tensor(pref + "murn", [128, NCH], F32))
    mukn = E(nc.sbuf_tensor(pref + "mukn", [128, NCH], F32))
    idb = E(nc.sbuf_tensor(pref + "idb", [128, 128], BF16))
    xbf = E(nc.sbuf_tensor(pref + "xbf", [128, 2 * C], BF16))
    NE = NCH * 129
    xfm = E(nc.sbuf_tensor(pref + "xfm", [128, 2 * NE], BF16))
    xr = E(nc.sbuf_tensor(pref + "xr", [128, 2 * C], BF16))
    xk = E(nc.sbuf_tensor(pref + "xk", [128, 2 * C], BF16))
    rt = E(nc.sbuf_tensor(pref + "rt", [128, 2 * C], F32))
    dif = E(nc.sbuf_tensor(pref + "dif", [128, 2 * C], BF16))
    kab = E(nc.sbuf_tensor(pref + "kab", [128, 2 * C], BF16))
    kafm = E(nc.sbuf_tensor(pref + "kafm", [128, 2 * C], BF16))
    zt = E(nc.sbuf_tensor(pref + "zt", [128, 2 * C], F32))
    scr = E(nc.sbuf_tensor(pref + "scr", [128, 2 * C], F32))
    st = E(nc.sbuf_tensor(pref + "st", [128, 16], F32))
    yst = E(nc.sbuf_tensor(pref + "yst", [128, 4 * C], F32))
    pxT = E(nc.psum_tensor(pref + "pxT", [128, 1024], BF16))
    pr = E(nc.psum_tensor(pref + "pr", [128, 1024], F32))
    pk = E(nc.psum_tensor(pref + "pk", [128, 1024], F32))
    pkT = E(nc.psum_tensor(pref + "pkT", [128, 1024], BF16))
    pv = E(nc.psum_tensor(pref + "pv", [128, 1024], F32))

    sW = Sem(ses, nc, pref + "sW")
    sC = Sem(ses, nc, pref + "sC")
    sX = Sem(ses, nc, pref + "sX")
    sT = Sem(ses, nc, pref + "sT")
    sV = Sem(ses, nc, pref + "sV")
    sS = Sem(ses, nc, pref + "sS")
    sAO0 = Sem(ses, nc, pref + "sAO0")
    sAO1 = Sem(ses, nc, pref + "sAO1")
    sAOp = [sAO0, sAO1]
    sAOr = Sem(ses, nc, pref + "sAOr")
    sO = Sem(ses, nc, pref + "sO")
    sLN = Sem(ses, nc, pref + "sLN")

    def g0(g):
        g.dma_start(wr[:].rearrange("p (ch d) -> p ch d", d=C), wrt_d.rearrange("(ch p) d -> p ch d", p=128)).then_inc(sW.dinc(), 16)
        g.dma_start(wk[:].rearrange("p (ch d) -> p ch d", d=C), wkt_d.rearrange("(ch p) d -> p ch d", p=128)).then_inc(sW.dinc(), 16)
        g.dma_start(wv[:].rearrange("p (ch d) -> p ch d", d=C), wvt_d.rearrange("(ch p) d -> p ch d", p=128)).then_inc(sW.dinc(), 16)
    blk.gpsimd(g0)

    def s0(sync):
        sync.dma_start(mur[:], mur_d).then_inc(sC.dinc(), 16)
        sync.dma_start(muk[:], muk_d).then_inc(sC.dinc(), 16)
        sync.dma_start(idb[:], id_d).then_inc(sC.dinc(), 16)
    blk.sync(s0)
    consts_in = sC.v

    def v0(v):
        v.wait_ge(sC.h, consts_in)
        v.tensor_scalar(murn[:], mur[:], -1.0, 1.0, ALU.mult, ALU.add)
        v.tensor_scalar(mukn[:], muk[:], -1.0, 1.0, ALU.mult, ALU.add).then_inc(sV.inc(), 1)
    blk.vector(v0)
    prod["mu_ready"] = (sV.h, sV.v)

    for b in range(B):
        def sxb(sync, b=b):
            if b > 0:
                sync.wait_ge(*prod[("ffn_zdone", b - 1, NT - 1)])
            sync.dma_start(xt[:].rearrange("p (n c) -> p n c", c=C), x_ap[b].rearrange("(n p) c -> p n c", p=128)).then_inc(sX.dinc(), 16)
        blk.sync(sxb)
        prod[("ffn_x", b)] = (sX.h, sX.v)

        for i in range(NT):
            k = b * NT + i
            pp = k % 2
            xtile = xt[:, i * C:(i + 1) * C]

            def vcast(v, b=b, i=i, pp=pp, xtile=xtile):
                v.wait_ge(*prod[("ffn_x", b)])
                if b * NT + i >= 2:
                    kk = b * NT + i - 2
                    v.wait_ge(*prod[("ffn_xT", kk // NT, kk % NT)])
                v.tensor_copy(xbf[:, pp * C:(pp + 1) * C], xtile).then_inc(sV.inc(), 1)
            blk.vector(vcast)
            prod[("ffn_cast", b, i)] = (sV.h, sV.v)

            def t1(t, b=b, i=i, pp=pp):
                t.wait_ge(sC.h, 48)
                t.wait_ge(*prod[("ffn_cast", b, i)])
                if b * NT + i >= 1:
                    kk = b * NT + i - 1
                    t.wait_ge(*prod[("ffn_evx", kk // NT, kk % NT)])
                ins = None
                for ch in range(NCH):
                    ins = t.transpose(pxT[:, ch * 128:(ch + 1) * 128],
                                      xbf[:, pp * C + ch * 128: pp * C + (ch + 1) * 128], idb[:])
                ins.then_inc(sT.inc(), 1)
            blk.tensor(t1)
            prod[("ffn_xT", b, i)] = (sT.h, sT.v)

            def vevac(v, b=b, i=i, pp=pp, k=k):
                v.wait_ge(*prod[("ffn_xT", b, i)])
                if k >= 2:
                    kk = k - 2
                    v.wait_ge(*prod[("ffn_mix", kk // NT, kk % NT)])
                if i == 0:
                    v.memset(xfm[:, pp * NE:(pp + 1) * NE].rearrange("p (ch e) -> p ch e", e=129)[:, :, 0:1], 0.0).then_inc(sV.inc(), 1)
                else:
                    v.tensor_copy(
                        xfm[:, pp * NE:(pp + 1) * NE].rearrange("p (ch e) -> p ch e", e=129)[:, :, 0:1],
                        xfm[:, (1 - pp) * NE:(2 - pp) * NE].rearrange("p (ch e) -> p ch e", e=129)[:, :, 128:129]).then_inc(sV.inc(), 1)
                v.wait_ge(sV.h, sV.v)
                v.tensor_copy(
                    xfm[:, pp * NE:(pp + 1) * NE].rearrange("p (ch e) -> p ch e", e=129)[:, :, 1:129],
                    pxT[:].rearrange("p (ch e) -> p ch e", e=128)).then_inc(sV.inc(), 1)
            blk.vector(vevac)
            prod[("ffn_evx", b, i)] = (sV.h, sV.v)

            def vmix(v, b=b, i=i, pp=pp, k=k):
                v.wait_ge(*prod[("ffn_evx", b, i)])
                v.wait_ge(*prod["mu_ready"])
                if k >= 2:
                    kk = k - 2
                    v.wait_ge(*prod[("ffn_kmm", kk // NT, kk % NT)])
                ext = xfm[:, pp * NE:(pp + 1) * NE].rearrange("p (ch e) -> p ch e", e=129)
                cur = ext[:, :, 1:129]
                prv = ext[:, :, 0:128]
                dif3 = dif[:, pp * C:(pp + 1) * C].rearrange("p (ch e) -> p ch e", e=128)
                v.tensor_tensor(dif3, cur, prv, ALU.subtract).then_inc(sV.inc(), 1)
                v.wait_ge(sV.h, sV.v)
                ins = None
                for ch in range(NCH):
                    v.scalar_tensor_tensor(xr[:, pp * C + ch * 128:pp * C + (ch + 1) * 128],
                                           dif3[:, ch], mur[:, ch:ch + 1],
                                           prv[:, ch], ALU.mult, ALU.add)
                    ins = v.scalar_tensor_tensor(xk[:, pp * C + ch * 128:pp * C + (ch + 1) * 128],
                                           dif3[:, ch], muk[:, ch:ch + 1],
                                           prv[:, ch], ALU.mult, ALU.add)
                ins.then_inc(sV.inc(), 1)
            blk.vector(vmix)
            prod[("ffn_mix", b, i)] = (sV.h, sV.v)

            def t2(t, b=b, i=i, pp=pp, k=k):
                t.wait_ge(sW.h, 48)
                t.wait_ge(*prod[("ffn_mix", b, i)])
                if k >= 1:
                    kk = k - 1
                    t.wait_ge(*prod[("ffn_sig", kk // NT, kk % NT)])
                    t.wait_ge(*prod[("ffn_relu", kk // NT, kk % NT)])
                ins = None
                for ch in range(NCH):
                    lhs = xr[:, pp * C + ch * 128: pp * C + (ch + 1) * 128]
                    t.matmul(pr[:, 0:512], lhs, wr[:, ch * C: ch * C + 512], start=(ch == 0), stop=(ch == NCH - 1))
                    ins = t.matmul(pr[:, 512:1024], lhs, wr[:, ch * C + 512:(ch + 1) * C], start=(ch == 0), stop=(ch == NCH - 1))
                ins.then_inc(sT.inc(), 1)
                for ch in range(NCH):
                    lhs = xk[:, pp * C + ch * 128: pp * C + (ch + 1) * 128]
                    t.matmul(pk[:, 0:512], lhs, wk[:, ch * C: ch * C + 512], start=(ch == 0), stop=(ch == NCH - 1))
                    ins = t.matmul(pk[:, 512:1024], lhs, wk[:, ch * C + 512:(ch + 1) * C], start=(ch == 0), stop=(ch == NCH - 1))
                ins.then_inc(sT.inc(), 1)
            blk.tensor(t2)
            prod[("ffn_rmm", b, i)] = (sT.h, sT.v - 1)
            prod[("ffn_kmm", b, i)] = (sT.h, sT.v)

            def ssig(s, b=b, i=i, pp=pp):
                s.wait_ge(*prod[("ffn_rmm", b, i)])
                s.activation(rt[:, pp * C:(pp + 1) * C], pr[:], AF.Exp, scale=-1.0).then_inc(sS.inc(), 1)
            blk.scalar(ssig)
            prod[("ffn_sige", b, i)] = (sS.h, sS.v)

            def vsig(v, b=b, i=i, pp=pp):
                v.wait_ge(*prod[("ffn_sige", b, i)])
                v.tensor_scalar(rt[:, pp * C:(pp + 1) * C], rt[:, pp * C:(pp + 1) * C], 1.0, None, ALU.add).then_inc(sV.inc(), 1)
                v.wait_ge(sV.h, sV.v)
                v.reciprocal(rt[:, pp * C:(pp + 1) * C], rt[:, pp * C:(pp + 1) * C]).then_inc(sV.inc(), 1)
            blk.vector(vsig)
            prod[("ffn_sig", b, i)] = (sV.h, sV.v)

            def vrelu(v, b=b, i=i, pp=pp, k=k):
                v.wait_ge(*prod[("ffn_kmm", b, i)])
                if k >= 1:
                    kk = k - 1
                    v.wait_ge(*prod[("ffn_kaT", kk // NT, kk % NT)])
                v.tensor_scalar(kab[:, pp * C:(pp + 1) * C], pk[:], 0.0, None, ALU.max).then_inc(sV.inc(), 1)
            blk.vector(vrelu)
            prod[("ffn_relumax", b, i)] = (sV.h, sV.v)

            def s_sq(s, b=b, i=i, pp=pp):
                s.wait_ge(*prod[("ffn_relumax", b, i)])
                s.activation(kab[:, pp * C:(pp + 1) * C], kab[:, pp * C:(pp + 1) * C],
                             AF.Square).then_inc(sS.inc(), 1)
            blk.scalar(s_sq)
            prod[("ffn_relu", b, i)] = (sS.h, sS.v)

            def t3(t, b=b, i=i, pp=pp, k=k):
                t.wait_ge(*prod[("ffn_relu", b, i)])
                if k >= 1:
                    kk = k - 1
                    t.wait_ge(*prod[("ffn_evk", kk // NT, kk % NT)])
                ins = None
                for ch in range(NCH):
                    ins = t.transpose(pkT[:, ch * 128:(ch + 1) * 128],
                                      kab[:, pp * C + ch * 128: pp * C + (ch + 1) * 128], idb[:])
                ins.then_inc(sT.inc(), 1)
            blk.tensor(t3)
            prod[("ffn_kaT", b, i)] = (sT.h, sT.v)

            def vevk(v, b=b, i=i, pp=pp, k=k):
                v.wait_ge(*prod[("ffn_kaT", b, i)])
                if k >= 2:
                    kk = k - 2
                    v.wait_ge(*prod[("ffn_vmm", kk // NT, kk % NT)])
                v.tensor_copy(kafm[:, pp * C:(pp + 1) * C], pkT[:]).then_inc(sV.inc(), 1)
            blk.vector(vevk)
            prod[("ffn_evk", b, i)] = (sV.h, sV.v)

            def t4(t, b=b, i=i, pp=pp, k=k):
                t.wait_ge(*prod[("ffn_evk", b, i)])
                if k >= 1:
                    kk = k - 1
                    t.wait_ge(*prod[("ffn_rv", kk // NT, kk % NT)])
                ins = None
                for ch in range(NCH):
                    lhs = kafm[:, pp * C + ch * 128: pp * C + (ch + 1) * 128]
                    t.matmul(pv[:, 0:512], lhs, wv[:, ch * C: ch * C + 512], start=(ch == 0), stop=(ch == NCH - 1))
                    ins = t.matmul(pv[:, 512:1024], lhs, wv[:, ch * C + 512:(ch + 1) * C], start=(ch == 0), stop=(ch == NCH - 1))
                ins.then_inc(sT.inc(), 1)
            blk.tensor(t4)
            prod[("ffn_vmm", b, i)] = (sT.h, sT.v)

            def vrv(v, b=b, i=i, pp=pp, xtile=xtile, k=k):
                v.wait_ge(*prod[("ffn_vmm", b, i)])
                v.wait_ge(*prod[("ffn_sig", b, i)])
                if k >= 2:
                    kk = k - 2
                    v.wait_ge(*prod[("ffn_lnout", kk // NT, kk % NT)])
                v.tensor_tensor(zt[:, pp * C:(pp + 1) * C], rt[:, pp * C:(pp + 1) * C], pv[:], ALU.mult).then_inc(sV.inc(), 1)
                v.wait_ge(sV.h, sV.v)
                v.tensor_tensor(zt[:, pp * C:(pp + 1) * C], zt[:, pp * C:(pp + 1) * C], xtile, ALU.add).then_inc(sV.inc(), 1)
            blk.vector(vrv)
            prod[("ffn_rv", b, i)] = (sV.h, sV.v)
            prod[("ffn_zdone", b, i)] = (sV.h, sV.v)

            slot = i % 4
            gkey = ("ffn_ydma", b, i // 4 - 1) if i // 4 >= 1 else (("ffn_ydma", b - 1, (NT - 1) // 4) if b > 0 else None)
            ews = [prod[("ffn_rv", b, i)]]
            if gkey is not None and gkey in prod:
                ews.append(prod[gkey])
            ln_emit(blk, sLN, zt[:, pp * C:(pp + 1) * C], yst[:, slot * C:(slot + 1) * C],
                    scr[:, pp * C:(pp + 1) * C], st[:, 8:12], C, prod, ("ffn_lnout", b, i),
                    extra_wait=ews)

            if slot == 3 or i == NT - 1:
                g0i = i - slot
                def sydma(sync, b=b, i=i, g0i=g0i, slot=slot):
                    for jj in range(g0i, i + 1):
                        sync.wait_ge(*prod[("ffn_lnout", b, jj)])
                    sync.dma_start(
                        y_ap[b].rearrange("(n p) c -> p n c", p=128)[:, g0i:i + 1],
                        yst[:, 0:(slot + 1) * C].rearrange("p (n c) -> p n c", c=C)).then_inc(sO.dinc(), 16)
                blk.sync(sydma)
                prod[("ffn_ydma", b, i // 4)] = (sO.h, sO.v)

    def sfin(sync):
        sync.wait_ge(sO.h, sO.v)
    blk.sync(sfin)
    return [v for v in locals().values() if isinstance(v, Sem)]


def l1ffn_host_inputs(W_r, W_k, W_v, mu_r, mu_k):
    C = W_r.shape[0]
    NCH = C // 128
    import ml_dtypes
    return {
        "f_WrT": np.ascontiguousarray(W_r.T),
        "f_WkT": np.ascontiguousarray(W_k.T),
        "f_WvT": np.ascontiguousarray(W_v.T),
        "f_mur": np.ascontiguousarray(mu_r.reshape(NCH, 128).T),
        "f_muk": np.ascontiguousarray(mu_k.reshape(NCH, 128).T),
        "f_ident": np.eye(128, dtype=ml_dtypes.bfloat16),
    }


def l0attn_host_inputs(Wqkv, bqkv, Wo, bo, pref="a_"):
    import ml_dtypes
    C = 1024
    FG = 16
    WqkvT = Wqkv.T
    cols_qk = []
    cols_v = []
    for fg in range(FG):
        cols_qk.extend(range(fg * 192, fg * 192 + 128))
        cols_v.extend(range(fg * 192 + 128, fg * 192 + 192))
    Wcat = np.ascontiguousarray(np.concatenate([WqkvT[:, cols_qk], WqkvT[:, cols_v]], axis=1))
    bqk = np.ascontiguousarray(np.stack([bqkv[cols_qk[fg * 128:(fg + 1) * 128]] for fg in range(FG)], axis=1))
    bv = bqkv[np.array(cols_v)].reshape(1, C).astype(np.float32)
    return {
        pref + "Wcat": Wcat.astype(np.float32),
        pref + "bqk": bqk.astype(np.float32),
        pref + "bv": bv,
        pref + "WoT": np.ascontiguousarray(Wo.T),
        pref + "bo": bo.reshape(1, C).astype(np.float32),
        pref + "ident": np.eye(128, dtype=ml_dtypes.bfloat16),
        pref + "ones": np.ones((1, 128), dtype=ml_dtypes.bfloat16),
    }


def build_l0attn(nc, es, blk, prod, B, pref="a_", x_ap=None, y_ap=None, ses=None):
    ses = ses or es
    C = 1024
    NH = 16
    T = NH * 128
    FG = 16
    NCH = 8
    NP = NH // 2
    NPAIR = B * NP
    w_d = nc.dram_tensor(pref + "Wcat", [C, 3 * C], F32, kind="ExternalInput").ap()
    bqk_d = nc.dram_tensor(pref + "bqk", [128, FG], F32, kind="ExternalInput").ap()
    bv_d = nc.dram_tensor(pref + "bv", [1, C], F32, kind="ExternalInput").ap()
    wo_d = nc.dram_tensor(pref + "WoT", [C, C], F32, kind="ExternalInput").ap()
    bo_d = nc.dram_tensor(pref + "bo", [1, C], F32, kind="ExternalInput").ap()
    id_d = nc.dram_tensor(pref + "ident", [128, 128], BF16, kind="ExternalInput").ap()
    on_d = nc.dram_tensor(pref + "ones", [1, 128], BF16, kind="ExternalInput").ap()

    E = es.enter_context
    wc = E(nc.sbuf_tensor(pref + "wc", [128, NCH * 3 * C], BF16))
    wo = E(nc.sbuf_tensor(pref + "wo", [128, NCH * C], BF16))
    bqk = E(nc.sbuf_tensor(pref + "bqk_s", [128, FG], F32))
    bvr = E(nc.sbuf_tensor(pref + "bvr", [1, C], BF16))
    bor = E(nc.sbuf_tensor(pref + "bor", [1, C], BF16))
    idb = E(nc.sbuf_tensor(pref + "idb", [128, 128], BF16))
    idf = E(nc.sbuf_tensor(pref + "idf", [128, 128], F32))
    ones = E(nc.sbuf_tensor(pref + "ones_s", [1, 128], BF16))
    xti = E(nc.sbuf_tensor(pref + "xti", [128, 4 * C], F32))
    xbf = E(nc.sbuf_tensor(pref + "xbf", [128, 2 * C], BF16))
    xfm = E(nc.sbuf_tensor(pref + "xfm", [128, 2 * C], BF16))
    qkA = E(nc.sbuf_tensor(pref + "qkA", [128, T], BF16))
    qkB = E(nc.sbuf_tensor(pref + "qkB", [128, T], BF16))
    kA = E(nc.sbuf_tensor(pref + "kA", [64, T], BF16))
    qB = E(nc.sbuf_tensor(pref + "qB", [128, T], BF16))
    v2 = E(nc.sbuf_tensor(pref + "v2", [128, 2 * FG * 65], BF16))
    pt = E(nc.sbuf_tensor(pref + "pt", [128, 4 * 1024], BF16))
    pvf = E(nc.sbuf_tensor(pref + "pvf", [65, 2 * 512], F32))
    vals = E(nc.sbuf_tensor(pref + "vals", [128, 2 * C], BF16))
    vfm = E(nc.sbuf_tensor(pref + "vfm", [128, C], BF16))
    zt = E(nc.sbuf_tensor(pref + "zt", [128, C], F32))
    scr = E(nc.sbuf_tensor(pref + "scr", [128, C], F32))
    st = E(nc.sbuf_tensor(pref + "st", [128, 8], F32))
    yst = E(nc.sbuf_tensor(pref + "yst", [128, 2 * C], F32))

    pool = E(nc.psum_tensor(pref + "pool", [128, 4096], F32))
    poolb = pool[:].bitcast(BF16)
    psc = [pool[:, r * 1024:(r + 1) * 1024] for r in range(2)]
    pxT = poolb[:, 5120:6144]
    pqk = pool[:, 2048:2560]
    pv_ = pool[:, 2560:3072]
    ppv = [pool[0:65, 3072 + hh * 512: 3072 + hh * 512 + 512] for hh in range(2)]
    pvT = [pool[0:128, 3072 + hh * 512: 3072 + hh * 512 + 260] for hh in range(2)]
    pvalT = poolb[:, 6144:7168]
    po = pool[:, 3072:4096]

    sW = Sem(ses, nc, pref + "sW")
    sC = Sem(ses, nc, pref + "sC")
    sXa = Sem(ses, nc, pref + "sXa")
    sXb = Sem(ses, nc, pref + "sXb")
    sXh = [sXa, sXb]
    sX3a = Sem(ses, nc, pref + "sX3a")
    sX3b = Sem(ses, nc, pref + "sX3b")
    sX3h = [sX3a, sX3b]
    sT = Sem(ses, nc, pref + "sT")
    sV = Sem(ses, nc, pref + "sV")
    sS = Sem(ses, nc, pref + "sS")
    sG = Sem(ses, nc, pref + "sG")
    sAO0 = Sem(ses, nc, pref + "sAO0")
    sAO1 = Sem(ses, nc, pref + "sAO1")
    sAOp = [sAO0, sAO1]
    sAOr = Sem(ses, nc, pref + "sAOr")
    sO = Sem(ses, nc, pref + "sO")
    sLN = Sem(ses, nc, pref + "sLN")

    def g0(g):
        g.dma_start(wc[:].rearrange("p (ch d) -> p ch d", d=3 * C),
                    w_d.rearrange("(ch p) d -> p ch d", p=128)).then_inc(sW.dinc(), 16)
        g.dma_start(wo[:].rearrange("p (ch d) -> p ch d", d=C),
                    wo_d.rearrange("(ch p) d -> p ch d", p=128)).then_inc(sW.dinc(), 16)
        g.dma_start(bvr[:], bv_d).then_inc(sW.dinc(), 16)
        g.dma_start(bor[:], bo_d).then_inc(sW.dinc(), 16)
    blk.gpsimd(g0)
    W_ALL = sW.v

    def s0(sync):
        sync.dma_start(bqk[:], bqk_d).then_inc(sC.dinc(), 16)
        sync.dma_start(idb[:], id_d).then_inc(sC.dinc(), 16)
        sync.dma_start(ones[:], on_d).then_inc(sC.dinc(), 16)
    blk.sync(s0)
    CIN = sC.v

    def v00(v):
        v.wait_ge(sC.h, CIN)
        v.tensor_copy(idf[:], idb[:])
        v.memset(v2[:].rearrange("p (a fg e) -> p a fg e", fg=FG, e=65)[:, :, :, 64:65], 1.0).then_inc(sV.inc(), 1)
    blk.vector(v00)
    prod["a_init"] = (sV.h, sV.v)

    def emit_ph1(kpair):
        b, hp = divmod(kpair, NP)
        for hh in range(2):
            h = 2 * hp + hh

            def sx(sync, b=b, h=h, hh=hh, kpair=kpair):
                if kpair >= 1:
                    sync.wait_ge(*prod[("a_xT", kpair - 1, hh)])
                sync.dma_start(xti[:, hh * C:(hh + 1) * C],
                               x_ap[b].rearrange("(n p) c -> p n c", p=128)[:, h]).then_inc(sXh[hh].dinc(), 16)
            blk.sync(sx)
            prod[("a_x", kpair, hh)] = (sXh[hh].h, sXh[hh].v)

            def vc(v, hh=hh, kpair=kpair):
                v.wait_ge(*prod[("a_x", kpair, hh)])
                if kpair >= 1:
                    v.wait_ge(*prod[("a_xT", kpair - 1, hh)])
                v.tensor_copy(xbf[:, hh * C:(hh + 1) * C], xti[:, hh * C:(hh + 1) * C]).then_inc(sV.inc(), 1)
            blk.vector(vc)
            prod[("a_cast", kpair, hh)] = (sV.h, sV.v)

            def t_xT(t, hh=hh, kpair=kpair):
                t.wait_ge(sC.h, CIN)
                t.wait_ge(*prod[("a_cast", kpair, hh)])
                pk = (kpair, 0) if hh == 1 else ((kpair - 1, 1) if kpair >= 1 else None)
                if pk is not None:
                    t.wait_ge(*prod[("a_vev", pk[0], pk[1], 1)])
                ins = None
                for ch in range(NCH):
                    ins = t.transpose(pxT[:, ch * 128:(ch + 1) * 128],
                                      xbf[:, hh * C + ch * 128: hh * C + (ch + 1) * 128], idb[:])
                ins.then_inc(sT.inc(), 1)
            blk.tensor(t_xT)
            prod[("a_xT", kpair, hh)] = (sT.h, sT.v)

            def vev(v, hh=hh, kpair=kpair):
                v.wait_ge(*prod[("a_xT", kpair, hh)])
                if kpair >= 1:
                    t = prod.get(("a_vmm", kpair - 1, hh, 1))
                    if t:
                        v.wait_ge(*t)
                v.tensor_copy(xfm[:, hh * C:(hh + 1) * C], pxT[:]).then_inc(sV.inc(), 1)
            blk.vector(vev)
            prod[("a_evxT", kpair, hh)] = (sV.h, sV.v)

            for grp in range(4):
                def t_qk(t, hh=hh, kpair=kpair, grp=grp):
                    t.wait_ge(sW.h, W_ALL)
                    t.wait_ge(*prod[("a_evxT", kpair, hh)])
                    pk = (kpair, hh, grp - 1) if grp >= 1 else (
                        (kpair, 0, 3) if hh == 1 else ((kpair - 1, 1, 3) if kpair >= 1 else None))
                    if pk is not None:
                        t.wait_ge(*prod[("a_qkev", *pk)])
                    ins = None
                    for fgi in range(4):
                        fg = grp * 4 + fgi
                        for ch in range(NCH):
                            ins = t.matmul(pqk[:, fgi * 128:(fgi + 1) * 128],
                                           wc[:, ch * 3 * C + fg * 128: ch * 3 * C + (fg + 1) * 128],
                                           xfm[:, hh * C + ch * 128: hh * C + (ch + 1) * 128],
                                           start=(ch == 0), stop=(ch == NCH - 1))
                    ins.then_inc(sT.inc(), 1)
                blk.tensor(t_qk)
                prod[("a_qkmm", kpair, hh, grp)] = (sT.h, sT.v)

                def v_qkev(v, hh=hh, kpair=kpair, grp=grp):
                    v.wait_ge(*prod[("a_qkmm", kpair, hh, grp)])
                    if kpair >= 1:
                        v.wait_ge(*prod[("a_scdone", kpair - 1, hh)])
                    ins = None
                    for fgi in range(4):
                        fg = grp * 4 + fgi
                        src = pqk[:, fgi * 128:(fgi + 1) * 128]
                        dst = (qkA if hh == 0 else qkB)[:, fg * 128:(fg + 1) * 128]
                        ins = v.tensor_scalar(dst, src, bqk[:, fg:fg + 1], None, ALU.add)
                    ins.then_inc(sV.inc(), 1)
                blk.vector(v_qkev)
                prod[("a_qkev", kpair, hh, grp)] = (sV.h, sV.v)

            def g_shift(g, hh=hh, kpair=kpair):
                for grp in range(4):
                    g.wait_ge(*prod[("a_qkev", kpair, hh, grp)])
                if kpair >= 1:
                    g.wait_ge(*prod[("a_scdone", kpair - 1, hh)])
                if hh == 0:
                    g.dma_start(kA[0:64, :], qkA[64:128, :]).then_inc(sG.dinc(), 16)
                else:
                    g.dma_start(qB[64:128, :], qkB[0:64, :]).then_inc(sG.dinc(), 16)
            blk.gpsimd(g_shift)
            prod[("a_shift", kpair, hh)] = (sG.h, sG.v)

            for half in range(2):
                def t_v(t, hh=hh, kpair=kpair, half=half):
                    t.wait_ge(sW.h, W_ALL)
                    t.wait_ge(*prod[("a_evxT", kpair, hh)])
                    if half == 1:
                        t.wait_ge(*prod[("a_vev", kpair, hh, 0)])
                    elif hh == 1:
                        t.wait_ge(*prod[("a_vev", kpair, 0, 1)])
                    elif kpair >= 1:
                        t.wait_ge(*prod[("a_vev", kpair - 1, 1, 1)])
                    for ch in range(NCH):
                        t.matmul(pv_[:], xfm[:, hh * C + ch * 128: hh * C + (ch + 1) * 128],
                                 wc[:, ch * 3 * C + 2 * C + half * 512: ch * 3 * C + 2 * C + (half + 1) * 512],
                                 start=(ch == 0), stop=False)
                    ins = t.matmul(pv_[:], ones[:], bvr[0:1, half * 512:(half + 1) * 512],
                                   start=False, stop=True)
                    ins.then_inc(sT.inc(), 1)
                blk.tensor(t_v)
                prod[("a_vmm", kpair, hh, half)] = (sT.h, sT.v)

                def v_vev(v, hh=hh, kpair=kpair, half=half):
                    v.wait_ge(*prod[("a_vmm", kpair, hh, half)])
                    v.wait_ge(*prod["a_init"])
                    if kpair >= 1:
                        v.wait_ge(*prod[("a_pvmm", kpair - 1, 3, hh, 7)])
                    dst = v2[:, hh * FG * 65:(hh + 1) * FG * 65].rearrange(
                        "p (fg e) -> p fg e", e=65)[:, half * 8:(half + 1) * 8, 0:64]
                    v.tensor_copy(dst, pv_[:].rearrange("p (fg e) -> p fg e", e=64)).then_inc(sV.inc(), 1)
                blk.vector(v_vev)
                prod[("a_vev", kpair, hh, half)] = (sV.h, sV.v)

    def emit_ph2(kpair):
        pv_emitters = []
        for qc in range(4):
            for hh in range(2):
                for fgp in range(8):
                    idx = qc * 16 + hh * 8 + fgp
                    rot = idx % 2

                    def t_sc(t, hh=hh, kpair=kpair, qc=qc, fgp=fgp, rot=rot, idx=idx):
                        if idx == 0:
                            t.wait_ge(*prod[("a_shift", kpair, 1)])
                        if idx >= 2:
                            pi = idx - 2
                            t.wait_ge(*prod[("a_exp", kpair, pi // 16, (pi % 16) // 8, pi % 8)])
                        elif kpair >= 1:
                            t.wait_ge(*prod[("a_exp", kpair - 1, 3, 1, 6 + idx)])
                        ins = None
                        for fgi in range(2):
                            fg = fgp * 2 + fgi
                            if hh == 0:
                                lhsT = kA[0:64, fg * 128:(fg + 1) * 128]
                                rhs = qkA[0:64, qc * 512:(qc + 1) * 512]
                            else:
                                lhsT = qkB[64:128, fg * 128:(fg + 1) * 128]
                                rhs = qB[64:128, qc * 512:(qc + 1) * 512]
                            ins = t.matmul(psc[rot][:, fgi * 512:(fgi + 1) * 512],
                                           lhsT, rhs, start=True, stop=True)
                        ins.then_inc(sT.inc(), 1)
                    blk.tensor(t_sc)
                    prod[("a_scmm", kpair, qc, hh, fgp)] = (sT.h, sT.v)

                    def s_exp(s, hh=hh, kpair=kpair, qc=qc, fgp=fgp, rot=rot):
                        s.wait_ge(*prod[("a_scmm", kpair, qc, hh, fgp)])
                        if fgp >= 2:
                            s.wait_ge(*prod[("a_pvmm", kpair, qc, hh, fgp - 2)])
                        elif qc >= 1:
                            s.wait_ge(*prod[("a_pvmm", kpair, qc - 1, hh, 6 + fgp)])
                        elif kpair >= 1:
                            s.wait_ge(*prod[("a_pvmm", kpair - 1, 3, hh, 6 + fgp)])
                        s.activation(pt[:, (hh * 2 + fgp % 2) * 1024:(hh * 2 + fgp % 2 + 1) * 1024],
                                     psc[rot][:], AF.Exp, scale=0.125).then_inc(sS.inc(), 1)
                    blk.scalar(s_exp)
                    prod[("a_exp", kpair, qc, hh, fgp)] = (sS.h, sS.v)

                    def mk_pv(hh=hh, kpair=kpair, qc=qc, fgp=fgp):
                      def t_pv(t, hh=hh, kpair=kpair, qc=qc, fgp=fgp):
                        t.wait_ge(*prod[("a_exp", kpair, qc, hh, fgp)])
                        if fgp == 0:
                            if qc == 0:
                                if kpair >= 1:
                                    t.wait_ge(*prod[("a_oz", kpair - 1, 1)])
                            else:
                                t.wait_ge(*prod[("a_pvev", kpair, qc - 1, hh)])
                                t.wait_ge(*prod[("a_norm", kpair, qc - 1, hh)])
                        ins = None
                        for fgi in range(2):
                            fg = fgp * 2 + fgi
                            vslab = v2[:, hh * FG * 65 + fg * 65:hh * FG * 65 + (fg + 1) * 65]
                            ins = t.matmul(ppv[hh], vslab,
                                           pt[:, (hh * 2 + fgp % 2) * 1024 + fgi * 512:(hh * 2 + fgp % 2) * 1024 + (fgi + 1) * 512],
                                           start=(fgp == 0 and fgi == 0), stop=(fgp == 7 and fgi == 1))
                        ins.then_inc(sT.inc(), 1)
                      t_pv._key = (kpair, qc, hh, fgp)
                      return t_pv
                    pv_emitters.append(mk_pv())
                    if len(pv_emitters) >= 2:
                        em = pv_emitters.pop(0)
                        blk.tensor(em)
                        prod[("a_pvmm",) + em._key] = (sT.h, sT.v)

                while pv_emitters:
                    em = pv_emitters.pop(0)
                    blk.tensor(em)
                    prod[("a_pvmm",) + em._key] = (sT.h, sT.v)

                def v_pvev(v, hh=hh, kpair=kpair, qc=qc):
                    v.wait_ge(*prod[("a_pvmm", kpair, qc, hh, 7)])
                    if qc >= 2:
                        v.wait_ge(*prod[("a_pvT", kpair, qc - 2, hh)])
                    elif kpair >= 1:
                        v.wait_ge(*prod[("a_pvT", kpair - 1, qc + 2, hh)])
                    v.tensor_copy(pvf[:, (qc % 2) * 512:(qc % 2 + 1) * 512], ppv[hh]).then_inc(sV.inc(), 1)
                blk.vector(v_pvev)
                prod[("a_pvev", kpair, qc, hh)] = (sV.h, sV.v)

                def t_pvT(t, hh=hh, kpair=kpair, qc=qc):
                    t.wait_ge(*prod[("a_pvev", kpair, qc, hh)])
                    t.wait_ge(*prod["a_init"])
                    ins = None
                    for sub in range(4):
                        ins = t.transpose(pvT[hh][:, sub * 65:(sub + 1) * 65],
                                          pvf[0:65, (qc % 2) * 512 + sub * 128:(qc % 2) * 512 + (sub + 1) * 128],
                                          idf[0:65, 0:65])
                    ins.then_inc(sT.inc(), 1)
                blk.tensor(t_pvT)
                prod[("a_pvT", kpair, qc, hh)] = (sT.h, sT.v)

                def v_norm(v, hh=hh, kpair=kpair, qc=qc):
                    v.wait_ge(*prod[("a_pvT", kpair, qc, hh)])
                    if qc == 0 and kpair >= 1:
                        v.wait_ge(*prod[("a_vT", kpair - 1, hh)])
                    ins = None
                    for sub in range(4):
                        ins = v.reciprocal(st[:, sub:sub + 1], pvT[hh][:, sub * 65 + 64:sub * 65 + 65])
                    ins.then_inc(sV.inc(), 1)
                    v.wait_ge(sV.h, sV.v)
                    for sub in range(4):
                        fgq = qc * 4 + sub
                        ins = v.tensor_scalar(vals[:, hh * C + fgq * 64: hh * C + (fgq + 1) * 64],
                                              pvT[hh][:, sub * 65:sub * 65 + 64], st[:, sub:sub + 1], None, ALU.mult)
                    ins.then_inc(sV.inc(), 1)
                blk.vector(v_norm)
                prod[("a_norm", kpair, qc, hh)] = (sV.h, sV.v)

        prod[("a_scdone", kpair, 0)] = prod[("a_scmm", kpair, 3, 0, 7)]
        prod[("a_scdone", kpair, 1)] = prod[("a_scmm", kpair, 3, 1, 7)]

    def emit_ph3(kpair):
        b, hp = divmod(kpair, NP)
        for hh in range(2):
            h = 2 * hp + hh
            px = 2 + hh

            def sx3(sync, b=b, h=h, hh=hh, kpair=kpair, px=px):
                if kpair >= 1:
                    sync.wait_ge(*prod[("a_oz", kpair - 1, hh)])
                sync.dma_start(xti[:, px * C:(px + 1) * C],
                               x_ap[b].rearrange("(n p) c -> p n c", p=128)[:, h]).then_inc(sX3h[hh].dinc(), 16)
            blk.sync(sx3)
            prod[("a_x3", kpair, hh)] = (sX3h[hh].h, sX3h[hh].v)

            def t_vT(t, hh=hh, kpair=kpair):
                for qcx in range(4):
                    t.wait_ge(*prod[("a_norm", kpair, qcx, hh)])
                t.wait_ge(*prod[("a_pvev", kpair, 3, hh)])
                if hh == 1:
                    t.wait_ge(*prod[("a_oz", kpair, 0)])
                elif kpair >= 1:
                    t.wait_ge(*prod[("a_oz", kpair - 1, 1)])
                ins = None
                for ch in range(NCH):
                    ins = t.transpose(pvalT[:, ch * 128:(ch + 1) * 128],
                                      vals[:, hh * C + ch * 128: hh * C + (ch + 1) * 128], idb[:])
                ins.then_inc(sT.inc(), 1)
            blk.tensor(t_vT)
            prod[("a_vT", kpair, hh)] = (sT.h, sT.v)

            def v_vfm(v, hh=hh, kpair=kpair):
                v.wait_ge(*prod[("a_vT", kpair, hh)])
                if hh == 1:
                    v.wait_ge(*prod[("a_omm", kpair, 0)])
                elif kpair >= 1:
                    v.wait_ge(*prod[("a_omm", kpair - 1, 1)])
                v.tensor_copy(vfm[:], pvalT[:]).then_inc(sV.inc(), 1)
            blk.vector(v_vfm)
            prod[("a_ovfm", kpair, hh)] = (sV.h, sV.v)

            def t_o(t, hh=hh, kpair=kpair):
                t.wait_ge(sW.h, W_ALL)
                t.wait_ge(*prod[("a_ovfm", kpair, hh)])
                if hh == 0:
                    t.wait_ge(*prod[("a_norm", kpair, 3, 1)])
                ins = None
                for half in range(2):
                    for ch in range(NCH):
                        t.matmul(po[:, half * 512:(half + 1) * 512],
                                 vfm[:, ch * 128:(ch + 1) * 128],
                                 wo[:, ch * C + half * 512: ch * C + (half + 1) * 512],
                                 start=(ch == 0), stop=False)
                    ins = t.matmul(po[:, half * 512:(half + 1) * 512], ones[:],
                                   bor[0:1, half * 512:(half + 1) * 512], start=False, stop=True)
                ins.then_inc(sT.inc(), 1)
            blk.tensor(t_o)
            prod[("a_omm", kpair, hh)] = (sT.h, sT.v)

            def v_z(v, hh=hh, kpair=kpair, px=px):
                v.wait_ge(*prod[("a_omm", kpair, hh)])
                v.wait_ge(*prod[("a_x3", kpair, hh)])
                if hh == 1:
                    v.wait_ge(*prod[("a_lnout", kpair, 0)])
                elif kpair >= 1:
                    v.wait_ge(*prod[("a_lnout", kpair - 1, 1)])
                v.tensor_tensor(zt[:], po[:], xti[:, px * C:(px + 1) * C], ALU.add).then_inc(sV.inc(), 1)
            blk.vector(v_z)
            prod[("a_oz", kpair, hh)] = (sV.h, sV.v)

            ews = [prod[("a_oz", kpair, hh)]]
            if kpair >= 1 and hh == 0:
                ews.append(prod[("a_ydma", kpair - 1)])
            ln_emit(blk, sLN, zt[:], yst[:, hh * C:(hh + 1) * C], scr[:], st[:, 4:8], C,
                    prod, ("a_lnout", kpair, hh), extra_wait=ews)

        def sydma(sync, b=b, hp=hp, kpair=kpair):
            sync.wait_ge(*prod[("a_lnout", kpair, 0)])
            sync.wait_ge(*prod[("a_lnout", kpair, 1)])
            sync.dma_start(y_ap[b].rearrange("(n p) c -> p n c", p=128)[:, 2 * hp:2 * hp + 2],
                           yst[:].rearrange("p (n c) -> p n c", c=C)).then_inc(sO.dinc(), 16)
        blk.sync(sydma)
        prod[("a_ydma", kpair)] = (sO.h, sO.v)

    emit_ph1(0)
    for k in range(NPAIR):
        emit_ph2(k)
        if k + 1 < NPAIR:
            emit_ph1(k + 1)
        emit_ph3(k)

    def sfin(sync):
        sync.wait_ge(sO.h, sO.v)
    blk.sync(sfin)
    return [v for v in locals().values() if isinstance(v, Sem)]


def build_l0moe(nc, es, blk, prod, B, pref="m_", x_ap=None, y_ap=None, ses=None):
    ses = ses or es
    """L0 MoE: y = LN(2u + routed + shared(+biases)). u = x_ap [B, 2048, 1024] fp32.
    Host inputs: m_WrT [16, 1024, 1024] f32 (each Wr[e].T), m_WsT [2, 1024, 1024] f32,
    m_br [16, 1024] f32, m_bs [1, 2048] f32 (two rows concat), m_centT [1024, 16] f32,
    m_eb [1, 16] f32, m_ident [128,128] bf16, m_onesb [1,128] bf16, m_onesf [1,128] f32."""
    C = 1024
    NT = 16
    T = NT * 128
    NE = 16
    NCH = 8
    wr_d = nc.dram_tensor(pref + "WrT", [NE, C, C], F32, kind="ExternalInput").ap()
    ws_d = nc.dram_tensor(pref + "WsT", [2, C, C], F32, kind="ExternalInput").ap()
    br_d = nc.dram_tensor(pref + "br", [NE, C], F32, kind="ExternalInput").ap()
    bs_d = nc.dram_tensor(pref + "bs", [1, 2 * C], F32, kind="ExternalInput").ap()
    ct_d = nc.dram_tensor(pref + "centT", [C, NE], F32, kind="ExternalInput").ap()
    eb_d = nc.dram_tensor(pref + "eb", [1, NE], F32, kind="ExternalInput").ap()
    id_d = nc.dram_tensor(pref + "ident", [128, 128], BF16, kind="ExternalInput").ap()
    ob_d = nc.dram_tensor(pref + "onesb", [1, 128], BF16, kind="ExternalInput").ap()
    of_d = nc.dram_tensor(pref + "onesf", [1, 128], F32, kind="ExternalInput").ap()

    E = es.enter_context
    ut = E(nc.sbuf_tensor(pref + "ut", [128, NT * C], F32))
    ufm = E(nc.sbuf_tensor(pref + "ufm", [128, NT * C], BF16))
    zbf = E(nc.sbuf_tensor(pref + "zbf", [128, NT * C], BF16))
    wre = E(nc.sbuf_tensor(pref + "wre", [128, 2 * NCH * C], BF16))
    ubf = E(nc.sbuf_tensor(pref + "ubf", [128, C], BF16))
    uf32 = E(nc.sbuf_tensor(pref + "uf32", [128, C], F32))
    gal = E(nc.sbuf_tensor(pref + "gal", [128, NT * NE], F32))
    ct = E(nc.sbuf_tensor(pref + "ct", [128, NCH * NE], F32))
    ebr = E(nc.sbuf_tensor(pref + "ebr", [1, NE], F32))
    brp = E(nc.sbuf_tensor(pref + "brp", [1, 2 * C], BF16))
    bsr = E(nc.sbuf_tensor(pref + "bsr", [1, 2 * C], BF16))
    idb = E(nc.sbuf_tensor(pref + "idb", [128, 128], BF16))
    idf = E(nc.sbuf_tensor(pref + "idf", [128, 128], F32))
    onb = E(nc.sbuf_tensor(pref + "onb", [1, 128], BF16))
    onf = E(nc.sbuf_tensor(pref + "onf", [1, 128], F32))
    stx = E(nc.sbuf_tensor(pref + "stx", [128, 16], F32))
    zt = E(nc.sbuf_tensor(pref + "zt", [128, C], F32))
    scr = E(nc.sbuf_tensor(pref + "scr", [128, C], F32))
    yst = E(nc.sbuf_tensor(pref + "yst", [128, 2 * C], F32))

    pool = E(nc.psum_tensor(pref + "pool", [128, 4096], F32))
    poolb = pool[:].bitcast(BF16)
    pTf = pool[:, 0:1024]
    pTb = poolb[:, 2048:3072]
    pg = pool[:, 1536:1552]
    pe = [pool[:, 2048 + i * 1024: 2048 + (i + 1) * 1024] for i in range(2)]

    sW0 = Sem(ses, nc, pref + "sW0")
    sW1 = Sem(ses, nc, pref + "sW1")
    sWs = [sW0, sW1]
    sB = Sem(ses, nc, pref + "sB")
    sC = Sem(ses, nc, pref + "sC")
    sX = Sem(ses, nc, pref + "sX")
    sT = Sem(ses, nc, pref + "sT")
    sV = Sem(ses, nc, pref + "sV")
    sS = Sem(ses, nc, pref + "sS")
    sAO0 = Sem(ses, nc, pref + "sAO0")
    sAO1 = Sem(ses, nc, pref + "sAO1")
    sAOp = [sAO0, sAO1]
    sAOr = Sem(ses, nc, pref + "sAOr")
    sO = Sem(ses, nc, pref + "sO")
    sLN = Sem(ses, nc, pref + "sLN")

    def s0(sync):
        sync.dma_start(ct[:].rearrange("p (ch e) -> p ch e", e=NE),
                       ct_d.rearrange("(ch p) e -> p ch e", p=128)).then_inc(sC.dinc(), 16)
        sync.dma_start(ebr[:], eb_d).then_inc(sC.dinc(), 16)
        sync.dma_start(idb[:], id_d).then_inc(sC.dinc(), 16)
        sync.dma_start(onb[:], ob_d).then_inc(sC.dinc(), 16)
        sync.dma_start(onf[:], of_d).then_inc(sC.dinc(), 16)
    blk.sync(s0)
    CIN = sC.v

    def g0(g):
        g.dma_start(bsr[:], bs_d).then_inc(sB.dinc(), 16)
    blk.gpsimd(g0)

    def v0(v):
        v.wait_ge(sC.h, CIN)
        v.tensor_copy(idf[:], idb[:]).then_inc(sV.inc(), 1)
    blk.vector(v0)
    prod["m_init"] = (sV.h, sV.v)

    for b in range(B):
        def gw(g, b=b):
            g.wait_ge(sC.h, CIN)
            if b > 0:
                g.wait_ge(*prod[("m_emm", b - 1, NE - 1, NT - 1)])
                g.wait_ge(*prod[("m_eev", b - 1, NE - 1, NT - 1)])
            g.dma_start(wre[:, 0:NCH * C].rearrange("p (ch d) -> p ch d", d=C),
                        ws_d[0].rearrange("(ch p) d -> p ch d", p=128)).then_inc(sW0.dinc(), 16)
            g.dma_start(wre[:, NCH * C:].rearrange("p (ch d) -> p ch d", d=C),
                        ws_d[1].rearrange("(ch p) d -> p ch d", p=128)).then_inc(sW1.dinc(), 16)
        blk.gpsimd(gw)
        WS_IN = (sW0.v, sW1.v)
        def sxb(sync, b=b):
            if b > 0:
                sync.wait_ge(*prod[("m_sev", b - 1, NT - 1)])
            sync.dma_start(ut[:].rearrange("p (n c) -> p n c", c=C),
                           x_ap[b].rearrange("(n p) c -> p n c", p=128)).then_inc(sX.dinc(), 16)
        blk.sync(sxb)
        prod[("m_x", b)] = (sX.h, sX.v)

        for i in range(NT):
            def vc(v, b=b, i=i):
                v.wait_ge(*prod[("m_x", b)])
                if b > 0 and i == 0:
                    v.wait_ge(*prod[("m_emm", b - 1, NE - 1, NT - 1)])
                v.tensor_copy(ubf[:], ut[:, i * C:(i + 1) * C]).then_inc(sV.inc(), 1)
            blk.vector(vc)
            prod[("m_cast", b, i)] = (sV.h, sV.v)

            def t_bT(t, b=b, i=i):
                t.wait_ge(sC.h, CIN)
                t.wait_ge(*prod[("m_cast", b, i)])
                if i >= 1:
                    t.wait_ge(*prod[("m_evb", b, i - 1)])
                elif b > 0:
                    t.wait_ge(*prod[("m_evb", b - 1, NT - 1)])
                ins = None
                for ch in range(NCH):
                    ins = t.transpose(pTb[:, ch * 128:(ch + 1) * 128],
                                      ubf[:, ch * 128:(ch + 1) * 128], idb[:])
                ins.then_inc(sT.inc(), 1)
            blk.tensor(t_bT)
            prod[("m_bT", b, i)] = (sT.h, sT.v)

            def vevb(v, b=b, i=i):
                v.wait_ge(*prod[("m_bT", b, i)])
                v.tensor_copy(ufm[:, i * C:(i + 1) * C], pTb[:]).then_inc(sV.inc(), 1)
            blk.vector(vevb)
            prod[("m_evb", b, i)] = (sV.h, sV.v)

            def t_fT(t, b=b, i=i):
                t.wait_ge(*prod["m_init"])
                t.wait_ge(*prod[("m_x", b)])
                if i >= 1:
                    t.wait_ge(*prod[("m_evf", b, i - 1)])
                elif b > 0:
                    t.wait_ge(*prod[("m_evf", b - 1, NT - 1)])
                ins = None
                for ch in range(NCH):
                    ins = t.transpose(pTf[:, ch * 128:(ch + 1) * 128],
                                      ut[:, i * C + ch * 128: i * C + (ch + 1) * 128], idf[:])
                ins.then_inc(sT.inc(), 1)
            blk.tensor(t_fT)
            prod[("m_fT", b, i)] = (sT.h, sT.v)

            def vevf(v, b=b, i=i):
                v.wait_ge(*prod[("m_fT", b, i)])
                if i >= 1:
                    v.wait_ge(*prod[("m_gmm", b, i - 1)])
                v.tensor_copy(uf32[:], pTf[:]).then_inc(sV.inc(), 1)
            blk.vector(vevf)
            prod[("m_evf", b, i)] = (sV.h, sV.v)

            def t_g(t, b=b, i=i):
                t.wait_ge(sC.h, CIN)
                t.wait_ge(*prod[("m_evf", b, i)])
                if i >= 1:
                    t.wait_ge(*prod[("m_sm", b, i - 1)])
                elif b > 0:
                    t.wait_ge(*prod[("m_sm", b - 1, NT - 1)])
                for ch in range(NCH):
                    t.matmul(pg[:], uf32[:, ch * 128:(ch + 1) * 128],
                             ct[:, ch * NE:(ch + 1) * NE], start=(ch == 0), stop=False)
                t.matmul(pg[:], onf[:], ebr[:], start=False, stop=True).then_inc(sT.inc(), 1)
            blk.tensor(t_g)
            prod[("m_gmm", b, i)] = (sT.h, sT.v)

            def v_sm1(v, b=b, i=i):
                v.wait_ge(*prod[("m_gmm", b, i)])
                v.reduce_max(stx[:, 0:1], pg[:], axis=AX, negate=True).then_inc(sV.inc(), 1)
            blk.vector(v_sm1)
            prod[("m_nm", b, i)] = (sV.h, sV.v)

            def s_sm(s, b=b, i=i):
                s.wait_ge(*prod[("m_nm", b, i)])
                if b > 0 and i == 0:
                    s.wait_ge(*prod[("m_eev", b - 1, NE - 1, NT - 1)])
                s.activation(gal[:, i * NE:(i + 1) * NE], pg[:], AF.Exp,
                             bias=stx[:, 0:1], accum_out=stx[:, 1:2]).then_inc(sS.inc(), 1)
            blk.scalar(s_sm)
            prod[("m_exp", b, i)] = (sS.h, sS.v)

            def v_sm2(v, b=b, i=i):
                g = gal[:, i * NE:(i + 1) * NE]
                v.wait_ge(*prod[("m_exp", b, i)])
                if b > 0 and i == 0:
                    v.wait_ge(*prod[("m_fln", b - 1, NT - 1)])
                v.reciprocal(stx[:, 2:3], stx[:, 1:2]).then_inc(sV.inc(), 1)
                v.wait_ge(sV.h, sV.v)
                v.tensor_scalar(g, g, stx[:, 2:3], None, ALU.mult).then_inc(sV.inc(), 1)
                v.wait_ge(sV.h, sV.v)
                v.reduce_max(stx[:, 3:4], g, axis=AX).then_inc(sV.inc(), 1)
                v.wait_ge(sV.h, sV.v)
                v.tensor_scalar(scr[:, 0:NE], g, stx[:, 3:4], None, ALU.is_equal).then_inc(sV.inc(), 1)
                v.wait_ge(sV.h, sV.v)
                v.scalar_tensor_tensor(scr[:, NE:2 * NE], scr[:, 0:NE], -1e9, g, ALU.mult, ALU.add).then_inc(sV.inc(), 1)
                v.wait_ge(sV.h, sV.v)
                v.reduce_max(stx[:, 4:5], scr[:, NE:2 * NE], axis=AX).then_inc(sV.inc(), 1)
                v.wait_ge(sV.h, sV.v)
                v.tensor_scalar(scr[:, 0:NE], g, stx[:, 4:5], None, ALU.is_ge).then_inc(sV.inc(), 1)
                v.wait_ge(sV.h, sV.v)
                v.tensor_tensor(g, g, scr[:, 0:NE], ALU.mult).then_inc(sV.inc(), 1)
            blk.vector(v_sm2)
            prod[("m_sm", b, i)] = (sV.h, sV.v)

        for i in range(NT):
            def t_s(t, b=b, i=i):
                t.wait_ge(sW0.h, WS_IN[0])
                t.wait_ge(sW1.h, WS_IN[1])
                t.wait_ge(sB.h, 16)
                t.wait_ge(*prod[("m_evb", b, i)])
                if i >= 2:
                    t.wait_ge(*prod[("m_sev", b, i - 2)])
                elif b > 0:
                    t.wait_ge(*prod[("m_fln", b - 1, NT - 1)])
                ins = None
                for half in range(2):
                    dst = pe[i % 2][:, half * 512:(half + 1) * 512]
                    for w in range(2):
                        for ch in range(NCH):
                            t.matmul(dst, ufm[:, i * C + ch * 128: i * C + (ch + 1) * 128],
                                     wre[:, w * NCH * C + ch * C + half * 512: w * NCH * C + ch * C + (half + 1) * 512],
                                     start=(w == 0 and ch == 0), stop=False)
                    t.matmul(dst, onb[:], bsr[0:1, half * 512:(half + 1) * 512], start=False, stop=False)
                    ins = t.matmul(dst, onb[:], bsr[0:1, C + half * 512:C + (half + 1) * 512], start=False, stop=True)
                ins.then_inc(sT.inc(), 1)
            blk.tensor(t_s)
            prod[("m_smm", b, i)] = (sT.h, sT.v)

            def v_sev(v, b=b, i=i):
                v.wait_ge(*prod[("m_smm", b, i)])
                if i >= 1:
                    v.wait_ge(*prod[("m_sev", b, i - 1)])
                elif b > 0:
                    v.wait_ge(*prod[("m_fln", b - 1, NT - 1)])
                v.scalar_tensor_tensor(zt[:], pe[i % 2][:], 0.5, ut[:, i * C:(i + 1) * C],
                                       ALU.mult, ALU.add).then_inc(sV.inc(), 1)
                v.wait_ge(sV.h, sV.v)
                v.tensor_tensor(zbf[:, i * C:(i + 1) * C], zt[:], ut[:, i * C:(i + 1) * C], ALU.add).then_inc(sV.inc(), 1)
            blk.vector(v_sev)
            prod[("m_sev", b, i)] = (sV.h, sV.v)

        for e in range(NE):
            sl = e % 2
            def g_we(g, b=b, e=e, sl=sl):
                if e >= 2 or b > 0:
                    pk = ("m_emm", b, e - 2, NT - 1) if e >= 2 else ("m_smm", b, NT - 1)
                    g.wait_ge(*prod[pk])
                    if e < 2 and b > 0:
                        pass
                if e < 2:
                    g.wait_ge(*prod[("m_smm", b, NT - 1)])
                g.dma_start(wre[:, sl * NCH * C:(sl + 1) * NCH * C].rearrange("p (ch d) -> p ch d", d=C),
                            wr_d[e].rearrange("(ch p) d -> p ch d", p=128)).then_inc(sWs[sl].dinc(), 16)
                g.dma_start(brp[0:1, sl * C:(sl + 1) * C], br_d[e:e + 1, :]).then_inc(sWs[sl].dinc(), 16)
            blk.gpsimd(g_we)
            prod[("m_we", b, e)] = (sWs[sl].h, sWs[sl].v)

            for i in range(NT):
                def t_e(t, b=b, e=e, i=i, sl=sl):
                    t.wait_ge(*prod[("m_we", b, e)])
                    if i >= 2:
                        t.wait_ge(*prod[("m_eev", b, e, i - 2)])
                    elif e >= 1:
                        t.wait_ge(*prod[("m_eev", b, e - 1, NT - 2 + i)])
                    else:
                        t.wait_ge(*prod[("m_sev", b, NT - 2 + i)])
                    ins = None
                    for half in range(2):
                        dst = pe[i % 2][:, half * 512:(half + 1) * 512]
                        for ch in range(NCH):
                            t.matmul(dst, ufm[:, i * C + ch * 128: i * C + (ch + 1) * 128],
                                     wre[:, sl * NCH * C + ch * C + half * 512: sl * NCH * C + ch * C + (half + 1) * 512],
                                     start=(ch == 0), stop=False)
                        ins = t.matmul(dst, onb[:], brp[0:1, sl * C + half * 512: sl * C + (half + 1) * 512],
                                       start=False, stop=True)
                    ins.then_inc(sT.inc(), 1)
                blk.tensor(t_e)
                prod[("m_emm", b, e, i)] = (sT.h, sT.v)

                def v_eev(v, b=b, e=e, i=i):
                    v.wait_ge(*prod[("m_emm", b, e, i)])
                    v.scalar_tensor_tensor(zbf[:, i * C:(i + 1) * C], pe[i % 2][:],
                                           gal[:, i * NE + e:i * NE + e + 1],
                                           zbf[:, i * C:(i + 1) * C], ALU.mult, ALU.add).then_inc(sV.inc(), 1)
                blk.vector(v_eev)
                prod[("m_eev", b, e, i)] = (sV.h, sV.v)

        for i in range(NT):
            def v_f(v, b=b, i=i):
                v.wait_ge(*prod[("m_eev", b, NE - 1, i)])
                if i >= 1:
                    v.wait_ge(*prod[("m_fln", b, i - 1)])
                elif b > 0:
                    v.wait_ge(*prod[("m_fln", b - 1, NT - 1)])
                v.tensor_copy(zt[:], zbf[:, i * C:(i + 1) * C]).then_inc(sV.inc(), 1)
            blk.vector(v_f)
            prod[("m_z", b, i)] = (sV.h, sV.v)

            ews = [prod[("m_z", b, i)]]
            if i >= 2:
                ews.append(prod[("m_ydma", b, i // 2 - 1)])
            elif b > 0:
                ews.append(prod[("m_ydma", b - 1, NT // 2 - 1)])
            ln_emit(blk, sLN, zt[:], yst[:, (i % 2) * C:(i % 2 + 1) * C], scr[:], stx[:, 8:12], C,
                    prod, ("m_fln", b, i), extra_wait=ews)

            if i % 2 == 1:
                def sydma(sync, b=b, i=i):
                    sync.wait_ge(*prod[("m_fln", b, i - 1)])
                    sync.wait_ge(*prod[("m_fln", b, i)])
                    sync.dma_start(y_ap[b].rearrange("(n p) c -> p n c", p=128)[:, i - 1:i + 1],
                                   yst[:].rearrange("p (n c) -> p n c", c=C)).then_inc(sO.dinc(), 16)
                blk.sync(sydma)
                prod[("m_ydma", b, i // 2)] = (sO.h, sO.v)

    def sfin(sync):
        sync.wait_ge(sO.h, sO.v)
    blk.sync(sfin)
    return [v for v in locals().values() if isinstance(v, Sem)]


def l0moe_host_inputs(centroids, expert_biases, Wr, br, Ws, bs, pref="m_"):
    import ml_dtypes
    C = 1024
    return {
        pref + "WrT": np.ascontiguousarray(np.transpose(Wr, (0, 2, 1))),
        pref + "WsT": np.ascontiguousarray(np.transpose(Ws, (0, 2, 1))),
        pref + "br": np.ascontiguousarray(br),
        pref + "bs": np.ascontiguousarray(bs.reshape(1, 2 * C)),
        pref + "centT": np.ascontiguousarray(centroids.T),
        pref + "eb": expert_biases.reshape(1, -1).astype(np.float32),
        pref + "ident": np.eye(128, dtype=ml_dtypes.bfloat16),
        pref + "onesb": np.ones((1, 128), dtype=ml_dtypes.bfloat16),
        pref + "onesf": np.ones((1, 128), dtype=np.float32),
    }


def build_l1attn(nc, es, blk, prod, B, pref="w_", x_ap=None, y_ap=None, ses=None):
    ses = ses or es
    """L1 windowed bond/damage attention + residual + postLN.
    Host inputs (see l1attn_host_inputs): w_wdisp [1024,256], w_bdisp2 [128,2],
    w_wval [1024,1024], w_bval [1,1024], w_wcproj [1024,1024], w_bcproj [1,1024],
    w_bd [128, 640] (5 blockdiag 16x16 mats: state, strain, state_i, state_j, dmg),
    w_bdout [128, 16] (bond-out | dmg-out), w_relT [16,32], w_wpos [16,16],
    w_rep16 [16,128], w_reps [128, 8] (bstate,bstr,bdmg,bbond,bdout cols),
    w_rc [1? bf16 [128, 2048] replicated 1/cnt, w_ident [128,128] bf16, w_ones [1,128] bf16.
    """
    C = 1024
    NT = 16
    T = NT * 128
    NH = 16
    NCH = 8
    PAD = 31
    TP = T + PAD
    J = 32

    wd_d = nc.dram_tensor(pref + "wdisp", [C, 256], F32, kind="ExternalInput").ap()
    bd2_d = nc.dram_tensor(pref + "bdisp2", [128, 2], F32, kind="ExternalInput").ap()
    wv_d = nc.dram_tensor(pref + "wval", [C, C], F32, kind="ExternalInput").ap()
    bv_d = nc.dram_tensor(pref + "bval", [1, C], F32, kind="ExternalInput").ap()
    wc_d = nc.dram_tensor(pref + "wcproj", [C, C], F32, kind="ExternalInput").ap()
    bc_d = nc.dram_tensor(pref + "bcproj", [1, C], F32, kind="ExternalInput").ap()
    bd_d = nc.dram_tensor(pref + "bd", [128, 640], F32, kind="ExternalInput").ap()
    bo_d = nc.dram_tensor(pref + "bdout", [128, 16], F32, kind="ExternalInput").ap()
    rt_d = nc.dram_tensor(pref + "relT", [16, 32], F32, kind="ExternalInput").ap()
    wp_d = nc.dram_tensor(pref + "wpos", [16, 16], F32, kind="ExternalInput").ap()
    rp_d = nc.dram_tensor(pref + "rep16", [16, 128], F32, kind="ExternalInput").ap()
    rs_d = nc.dram_tensor(pref + "reps", [128, 8], F32, kind="ExternalInput").ap()
    rc_d = nc.dram_tensor(pref + "rc", [128, 32], F32, kind="ExternalInput").ap()
    id_d = nc.dram_tensor(pref + "ident", [128, 128], BF16, kind="ExternalInput").ap()
    on_d = nc.dram_tensor(pref + "ones", [1, 128], BF16, kind="ExternalInput").ap()
    skew_d = nc.dram_tensor(pref + "skew", [2, 128, 2048], BF16, kind="Internal").ap()
    ao_d = nc.dram_tensor(pref + "aod", [B, T, C], BF16, kind="Internal").ap()

    E = es.enter_context
    wdsp = E(nc.sbuf_tensor(pref + "wdsp", [128, NCH * 256], BF16))
    wval = E(nc.sbuf_tensor(pref + "wval_s", [128, NCH * C], BF16))
    wcpj = E(nc.sbuf_tensor(pref + "wcpj", [128, NCH * C], BF16))
    bd2 = E(nc.sbuf_tensor(pref + "bd2", [128, 2], F32))
    bvr = E(nc.sbuf_tensor(pref + "bvr", [1, C], BF16))
    bcr = E(nc.sbuf_tensor(pref + "bcr", [1, C], BF16))
    wbd = E(nc.sbuf_tensor(pref + "wbd", [128, 640], BF16))
    wbo = E(nc.sbuf_tensor(pref + "wbo", [128, 16], BF16))
    relT = E(nc.sbuf_tensor(pref + "relT_s", [16, 32], BF16))
    wpos = E(nc.sbuf_tensor(pref + "wpos_s", [16, 16], BF16))
    rep16 = E(nc.sbuf_tensor(pref + "rep16_s", [16, 128], BF16))
    reps = E(nc.sbuf_tensor(pref + "reps_s", [128, 8], F32))
    rc = E(nc.sbuf_tensor(pref + "rc_s", [128, 32], BF16))
    idb = E(nc.sbuf_tensor(pref + "idb", [128, 128], BF16))
    ones = E(nc.sbuf_tensor(pref + "ones_s", [1, 128], BF16))
    gsm = E(nc.sbuf_tensor(pref + "gsm", [16, 32], BF16))
    G = E(nc.sbuf_tensor(pref + "G", [128, 32], F32))
    negGx = E(nc.sbuf_tensor(pref + "negGx", [128, 4096], BF16))
    xti = E(nc.sbuf_tensor(pref + "xti", [128, 2 * C], F32))
    xbf = E(nc.sbuf_tensor(pref + "xbf", [128, 2 * C], BF16))
    xfm = E(nc.sbuf_tensor(pref + "xfm", [128, 2 * C], BF16))
    vsb = E(nc.sbuf_tensor(pref + "vsb", [128, NT * NH * 65], BF16))
    aow = E(nc.sbuf_tensor(pref + "aow", [128, 2 * 512], BF16))
    aot = E(nc.sbuf_tensor(pref + "aot", [128, 2 * C], BF16))
    dsp = E(nc.sbuf_tensor(pref + "dsp", [128, TP], BF16))
    Ast = E(nc.sbuf_tensor(pref + "Ast", [128, TP], BF16))
    A1 = E(nc.sbuf_tensor(pref + "A1", [128, TP], BF16))
    Adm = E(nc.sbuf_tensor(pref + "Adm", [128, TP], BF16))
    stt_ = None
    Ebuf = E(nc.sbuf_tensor(pref + "Ebuf", [128, TP], BF16))
    Fbuf = E(nc.sbuf_tensor(pref + "Fbuf", [128, T], BF16))
    Sa = E(nc.sbuf_tensor(pref + "Sa", [128, TP], BF16))
    Sb = Ebuf
    grid = E(nc.sbuf_tensor(pref + "grid", [128, 2 * 8192], BF16))
    BLs = E(nc.sbuf_tensor(pref + "BLs", [128, 8 * 256], BF16))
    DLs = E(nc.sbuf_tensor(pref + "DLs", [128, 8 * 256], BF16))
    ptk = E(nc.sbuf_tensor(pref + "ptk", [128, 2 * 256], BF16))
    p2d = E(nc.sbuf_tensor(pref + "p2d", [128, 2 * 2048], BF16))
    Pt = E(nc.sbuf_tensor(pref + "Pt", [128, 2 * 2048], BF16))
    zt = E(nc.sbuf_tensor(pref + "zt", [128, C], F32))
    scr = E(nc.sbuf_tensor(pref + "scr", [128, C], F32))
    stx = E(nc.sbuf_tensor(pref + "stx", [128, 8], F32))
    yst = E(nc.sbuf_tensor(pref + "yst", [128, C], F32))

    pool = E(nc.psum_tensor(pref + "pool", [128, 4096], F32))
    poolb = pool[:].bitcast(BF16)
    pT = poolb[:, 0:1024]
    pP = [poolb[:, 0:1024], poolb[:, 1024:2048]]
    pj = pool[:, 512:1536]
    pRedB = pool[:, 1024:1280]
    pRedD = pool[:, 1536:1792]
    pBD = pool[:, 3072:4096]
    pO = [pool[0:128, 2048 + i * 512: 2048 + i * 512 + 65] for i in range(2)]
    pGm = pool[0:16, 2048:2080]
    pGr = pool[:, 2560:2592]

    sW = Sem(ses, nc, pref + "sW")
    sC = Sem(ses, nc, pref + "sC")
    sX0 = Sem(ses, nc, pref + "sX0")
    sX1 = Sem(ses, nc, pref + "sX1")
    sXp = [sX0, sX1]
    sT = Sem(ses, nc, pref + "sT")
    sV = Sem(ses, nc, pref + "sV")
    sS = Sem(ses, nc, pref + "sS")
    sG = Sem(ses, nc, pref + "sG")
    sD0 = Sem(ses, nc, pref + "sD0")
    sD1 = Sem(ses, nc, pref + "sD1")
    sDp = [sD0, sD1]
    sAO0 = Sem(ses, nc, pref + "sAO0")
    sAO1 = Sem(ses, nc, pref + "sAO1")
    sAOp = [sAO0, sAO1]
    sAOr = Sem(ses, nc, pref + "sAOr")
    sO = Sem(ses, nc, pref + "sO")
    sLN = Sem(ses, nc, pref + "sLN")
    sXK = [Sem(ses, nc, pref + "sXK%d" % q) for q in range(4)]
    sAK = [Sem(ses, nc, pref + "sAK%d" % q) for q in range(4)]

    def g0(g):
        g.dma_start(wdsp[:].rearrange("p (ch d) -> p ch d", d=256),
                    wd_d.rearrange("(ch p) d -> p ch d", p=128)).then_inc(sW.dinc(), 16)
        g.dma_start(wval[:].rearrange("p (ch d) -> p ch d", d=C),
                    wv_d.rearrange("(ch p) d -> p ch d", p=128)).then_inc(sW.dinc(), 16)
        g.dma_start(wcpj[:].rearrange("p (ch d) -> p ch d", d=C),
                    wc_d.rearrange("(ch p) d -> p ch d", p=128)).then_inc(sW.dinc(), 16)
        g.dma_start(bvr[:], bv_d).then_inc(sW.dinc(), 16)
        g.dma_start(bcr[:], bc_d).then_inc(sW.dinc(), 16)
        g.dma_start(wbd[:], bd_d).then_inc(sW.dinc(), 16)
        g.dma_start(wbo[:], bo_d).then_inc(sW.dinc(), 16)
        g.dma_start(relT[:], rt_d).then_inc(sW.dinc(), 16)
        g.dma_start(wpos[:], wp_d).then_inc(sW.dinc(), 16)
        g.dma_start(rep16[:], rp_d).then_inc(sW.dinc(), 16)
        g.dma_start(rc[:], rc_d).then_inc(sW.dinc(), 16)
    blk.gpsimd(g0)
    W_ALL = sW.v

    def s0(sync):
        sync.dma_start(bd2[:], bd2_d).then_inc(sC.dinc(), 16)
        sync.dma_start(reps[:], rs_d).then_inc(sC.dinc(), 16)
        sync.dma_start(idb[:], id_d).then_inc(sC.dinc(), 16)
        sync.dma_start(ones[:], on_d).then_inc(sC.dinc(), 16)
    blk.sync(s0)
    CIN = sC.v

    def v_init(v):
        v.wait_ge(sC.h, CIN)
        v.memset(grid[:, 0:2048], 0.0)
        v.memset(vsb[:].rearrange("p (n h e) -> p n h e", h=NH, e=65)[:, :, :, 64:65], 1.0)
        v.memset(dsp[:, 0:PAD], 0.0)
        v.memset(Ast[:, 0:PAD], 0.0)
        v.memset(A1[:, 0:PAD], 0.0)
        v.memset(Adm[:, 0:PAD], 0.0)
        v.memset(Ebuf[:, 0:PAD], 0.0)
        v.memset(Sa[:, 0:PAD], 0.0).then_inc(sV.inc(), 1)
    blk.vector(v_init)
    prod["w_init"] = (sV.h, sV.v)

    def s_zskew(sync):
        sync.wait_ge(*prod["w_init"])
        sync.dma_start(skew_d[0], grid[:, 0:2048]).then_inc(sD0.dinc(), 16)
        sync.dma_start(skew_d[1], grid[:, 0:2048]).then_inc(sD1.dinc(), 16)
    blk.sync(s_zskew)
    prod["w_zskew"] = None

    def t_G(t):
        t.wait_ge(sW.h, W_ALL)
        t.matmul(pGm[:], wpos[:], relT[:], start=True, stop=True).then_inc(sT.inc(), 1)
    blk.tensor(t_G)
    prod["w_G1"] = (sT.h, sT.v)

    def v_G1(v):
        v.wait_ge(*prod[("w_G1")])
        v.tensor_copy(gsm[:], pGm[:]).then_inc(sV.inc(), 1)
    blk.vector(v_G1)
    prod["w_G2"] = (sV.h, sV.v)

    def t_G2(t):
        t.wait_ge(*prod["w_G2"])
        t.matmul(pGr[:], rep16[:], gsm[:], start=True, stop=True).then_inc(sT.inc(), 1)
    blk.tensor(t_G2)
    prod["w_G3"] = (sT.h, sT.v)

    def v_G2(v):
        v.wait_ge(*prod["w_G3"])
        v.tensor_scalar(G[:], pGr[:], reps[:, 1:2], None, ALU.add).then_inc(sV.inc(), 1)
    blk.vector(v_G2)
    prod["w_G"] = (sV.h, sV.v)

    def v_Gx(v):
        v.wait_ge(*prod["w_G"])
        Gb = G[:].unsqueeze(2).broadcast_to([128, 32, 128])
        v.tensor_scalar(negGx[:].rearrange("p (j t) -> p j t", t=128),
                        Gb, -1.0, None, ALU.mult).then_inc(sV.inc(), 1)
    blk.vector(v_Gx)
    prod["w_Gx"] = (sV.h, sV.v)

    def v_reps2(v):
        v.wait_ge(sC.h, CIN)
        v.tensor_scalar(reps[:, 5:6], reps[:, 4:5], 0.5, None, ALU.mult)
        v.tensor_scalar(reps[:, 6:7], reps[:, 3:4], -5.0, None, ALU.add).then_inc(sV.inc(), 1)
    blk.vector(v_reps2)
    prod["w_reps2"] = (sV.h, sV.v)

    for b in range(B):
        if b > 0:
            def v_ones(v):
                v.memset(vsb[:].rearrange("p (n h e) -> p n h e", h=NH, e=65)[:, :, :, 64:65],
                         1.0).then_inc(sV.inc(), 1)
            blk.vector(v_ones)

        for i in range(NT):
            k = b * NT + i
            pp = k % 2

            def s_x(sync, b=b, i=i, pp=pp, k=k):
                if i < 2 and b > 0:
                    sync.wait_ge(*prod[("w_z", b - 1, NT - 1)])
                if k >= 2:
                    sync.wait_ge(*prod[("w_xT", k - 2)])
                sync.dma_start(xti[:, pp * C:(pp + 1) * C],
                               x_ap[b].rearrange("(n p) c -> p n c", p=128)[:, i]).then_inc(sXp[pp].dinc(), 16)
            blk.sync(s_x)
            prod[("w_x", k)] = (sXp[pp].h, sXp[pp].v)

            def v_c(v, b=b, i=i, pp=pp, k=k):
                v.wait_ge(*prod[("w_x", k)])
                if k >= 2:
                    v.wait_ge(*prod[("w_xT", k - 2)])
                v.tensor_copy(xbf[:, pp * C:(pp + 1) * C], xti[:, pp * C:(pp + 1) * C]).then_inc(sV.inc(), 1)
            blk.vector(v_c)
            prod[("w_cast", k)] = (sV.h, sV.v)

            def t_xT(t, pp=pp, k=k, b=b):
                t.wait_ge(sC.h, CIN)
                t.wait_ge(*prod[("w_cast", k)])
                if k % NT == 0 and b > 0:
                    t.wait_ge(*prod[("w_aev", b - 1, NT - 1)])
                    t.wait_ge(*prod[("w_Ptev", 2 * b - 1, NT - 1)])
                if k >= 1:
                    t.wait_ge(*prod[("w_evx", k - 1)])
                ins = None
                for ch in range(NCH):
                    ins = t.transpose(pT[:, ch * 128:(ch + 1) * 128],
                                      xbf[:, pp * C + ch * 128: pp * C + (ch + 1) * 128], idb[:])
                ins.then_inc(sT.inc(), 1)
            blk.tensor(t_xT)
            prod[("w_xT", k)] = (sT.h, sT.v)

            def v_evx(v, pp=pp, k=k):
                v.wait_ge(*prod[("w_xT", k)])
                if k >= 2:
                    v.wait_ge(*prod[("w_vmm", k - 2)])
                v.tensor_copy(xfm[:, pp * C:(pp + 1) * C], pT[:]).then_inc(sV.inc(), 1)
            blk.vector(v_evx)
            prod[("w_evx", k)] = (sV.h, sV.v)

            def t_v(t, pp=pp, k=k, b=b, i=i):
                t.wait_ge(sW.h, W_ALL)
                t.wait_ge(*prod[("w_evx", k)])
                if i == 0 and b > 0:
                    t.wait_ge(*prod[("w_z", b - 1, NT - 1)])
                    t.wait_ge(*prod[("w_Ptev", 2 * b - 1, NT - 1)])
                if k >= 1:
                    t.wait_ge(*prod[("w_vev", k - 1)])
                ins = None
                for half in range(2):
                    dst = pj[:, half * 512:(half + 1) * 512]
                    for ch in range(NCH):
                        t.matmul(dst, xfm[:, pp * C + ch * 128: pp * C + (ch + 1) * 128],
                                 wval[:, ch * C + half * 512: ch * C + (half + 1) * 512],
                                 start=(ch == 0), stop=False)
                    ins = t.matmul(dst, ones[:], bvr[0:1, half * 512:(half + 1) * 512],
                                   start=False, stop=True)
                ins.then_inc(sT.inc(), 1)
            blk.tensor(t_v)
            prod[("w_vmm", k)] = (sT.h, sT.v)

            def v_vev(v, k=k, b=b, i=i):
                v.wait_ge(*prod[("w_vmm", k)])
                v.wait_ge(*prod["w_init"])
                if b > 0 and i == 0:
                    v.wait_ge(*prod[("w_bandmm", (b - 1) * 2 + 1, NT - 1, 7)])
                dst = vsb[:, i * NH * 65:(i + 1) * NH * 65].rearrange("p (h e) -> p h e", e=65)[:, :, 0:64]
                v.tensor_copy(dst, pj[:].rearrange("p (h e) -> p h e", e=64)).then_inc(sV.inc(), 1)
            blk.vector(v_vev)
            prod[("w_vev", k)] = (sV.h, sV.v)

        for grp in range(2):
            gk = b * 2 + grp
            for i in range(NT):
                k = b * NT + i

                def s_x2(sync, b=b, i=i, grp=grp, gk=gk):
                    pp2 = i % 2
                    if i < 2 and gk % 2 == 0:
                        sync.wait_ge(*prod[("w_cast", b * NT + NT - 1)])
                    if i >= 2:
                        sync.wait_ge(*prod[("w_xT2", gk, i - 2)])
                    elif gk > 0:
                        sync.wait_ge(*prod[("w_xT2", gk - 1, NT - 2 + i)])
                    sync.dma_start(xti[:, pp2 * C:(pp2 + 1) * C],
                                   x_ap[b].rearrange("(n p) c -> p n c", p=128)[:, i]).then_inc(sXp[pp2].dinc(), 16)
                blk.sync(s_x2)
                prod[("w_x2", gk, i)] = (sXp[i % 2].h, sXp[i % 2].v)

                def v_c2(v, i=i, gk=gk):
                    pp2 = i % 2
                    v.wait_ge(*prod[("w_x2", gk, i)])
                    if i >= 2:
                        v.wait_ge(*prod[("w_xT2", gk, i - 2)])
                    elif gk > 0:
                        v.wait_ge(*prod[("w_xT2", gk - 1, NT - 2 + i)])
                    v.tensor_copy(xbf[:, pp2 * C:(pp2 + 1) * C], xti[:, pp2 * C:(pp2 + 1) * C]).then_inc(sV.inc(), 1)
                blk.vector(v_c2)
                prod[("w_cast2", gk, i)] = (sV.h, sV.v)

                def t_xT2(t, i=i, gk=gk):
                    pp2 = i % 2
                    t.wait_ge(*prod[("w_cast2", gk, i)])
                    if i >= 1:
                        t.wait_ge(*prod[("w_evx2", gk, i - 1)])
                    elif gk > 0:
                        t.wait_ge(*prod[("w_evx2", gk - 1, NT - 1)])
                        t.wait_ge(*prod[("w_Ptev", gk - 1, NT - 1)])
                    ins = None
                    for ch in range(NCH):
                        ins = t.transpose(pT[:, ch * 128:(ch + 1) * 128],
                                          xbf[:, pp2 * C + ch * 128: pp2 * C + (ch + 1) * 128], idb[:])
                    ins.then_inc(sT.inc(), 1)
                blk.tensor(t_xT2)
                prod[("w_xT2", gk, i)] = (sT.h, sT.v)

                def v_evx2(v, i=i, gk=gk):
                    pp2 = i % 2
                    v.wait_ge(*prod[("w_xT2", gk, i)])
                    if i >= 2:
                        v.wait_ge(*prod[("w_dmm", gk, i - 2)])
                    elif gk > 0:
                        v.wait_ge(*prod[("w_dmm", gk - 1, NT - 2 + i)])
                    v.tensor_copy(xfm[:, pp2 * C:(pp2 + 1) * C], pT[:]).then_inc(sV.inc(), 1)
                blk.vector(v_evx2)
                prod[("w_evx2", gk, i)] = (sV.h, sV.v)

                def t_d(t, i=i, grp=grp, gk=gk):
                    pp2 = i % 2
                    t.wait_ge(sW.h, W_ALL)
                    t.wait_ge(*prod[("w_evx2", gk, i)])
                    if i >= 2:
                        t.wait_ge(*prod[("w_dev", gk, i - 2)])
                    elif gk > 0:
                        t.wait_ge(*prod[("w_dev", gk - 1, NT - 2 + i)])
                        t.wait_ge(*prod[("w_F", gk - 1)])
                    ins = None
                    for ch in range(NCH):
                        ins = t.matmul(pBD[:, (i % 2) * 512:(i % 2) * 512 + 128],
                                       wdsp[:, ch * 256 + grp * 128: ch * 256 + (grp + 1) * 128],
                                       xfm[:, pp2 * C + ch * 128: pp2 * C + (ch + 1) * 128],
                                       start=(ch == 0), stop=(ch == NCH - 1))
                    ins.then_inc(sT.inc(), 1)
                blk.tensor(t_d)
                prod[("w_dmm", gk, i)] = (sT.h, sT.v)

                def v_dev(v, i=i, grp=grp, gk=gk):
                    v.wait_ge(*prod[("w_dmm", gk, i)])
                    v.wait_ge(*prod["w_init"])
                    if gk > 0 and i == 0:
                        v.wait_ge(*prod[("w_bdm", gk - 1, 5)])
                    v.tensor_scalar(dsp[:, PAD + i * 128:PAD + (i + 1) * 128],
                                    pBD[:, (i % 2) * 512:(i % 2) * 512 + 128],
                                    bd2[:, grp:grp + 1], None, ALU.add).then_inc(sV.inc(), 1)
                blk.vector(v_dev)
                prod[("w_dev", gk, i)] = (sV.h, sV.v)
            prod[("w_disp", gk)] = prod[("w_dev", gk, NT - 1)]

            for bi, (wslice, dstbuf) in enumerate([(0, Ast), (1, A1), (4, Adm)]):
                for rnd in range(2):
                    def t_bd(t, bi=bi, wslice=wslice, gk=gk, rnd=rnd):
                        t.wait_ge(*prod[("w_disp", gk)])
                        if bi > 0 or rnd > 0:
                            t.wait_ge(*prod[("w_bdev", gk, bi * 2 + rnd - 1)])
                        ins = None
                        for q in range(2):
                            ins = t.matmul(pBD[:, q * 512:(q + 1) * 512],
                                           wbd[:, wslice * 128:(wslice + 1) * 128],
                                           dsp[:, PAD + (rnd * 2 + q) * 512:PAD + (rnd * 2 + q + 1) * 512],
                                           start=True, stop=True)
                        ins.then_inc(sT.inc(), 1)
                    blk.tensor(t_bd)
                    prod[("w_bdm", gk, bi * 2 + rnd)] = (sT.h, sT.v)

                    def v_bdev(v, bi=bi, dstbuf=dstbuf, gk=gk, rnd=rnd):
                        v.wait_ge(*prod[("w_bdm", gk, bi * 2 + rnd)])
                        if bi == 2 and rnd == 0 and gk > 0:
                            v.wait_ge(*prod[("w_asm", gk - 1, NT - 1, 1)])
                        v.tensor_copy(dstbuf[:, PAD + rnd * 1024:PAD + (rnd + 1) * 1024],
                                      pBD[:, 0:1024]).then_inc(sV.inc(), 1)
                    blk.vector(v_bdev)
                    prod[("w_bdev", gk, bi * 2 + rnd)] = (sV.h, sV.v)

            def v_slide(v, gk=gk):
                v.wait_ge(*prod[("w_bdev", gk, 1)])
                v.wait_ge(sW.h, W_ALL)
                v.tensor_tensor(Sa[:, PAD:PAD + T], Ast[:, PAD:PAD + T], Ast[:, PAD - 1:PAD - 1 + T], ALU.add).then_inc(sV.inc(), 1)
                v.wait_ge(sV.h, sV.v)
                v.tensor_tensor(Sb[:, PAD:PAD + T], Sa[:, PAD:PAD + T], Sa[:, PAD - 2:PAD - 2 + T], ALU.add).then_inc(sV.inc(), 1)
                v.wait_ge(sV.h, sV.v)
                v.tensor_tensor(Sa[:, PAD:PAD + T], Sb[:, PAD:PAD + T], Sb[:, PAD - 4:PAD - 4 + T], ALU.add).then_inc(sV.inc(), 1)
                v.wait_ge(sV.h, sV.v)
                v.tensor_tensor(Sb[:, PAD:PAD + T], Sa[:, PAD:PAD + T], Sa[:, PAD - 8:PAD - 8 + T], ALU.add).then_inc(sV.inc(), 1)
                v.wait_ge(sV.h, sV.v)
                v.tensor_tensor(Sa[:, PAD:PAD + T], Sb[:, PAD:PAD + T], Sb[:, PAD - 16:PAD - 16 + T], ALU.add).then_inc(sV.inc(), 1)
                v.wait_ge(sV.h, sV.v)
                v.tensor_tensor(Sb[:, PAD:PAD + 32], Sa[:, PAD:PAD + 32], rc[:], ALU.mult).then_inc(sV.inc(), 1)
                v.wait_ge(sV.h, sV.v)
                v.tensor_scalar(Sb[:, PAD + 32:PAD + T], Sa[:, PAD + 32:PAD + T], 1.0 / 32.0, None, ALU.mult).then_inc(sV.inc(), 1)
                v.wait_ge(sV.h, sV.v)
                v.scalar_tensor_tensor(Sa[:, PAD:PAD + T], Ast[:, PAD:PAD + T], -1.0, Sb[:, PAD:PAD + T], ALU.mult, ALU.add).then_inc(sV.inc(), 1)
                v.wait_ge(sV.h, sV.v)
                v.tensor_scalar(Sa[:, PAD:PAD + T], Sa[:, PAD:PAD + T], reps[:, 0:1], None, ALU.add).then_inc(sV.inc(), 1)
            blk.vector(v_slide)
            prod[("w_state", gk)] = (sV.h, sV.v)

            for rnd in range(2):
                def t_a3(t, gk=gk, rnd=rnd):
                    t.wait_ge(*prod[("w_state", gk)])
                    if rnd > 0:
                        t.wait_ge(*prod[("w_E", gk, 0)])
                    ins = None
                    for q in range(2):
                        ins = t.matmul(pBD[:, q * 512:(q + 1) * 512],
                                       wbd[:, 3 * 128:4 * 128],
                                       Sa[:, PAD + (rnd * 2 + q) * 512:PAD + (rnd * 2 + q + 1) * 512],
                                       start=True, stop=True)
                    ins.then_inc(sT.inc(), 1)
                blk.tensor(t_a3)
                prod[("w_a3m", gk, rnd)] = (sT.h, sT.v)

                def v_E(v, gk=gk, rnd=rnd):
                    v.wait_ge(*prod[("w_a3m", gk, rnd)])
                    if rnd == 0 and gk > 0:
                        v.wait_ge(*prod[("w_asm", gk - 1, NT - 1, 0)])
                    v.tensor_tensor(Ebuf[:, PAD + rnd * 1024:PAD + (rnd + 1) * 1024],
                                    A1[:, PAD + rnd * 1024:PAD + (rnd + 1) * 1024],
                                    pBD[:, 0:1024], ALU.add).then_inc(sV.inc(), 1)
                blk.vector(v_E)
                prod[("w_E", gk, rnd)] = (sV.h, sV.v)
            for rnd in range(2):
                def t_asi(t, gk=gk, rnd=rnd):
                    t.wait_ge(*prod[("w_E", gk, 1)])
                    if rnd > 0:
                        t.wait_ge(*prod[("w_F", gk, 0)])
                    ins = None
                    for q in range(2):
                        ins = t.matmul(pBD[:, q * 512:(q + 1) * 512],
                                       wbd[:, 2 * 128:3 * 128],
                                       Sa[:, PAD + (rnd * 2 + q) * 512:PAD + (rnd * 2 + q + 1) * 512],
                                       start=True, stop=True)
                    ins.then_inc(sT.inc(), 1)
                blk.tensor(t_asi)
                prod[("w_asim", gk, rnd)] = (sT.h, sT.v)

                def v_F(v, gk=gk, rnd=rnd):
                    v.wait_ge(*prod[("w_asim", gk, rnd)])
                    v.tensor_tensor(Fbuf[:, rnd * 1024:(rnd + 1) * 1024], pBD[:, 0:1024],
                                    A1[:, PAD + rnd * 1024:PAD + (rnd + 1) * 1024], ALU.subtract).then_inc(sV.inc(), 1)
                blk.vector(v_F)
                prod[("w_F", gk, rnd)] = (sV.h, sV.v)
            prod[("w_F", gk)] = prod[("w_F", gk, 1)]

            def emit_B(i, gk=gk, b=b, grp=grp):
                pp = i % 2

                def s_rb(sync, i=i, gk=gk, pp=pp):
                    sync.wait_ge(*prod[("w_skw", gk, i)])
                    if i >= 2:
                        sync.wait_ge(*prod[("w_PtT", gk, i - 2)])
                    elif gk > 0:
                        sync.wait_ge(*prod[("w_PtT", gk - 1, NT - 2 + i)])
                    sync.dma_start(p2d[:, pp * 2048:(pp + 1) * 2048], skew_d[pp]).then_inc(sDp[pp].dinc(), 16)
                blk.sync(s_rb)
                prod[("w_rb", gk, i)] = (sDp[pp].h, sDp[pp].v)

                def t_PtT(t, i=i, gk=gk, pp=pp, b=b):
                    t.wait_ge(*prod[("w_rb", gk, i)])
                    if i == 0:
                        t.wait_ge(*prod[("w_evx2", gk, NT - 1)])
                        t.wait_ge(*prod[("w_vev", b * NT + NT - 1)])
                    if i >= 1:
                        t.wait_ge(*prod[("w_Ptev", gk, i - 1)])
                    elif gk > 0:
                        t.wait_ge(*prod[("w_Ptev", gk - 1, NT - 1)])
                    ins = None
                    for h in range(8):
                        for ab in range(2):
                            srcv = p2d[:, pp * 2048:(pp + 1) * 2048].rearrange(
                                "p (tl h) -> p tl h", h=8)[:, ab * 128:(ab + 1) * 128, h]
                            idx = h * 2 + ab
                            ins = t.transpose(pP[idx // 8][:, (idx % 8) * 128:(idx % 8 + 1) * 128],
                                              srcv, idb[:])
                    ins.then_inc(sT.inc(), 1)
                blk.tensor(t_PtT)
                prod[("w_PtT", gk, i)] = (sT.h, sT.v)

                def v_Ptev(v, i=i, gk=gk, pp=pp):
                    v.wait_ge(*prod[("w_PtT", gk, i)])
                    if i >= 2:
                        v.wait_ge(*prod[("w_bandmm", gk, i - 2, 7)])
                    elif gk > 0:
                        v.wait_ge(*prod[("w_bandmm", gk - 1, NT - 2 + i, 7)])
                    v.tensor_copy(Pt[:, pp * 2048:pp * 2048 + 1024], pP[0][:]).then_inc(sV.inc(), 1)
                    v.tensor_copy(Pt[:, pp * 2048 + 1024:(pp + 1) * 2048], pP[1][:]).then_inc(sV.inc(), 1)
                blk.vector(v_Ptev)
                prod[("w_Ptev", gk, i)] = (sV.h, sV.v)

                for h in range(8):
                    hG = grp * 8 + h

                    def t_band(t, i=i, gk=gk, pp=pp, h=h, hG=hG, b=b):
                        if (gk, i, h) == (0, 0, 0):
                            t.wait_ge(*prod["w_G"])
                        t.wait_ge(*prod[("w_Ptev", gk, i)])
                        t.wait_ge(*prod[("w_vev", b * NT + i)])
                        if i > 0:
                            t.wait_ge(*prod[("w_vev", b * NT + i - 1)])
                        key = ("w_norm", gk, i, h - 2)
                        if h >= 2:
                            t.wait_ge(*prod[key])
                        elif i >= 1:
                            t.wait_ge(*prod[("w_norm", gk, i - 1, 6 + h)])
                        elif gk > 0:
                            t.wait_ge(*prod[("w_norm", gk - 1, NT - 1, 6 + h)])
                        PtA = Pt[:, pp * 2048 + h * 256: pp * 2048 + h * 256 + 128]
                        PtB = Pt[:, pp * 2048 + h * 256 + 128: pp * 2048 + h * 256 + 256]
                        ins = None
                        if i > 0:
                            vA = vsb[:, (i - 1) * NH * 65 + hG * 65:(i - 1) * NH * 65 + (hG + 1) * 65]
                            ins = t.matmul(pO[h % 2], PtA, vA, start=True, stop=False)
                        vB = vsb[:, i * NH * 65 + hG * 65:i * NH * 65 + (hG + 1) * 65]
                        ins = t.matmul(pO[h % 2], PtB, vB, start=(i == 0), stop=True)
                        ins.then_inc(sT.inc(), 1)
                    blk.tensor(t_band)
                    prod[("w_bandmm", gk, i, h)] = (sT.h, sT.v)

                    def v_norm(v, i=i, gk=gk, h=h, hG=hG, b=b):
                        v.wait_ge(*prod[("w_bandmm", gk, i, h)])
                        if h == 0:
                            if i >= 2:
                                v.wait_ge(*prod[("w_aodma", gk, i - 2)])
                            elif gk > 0:
                                v.wait_ge(*prod[("w_aodma", gk - 1, NT - 2 + i)])
                        v.reciprocal(stx[:, h % 2:h % 2 + 1], pO[h % 2][:, 64:65]).then_inc(sV.inc(), 1)
                        v.wait_ge(sV.h, sV.v)
                        v.tensor_scalar(aow[:, (i % 2) * 512 + h * 64:(i % 2) * 512 + (h + 1) * 64],
                                        pO[h % 2][:, 0:64], stx[:, h % 2:h % 2 + 1], None, ALU.mult).then_inc(sV.inc(), 1)
                    blk.vector(v_norm)
                    prod[("w_norm", gk, i, h)] = (sV.h, sV.v)

            def emit_B_aow(i, gk=gk, b=b, grp=grp):
                def s_aow(sync, i=i, gk=gk, grp=grp, b=b):
                    sync.wait_ge(*prod[("w_norm", gk, i, 7)])
                    sync.dma_start(ao_d[b].rearrange("(n p) c -> p n c", p=128)[:, i, grp * 512:(grp + 1) * 512],
                                   aow[:, (i % 2) * 512:(i % 2 + 1) * 512]).then_inc(sAOp[i % 2].dinc(), 16)
                blk.sync(s_aow)
                prod[("w_aodma", gk, i)] = (sAOp[i % 2].h, sAOp[i % 2].v)

            def emit_lev(j, gk=gk):
                sl2 = j % 8

                def v_levB(v, i=j, gk=gk, sl=sl2):
                    v.wait_ge(*prod[("w_redm", gk, i, 0)])
                    if i >= 8:
                        v.wait_ge(*prod[("w_exp", gk, i - 8)])
                    elif gk > 0:
                        v.wait_ge(*prod[("w_exp", gk - 1, i + 8)])
                    v.tensor_copy(BLs[:, sl * 256:(sl + 1) * 256], pRedB[:]).then_inc(sV.inc(), 1)
                blk.vector(v_levB)
                prod[("w_levB", gk, j)] = (sV.h, sV.v)

                def v_lg(v, i=j, gk=gk, sl=sl2):
                    v.wait_ge(*prod[("w_tanh", gk, i)])
                    v.wait_ge(*prod[("w_levB", gk, i)])
                    v.scalar_tensor_tensor(BLs[:, sl * 256:(sl + 1) * 256],
                                           DLs[:, sl * 256:(sl + 1) * 256], -5.0,
                                           BLs[:, sl * 256:(sl + 1) * 256],
                                           ALU.mult, ALU.add).then_inc(sV.inc(), 1)
                blk.vector(v_lg)
                prod[("w_lg", gk, j)] = (sV.h, sV.v)

            def emit_exp(j, gk=gk):
                sl2 = j % 8
                pp2 = j % 2

                def s_exp(s, i=j, gk=gk, pp=pp2, sl=sl2):
                    s.wait_ge(*prod[("w_lg", gk, i)])
                    if i >= 2:
                        s.wait_ge(*prod[("w_skw", gk, i - 2)])
                    elif gk > 0:
                        s.wait_ge(*prod[("w_skw", gk - 1, NT - 2 + i)])
                    s.activation(ptk[:, pp * 256:(pp + 1) * 256], BLs[:, sl * 256:(sl + 1) * 256], AF.Exp,
                                 bias=reps[:, 6:7]).then_inc(sS.inc(), 1)
                blk.scalar(s_exp)
                prod[("w_exp", gk, j)] = (sS.h, sS.v)

                if j == 0:
                    def v_mask(v, gk=gk):
                        v.wait_ge(*prod[("w_exp", gk, 0)])
                        ins = None
                        for jj in range(J - 1):
                            ins = v.memset(ptk[0:PAD - jj, jj * 8:(jj + 1) * 8], 0.0)
                        ins.then_inc(sV.inc(), 1)
                    blk.vector(v_mask)
                    prod[("w_msk", gk, 0)] = (sV.h, sV.v)
                else:
                    prod[("w_msk", gk, j)] = prod[("w_exp", gk, j)]

                def s_skw(sync, i=j, gk=gk, pp=pp2):
                    sync.wait_ge(*prod[("w_msk", gk, i)])
                    if gk == 0 and i < 2:
                        sync.wait_ge(sDp[i % 2].h, 16)
                    if i >= 2:
                        sync.wait_ge(*prod[("w_rb", gk, i - 2)])
                    elif gk > 0:
                        sync.wait_ge(*prod[("w_rb", gk - 1, NT - 2 + i)])
                    flat = skew_d[pp].rearrange("p n -> (p n)")
                    dst = bass.AP(flat.tensor, flat.offset + 776,
                                  [[2056, 128], [1, 256]])
                    sync.dma_start(dst, ptk[:, pp * 256:(pp + 1) * 256]).then_inc(sDp[pp].dinc(), 16)
                blk.sync(s_skw)
                prod[("w_skw", gk, j)] = (sDp[pp2].h, sDp[pp2].v)

            for i in range(NT):
                pp = i % 2
                sl = i % 8
                gb = pp * 8192

                def v_asm(v, i=i, gk=gk, gb=gb):
                    v.wait_ge(*prod[("w_F", gk)])
                    if gk == 0 and i == 0:
                        v.wait_ge(sD0.h, 16)
                        v.wait_ge(sD1.h, 16)
                    if i >= 2:
                        v.wait_ge(*prod[("w_redm", gk, i - 2, 1)])
                    elif gk > 0:
                        v.wait_ge(*prod[("w_redm", gk - 1, NT - 2 + i, 1)])
                    g3 = grid[:, gb:gb + 4096].rearrange("p (j t) -> p j t", t=128)
                    Ei = bass.AP(Ebuf[:].tensor, Ebuf[:].offset + i * 128, [[TP, 128], [1, J], [1, 128]])
                    Fb = Fbuf[:, i * 128:(i + 1) * 128].unsqueeze(1).broadcast_to([128, J, 128])
                    v.tensor_tensor(g3, Ei, Fb, ALU.add).then_inc(sV.inc(), 1)
                    v.wait_ge(sV.h, sV.v)
                    if gk == 0 and i == 0:
                        v.wait_ge(*prod["w_Gx"])
                    gB = grid[:, gb:gb + 4096]
                    v.tensor_tensor(gB, gB, negGx[:], ALU.subtract).then_inc(sV.inc(), 1)
                    v.wait_ge(sV.h, sV.v)
                    g3d = grid[:, gb + 4096:gb + 8192].rearrange("p (j t) -> p j t", t=128)
                    Ai2 = bass.AP(Adm[:].tensor, Adm[:].offset + i * 128, [[TP, 128], [1, J], [1, 128]])
                    Ab2 = Adm[:, PAD + i * 128:PAD + (i + 1) * 128].unsqueeze(1).broadcast_to([128, J, 128])
                    v.tensor_tensor(g3d, Ai2, Ab2, ALU.subtract).then_inc(sV.inc(), 1)
                blk.vector(v_asm)
                prod[("w_asm", gk, i, 0)] = (sV.h, sV.v - 1)
                prod[("w_asm", gk, i, 1)] = (sV.h, sV.v)

                if i >= 1:
                    emit_lev(i - 1)
                if i >= 2:
                    emit_B(i - 2)
                if i >= 3:
                    emit_B_aow(i - 3)

                for gi in range(2):
                    def s_gelu(s, i=i, gk=gk, gi=gi):
                        s.wait_ge(*prod[("w_asm", gk, i, gi)])
                        gb = (i % 2) * 8192
                        s.activation(grid[:, gb + gi * 4096:gb + (gi + 1) * 4096],
                                     grid[:, gb + gi * 4096:gb + (gi + 1) * 4096], AF.Gelu,
                                     bias=(0.0 if gi == 0 else reps[:, 2:3])).then_inc(sS.inc(), 1)
                    blk.scalar(s_gelu)
                    prod[("w_gelu", gk, i, gi)] = (sS.h, sS.v)

                    def t_red(t, i=i, gk=gk, gi=gi, b=b):
                        t.wait_ge(*prod[("w_gelu", gk, i, gi)])
                        if gi == 0 and i == 0 and gk % 2 == 0:
                            t.wait_ge(*prod[("w_vev", b * NT + NT - 1)])
                        if gi == 0:
                            if i >= 1:
                                t.wait_ge(*prod[("w_levB", gk, i - 1)])
                                t.wait_ge(*prod[("w_tanh", gk, i - 1)])
                            elif gk > 0:
                                t.wait_ge(*prod[("w_levB", gk - 1, NT - 1)])
                                t.wait_ge(*prod[("w_tanh", gk - 1, NT - 1)])
                        ins = None
                        gb = (i % 2) * 8192
                        pR = pRedB if gi == 0 else pRedD
                        for j in range(J):
                            ins = t.matmul(pR[:, j * 8:(j + 1) * 8],
                                           grid[:, gb + gi * 4096 + j * 128: gb + gi * 4096 + (j + 1) * 128],
                                           wbo[:, gi * 8:(gi + 1) * 8], start=True, stop=True)
                        ins.then_inc(sT.inc(), 1)
                    blk.tensor(t_red)
                    prod[("w_redm", gk, i, gi)] = (sT.h, sT.v)

                def s_tanh(s, i=i, gk=gk, sl=sl):
                    s.wait_ge(*prod[("w_redm", gk, i, 1)])
                    s.wait_ge(*prod["w_reps2"])
                    if i >= 8:
                        s.wait_ge(*prod[("w_lg", gk, i - 8)])
                    elif gk > 0:
                        s.wait_ge(*prod[("w_lg", gk - 1, i + 8)])
                    s.activation(DLs[:, sl * 256:(sl + 1) * 256], pRedD[:], AF.Tanh,
                                 scale=0.5, bias=reps[:, 5:6]).then_inc(sS.inc(), 1)
                blk.scalar(s_tanh)
                prod[("w_tanh", gk, i)] = (sS.h, sS.v)

                if i >= 1:
                    emit_exp(i - 1)

            emit_lev(NT - 1)
            emit_exp(NT - 1)
            emit_B(NT - 2)
            emit_B_aow(NT - 3)
            emit_B(NT - 1)
            emit_B_aow(NT - 2)
            emit_B_aow(NT - 1)

        xtiK = vsb[:].bitcast(F32)
        aotK = Pt
        for i in range(NT):
            k = b * NT + i
            pp = i % 2
            xs = i % 4
            asl = i % 4

            def s_x3(sync, b=b, i=i, xs=xs):
                if i < 4:
                    sync.wait_ge(*prod[("w_bandmm", b * 2 + 1, NT - 1, 7)])
                else:
                    sync.wait_ge(*prod[("w_z", b, i - 4)])
                sync.dma_start(xtiK[:, xs * C:(xs + 1) * C],
                               x_ap[b].rearrange("(n p) c -> p n c", p=128)[:, i]).then_inc(sXK[xs].dinc(), 16)
            blk.sync(s_x3)
            prod[("w_x3", b, i)] = (sXK[xs].h, sXK[xs].v)

            def s_aor(sync, b=b, i=i, asl=asl):
                sync.wait_ge(sAO0.h, prod[("w_aodma", b * 2 + 1, NT - 2)][1])
                sync.wait_ge(sAO1.h, prod[("w_aodma", b * 2 + 1, NT - 1)][1])
                if i < 4:
                    sync.wait_ge(*prod[("w_bandmm", b * 2 + 1, NT - 1, 7)])
                else:
                    sync.wait_ge(*prod[("w_aoT", b, i - 4)])
                sync.dma_start(aotK[:, asl * C:(asl + 1) * C],
                               ao_d[b].rearrange("(n p) c -> p n c", p=128)[:, i]).then_inc(sAK[asl].dinc(), 16)
            blk.sync(s_aor)
            prod[("w_aor", b, i)] = (sAK[asl].h, sAK[asl].v)

            def t_aoT(t, b=b, i=i, asl=asl):
                t.wait_ge(*prod[("w_aor", b, i)])
                if i == 0:
                    t.wait_ge(*prod[("w_Ptev", b * 2 + 1, NT - 1)])
                if i >= 1:
                    t.wait_ge(*prod[("w_aev", b, i - 1)])
                elif b > 0:
                    t.wait_ge(*prod[("w_aev", b - 1, NT - 1)])
                ins = None
                for ch in range(NCH):
                    ins = t.transpose(pT[:, ch * 128:(ch + 1) * 128],
                                      aotK[:, asl * C + ch * 128: asl * C + (ch + 1) * 128], idb[:])
                ins.then_inc(sT.inc(), 1)
            blk.tensor(t_aoT)
            prod[("w_aoT", b, i)] = (sT.h, sT.v)

            def v_aev(v, b=b, i=i, pp=pp):
                v.wait_ge(*prod[("w_aoT", b, i)])
                if i >= 2:
                    v.wait_ge(*prod[("w_cmm", b, i - 2)])
                v.tensor_copy(xfm[:, pp * C:(pp + 1) * C], pT[:]).then_inc(sV.inc(), 1)
            blk.vector(v_aev)
            prod[("w_aev", b, i)] = (sV.h, sV.v)

            def t_c(t, b=b, i=i, pp=pp):
                t.wait_ge(*prod[("w_aev", b, i)])
                if i >= 1:
                    t.wait_ge(*prod[("w_z", b, i - 1)])
                elif b > 0:
                    t.wait_ge(*prod[("w_z", b - 1, NT - 1)])
                ins = None
                for half in range(2):
                    dst = pj[:, half * 512:(half + 1) * 512]
                    for ch in range(NCH):
                        t.matmul(dst, xfm[:, pp * C + ch * 128: pp * C + (ch + 1) * 128],
                                 wcpj[:, ch * C + half * 512: ch * C + (half + 1) * 512],
                                 start=(ch == 0), stop=False)
                    ins = t.matmul(dst, ones[:], bcr[0:1, half * 512:(half + 1) * 512],
                                   start=False, stop=True)
                ins.then_inc(sT.inc(), 1)
            blk.tensor(t_c)
            prod[("w_cmm", b, i)] = (sT.h, sT.v)

            def v_z(v, b=b, i=i, pp=pp, xs=xs):
                v.wait_ge(*prod[("w_cmm", b, i)])
                v.wait_ge(*prod[("w_x3", b, i)])
                if i >= 1:
                    v.wait_ge(*prod[("w_fln", b, i - 1)])
                elif b > 0:
                    v.wait_ge(*prod[("w_fln", b - 1, NT - 1)])
                v.tensor_tensor(zt[:], pj[:], xtiK[:, xs * C:(xs + 1) * C], ALU.add).then_inc(sV.inc(), 1)
            blk.vector(v_z)
            prod[("w_z", b, i)] = (sV.h, sV.v)

            ews = [prod[("w_z", b, i)]]
            if i >= 1:
                ews.append(prod[("w_ydma", b, i - 1)])
            elif b > 0:
                ews.append(prod[("w_ydma", b - 1, NT - 1)])
            ln_emit(blk, sLN, zt[:], yst[:], scr[:], stx[:, 4:8], C,
                    prod, ("w_fln", b, i), extra_wait=ews)

            def sydma(sync, b=b, i=i):
                sync.wait_ge(*prod[("w_fln", b, i)])
                sync.dma_start(y_ap[b].rearrange("(n p) c -> p n c", p=128)[:, i],
                               yst[:]).then_inc(sO.dinc(), 16)
            blk.sync(sydma)
            prod[("w_ydma", b, i)] = (sO.h, sO.v)

    def sfin(sync):
        sync.wait_ge(sO.h, sO.v)
    blk.sync(sfin)
    return [v for v in locals().values() if isinstance(v, Sem)]


def l1attn_host_inputs(w_disp, b_disp, w_val, b_val, rel_pos_emb, w_state, b_state,
                       w_strain, b_strain, w_state_i, w_state_j, w_pos, w_bond_out,
                       b_bond_out, w_damage, b_damage, w_damage_out, b_damage_out,
                       w_cproj, b_cproj, pref="w_"):
    import ml_dtypes
    C = 1024
    bd = np.zeros((128, 640), np.float32)
    for bi, w in enumerate([w_state, w_strain, w_state_i, w_state_j, w_damage]):
        for h in range(8):
            bd[h * 16:(h + 1) * 16, bi * 128 + h * 16:bi * 128 + (h + 1) * 16] = w
    bdout = np.zeros((128, 16), np.float32)
    for h in range(8):
        bdout[h * 16:(h + 1) * 16, h] = w_bond_out[:, 0]
        bdout[h * 16:(h + 1) * 16, 8 + h] = w_damage_out[:, 0]
    rep16 = np.zeros((16, 128), np.float32)
    for h in range(8):
        rep16[:, h * 16:(h + 1) * 16] = np.eye(16)
    reps = np.zeros((128, 8), np.float32)
    reps[:, 0] = np.tile(b_state, 8)
    reps[:, 1] = np.tile(b_strain, 8)
    reps[:, 2] = np.tile(b_damage, 8)
    reps[:, 3] = b_bond_out[0]
    reps[:, 4] = b_damage_out[0]
    cnt = np.minimum(np.arange(32) + 1, 32).astype(np.float32)
    rc = np.broadcast_to((1.0 / cnt)[None, :], (128, 32)).copy()
    bdisp2 = np.stack([b_disp[0:128], b_disp[128:256]], axis=1).astype(np.float32)
    return {
        pref + "wdisp": np.ascontiguousarray(w_disp),
        pref + "bdisp2": bdisp2,
        pref + "wval": np.ascontiguousarray(w_val),
        pref + "bval": b_val.reshape(1, C).astype(np.float32),
        pref + "wcproj": np.ascontiguousarray(w_cproj),
        pref + "bcproj": b_cproj.reshape(1, C).astype(np.float32),
        pref + "bd": bd,
        pref + "bdout": bdout,
        pref + "relT": np.ascontiguousarray(rel_pos_emb.T),
        pref + "wpos": np.ascontiguousarray(w_pos),
        pref + "rep16": rep16,
        pref + "reps": reps,
        pref + "rc": rc,
        pref + "ident": np.eye(128, dtype=ml_dtypes.bfloat16),
        pref + "ones": np.ones((1, 128), dtype=ml_dtypes.bfloat16),
    }


class _RecIns:
    __slots__ = ("thens",)

    def __init__(self):
        self.thens = []

    def then_inc(self, sem, n):
        self.thens.append((sem, n))
        return self


class _RecEng:
    def __init__(self):
        self.q = []

    def __getattr__(self, name):
        if name.startswith("_"):
            raise AttributeError(name)

        def call(*a, **k):
            ri = _RecIns()
            self.q.append((name, a, k, ri))
            return ri
        return call


class MergeBlk:
    ENGS = ("gpsimd", "sync", "tensor", "vector", "scalar")

    def __init__(self, blk):
        self._blk = blk
        self._recs = {n: _RecEng() for n in self.ENGS}
        for n in self.ENGS:
            setattr(self, n, (lambda rec: lambda fn: fn(rec))(self._recs[n]))

    def flush(self):
        for n in self.ENGS:
            rec = self._recs[n]
            if not rec.q:
                continue
            q = rec.q
            rec.q = []

            def run(eng, q=q):
                for (nm, a, k, ri) in q:
                    ins = getattr(eng, nm)(*a, **k)
                    for (s, cnt) in ri.thens:
                        ins.then_inc(s, cnt)
            getattr(self._blk, n)(run)


from concourse.bass_utils import run_bass_kernel_spmd

N_CORES = 8
B_CORE = 2
_LAYERS = [build_l0attn, build_l0moe, build_l1attn, build_l1ffn]


def _build_fused():
    nc = bass.Bass("TRN2", target_bir_lowering=False, debug=False)
    x_d = nc.dram_tensor("x_in", [B_CORE, 2048, 1024], F32, kind="ExternalInput").ap()
    y1_d = nc.dram_tensor("y1_s", [B_CORE, 2048, 1024], F32, kind="Internal").ap()
    y2_d = nc.dram_tensor("y2_s", [B_CORE, 2048, 1024], F32, kind="Internal").ap()
    y3_d = nc.dram_tensor("y3_s", [B_CORE, 2048, 1024], F32, kind="Internal").ap()
    out_d = nc.dram_tensor("y_out", [B_CORE, 2048, 1024], F32, kind="ExternalOutput").ap()
    es = ExitStack()
    rblk = es.enter_context(nc.Block())
    blk = MergeBlk(rblk)
    bar = Sem(es, nc, "bar")
    prod = {}
    chain = [(build_l0attn, x_d, y1_d), (build_l0moe, y1_d, y2_d),
             (build_l1attn, y2_d, y3_d), (build_l1ffn, y3_d, out_d)]
    for li, (builder, xa, ya) in enumerate(chain):
        if li > 0:
            for engsec in (blk.gpsimd, blk.sync, blk.tensor, blk.vector, blk.scalar):
                engsec(lambda e, li=li: e.wait_ge(bar.h, li))
        with ExitStack() as les:
            lsems = builder(nc, les, blk, prod, B_CORE, x_ap=xa, y_ap=ya, ses=es)
        def _barrier(s, lsems=lsems):
            for sm in lsems:
                s.wait_ge(sm.h, sm.v)
            s.sem_inc(bar.h, 1)
        bar.v += 1
        blk.sync(_barrier)
        blk.flush()
    es.close()
    return nc


def _host_all(ins):
    h = {}
    h.update(l0attn_host_inputs(ins["L0_attn_Wqkv"], ins["L0_attn_bqkv"],
                                ins["L0_attn_Wo"], ins["L0_attn_bo"]))
    h.update(l0moe_host_inputs(ins["L0_ffn_centroids"], ins["L0_ffn_expert_biases"],
                               ins["L0_ffn_Wr"], ins["L0_ffn_br"],
                               ins["L0_ffn_Ws"], ins["L0_ffn_bs"]))
    h.update(l1attn_host_inputs(ins["L1_attn_w_disp"], ins["L1_attn_b_disp"],
                                ins["L1_attn_w_val"], ins["L1_attn_b_val"],
                                ins["L1_attn_rel_pos_emb"], ins["L1_attn_w_state"],
                                ins["L1_attn_b_state"], ins["L1_attn_w_strain"],
                                ins["L1_attn_b_strain"], ins["L1_attn_w_state_i"],
                                ins["L1_attn_w_state_j"], ins["L1_attn_w_pos"],
                                ins["L1_attn_w_bond_out"], ins["L1_attn_b_bond_out"],
                                ins["L1_attn_w_damage"], ins["L1_attn_b_damage"],
                                ins["L1_attn_w_damage_out"], ins["L1_attn_b_damage_out"],
                                ins["L1_attn_w_cproj"], ins["L1_attn_b_cproj"]))
    h.update(l1ffn_host_inputs(ins["L1_ffn_W_r"], ins["L1_ffn_W_k"], ins["L1_ffn_W_v"],
                               ins["L1_ffn_mu_r"], ins["L1_ffn_mu_k"]))
    return h


def kernel(**ins):
    for k in list(ins):
        ins[k] = np.asarray(ins[k])
    x = np.asarray(ins["L0_attn_x"], dtype=np.float32)
    host = _host_all(ins)
    nc = _build_fused()
    shards = np.split(np.ascontiguousarray(x), N_CORES, axis=0)
    in_maps = [{"x_in": s, **host} for s in shards]
    res = run_bass_kernel_spmd(nc, in_maps, core_ids=list(range(N_CORES)))
    return np.concatenate([np.asarray(r["y_out"]) for r in res.results], axis=0)
